# version 01
# baseline (speedup 1.0000x reference)
import numpy as np
import ml_dtypes
from contextlib import ExitStack

import concourse.bass as bass
import concourse.mybir as mybir
from concourse.bass_utils import run_bass_kernel_spmd

F32 = mybir.dt.float32
BF16 = mybir.dt.bfloat16
AF = mybir.ActivationFunctionType
ALU = mybir.AluOpType
AX = mybir.AxisListType

NCORES = 8
T = 2048
D = 1024
NTILE = T // 128
DCH = D // 128
EPS = 1e-5
M0 = 1.5 * 2.0 ** 23
BF = ml_dtypes.bfloat16

FF = 2816
FSL = FF // NCORES
WDSL = D // NCORES
NGRP = 4
GT = T // NGRP


class Sched:

    def __init__(self):
        self.ops = []

    def add(self, eng, fn, inc=1):
        self.ops.append((eng, inc, fn))

    def dma(self, fn):
        self.add("sync", fn, 16)

    def gdma(self, fn):
        self.add("gpsimd", fn, 16)

    def emit(self, block, step):
        totals = []
        t = 0
        for _, inc, _ in self.ops:
            totals.append(t)
            t += inc
        final_total = t

        def emit_for(name):
            def _(eng):
                for (en, inc, fn), before in zip(self.ops, totals):
                    if en != name:
                        continue
                    if before > 0:
                        eng.wait_ge(step, before)
                    fn(eng).then_inc(step, inc)
                if name == "sync":
                    eng.wait_ge(step, final_total)
            return _

        block.sync(emit_for("sync"))
        block.gpsimd(emit_for("gpsimd"))
        block.vector(emit_for("vector"))
        block.scalar(emit_for("scalar"))
        block.tensor(emit_for("tensor"))


class Arena:

    def __init__(self, tensor, cols):
        self.t = tensor
        self.cols = cols
        self.off = 0

    def alloc(self, cols):
        cols = (cols + 63) & ~63
        assert self.off + cols <= self.cols, \
            f"arena overflow: {self.off}+{cols} > {self.cols}"
        v = self.t[:, self.off:self.off + cols]
        self.off += cols
        return v

    def reset(self):
        self.off = 0


def emit_quant(s, q, pr, cols, blk, qmax, qeps, base_step, src, dst):
    nb = cols // blk
    bm, sc, a, c2, c4, ms = q["bm"], q["sc"], q["a"], q["c2"], q["c4"], q["ms"]
    v = "vector"
    s.add(v, lambda e: e.tensor_reduce(
        bm[0:pr, 0:nb], src.rearrange("p (b k) -> p b k", k=blk),
        axis=AX.X, op=ALU.max, apply_absolute_value=True))
    s.add(v, lambda e: e.tensor_scalar(
        out=sc[0:pr, 0:nb], in0=bm[0:pr, 0:nb], scalar1=1.0 / qmax,
        scalar2=qeps, op0=ALU.mult, op1=ALU.max))
    s.add(v, lambda e: e.reciprocal(bm[0:pr, 0:nb], sc[0:pr, 0:nb]))
    scb = sc[0:pr, 0:nb].unsqueeze(2).broadcast_to([pr, nb, blk])
    rsb = bm[0:pr, 0:nb].unsqueeze(2).broadcast_to([pr, nb, blk])
    s.add(v, lambda e: e.tensor_tensor(
        a[0:pr, 0:cols].rearrange("p (b k) -> p b k", k=blk),
        src.rearrange("p (b k) -> p b k", k=blk), rsb, op=ALU.mult))
    s.add("scalar", lambda e: e.activation(
        out=ms[0:pr, 0:cols], in_=a[0:pr, 0:cols], func=AF.Abs))
    s.add(v, lambda e: e.tensor_scalar(
        out=c2[0:pr, 0:cols], in0=ms[0:pr, 0:cols], scalar1=2.0,
        scalar2=base_step * M0, op0=ALU.is_ge, op1=ALU.mult))
    s.add(v, lambda e: e.tensor_scalar(
        out=c4[0:pr, 0:cols], in0=ms[0:pr, 0:cols], scalar1=4.0,
        scalar2=2.0 * base_step * M0, op0=ALU.is_ge, op1=ALU.mult))
    s.add(v, lambda e: e.scalar_tensor_tensor(
        out=ms[0:pr, 0:cols], in0=c2[0:pr, 0:cols], scalar=base_step * M0,
        in1=c4[0:pr, 0:cols], op0=ALU.add, op1=ALU.add))
    s.add(v, lambda e: e.tensor_tensor(
        c2[0:pr, 0:cols], a[0:pr, 0:cols], ms[0:pr, 0:cols], op=ALU.add))
    s.add(v, lambda e: e.tensor_tensor(
        c4[0:pr, 0:cols], c2[0:pr, 0:cols], ms[0:pr, 0:cols], op=ALU.subtract))
    s.add(v, lambda e: e.tensor_tensor(
        dst[0:pr, 0:cols].rearrange("p (b k) -> p b k", k=blk),
        c4[0:pr, 0:cols].rearrange("p (b k) -> p b k", k=blk), scb, op=ALU.mult))


QW = 1408


def quant_scratch(aF, cols=QW):
    return {"bm": aF.alloc(cols // 16), "sc": aF.alloc(cols // 16),
            "a": aF.alloc(cols), "c2": aF.alloc(cols),
            "c4": aF.alloc(cols), "ms": aF.alloc(cols)}


def emit_fp4_wq(s, ps, idn_b, q, qin, qout, ev16,
                wg_sl, wu_sl, wd_sl, wgT_sl_d, wuT_sl_d, wdT_sl_d,
                wgT_d, wuT_d, wdT_d):
    psT = ps[3]

    def quant_weight_slice(src_dram, dst_dram, rows, cols):
        nrt = (rows + 127) // 128
        for i in range(nrt):
            r0 = i * 128
            pr = min(rows, r0 + 128) - r0
            for h0 in range(0, cols, QW):
                hw = min(cols - h0, QW)
                s.dma(lambda e, r0=r0, pr=pr, h0=h0, hw=hw: e.dma_start(
                    out=qin[0:pr, 0:hw], in_=src_dram[r0:r0 + pr, h0:h0 + hw]))
                emit_quant(s, q, pr, hw, 16, 6.0, 1e-12, 0.5,
                           qin[0:pr, 0:hw], qout)
                for c in range(hw // 128):
                    k0 = h0 + c * 128
                    s.add("tensor", lambda e, c=c, pr=pr: e.matmul(
                        psT[0:128, 0:pr], qout[0:pr, c * 128:(c + 1) * 128],
                        idn_b[0:pr, 0:pr], start=True, stop=True))
                    s.add("vector", lambda e, pr=pr: e.tensor_copy(
                        ev16[0:128, 0:pr], psT[0:128, 0:pr]))
                    s.dma(lambda e, k0=k0, r0=r0, pr=pr: e.dma_start(
                        out=dst_dram[k0:k0 + 128, r0:r0 + pr],
                        in_=ev16[0:128, 0:pr]))

    quant_weight_slice(wg_sl, wgT_sl_d.ap(), FSL, D)
    quant_weight_slice(wu_sl, wuT_sl_d.ap(), FSL, D)
    quant_weight_slice(wd_sl, wdT_sl_d.ap(), WDSL, FF)
    for src_d, dst_d in ((wgT_sl_d, wgT_d), (wuT_sl_d, wuT_d), (wdT_sl_d, wdT_d)):
        s.add("gpsimd", lambda e, src_d=src_d, dst_d=dst_d: e.collective_compute(
            "AllGather", ALU.bypass, replica_groups=[list(range(NCORES))],
            ins=[src_d.ap().opt()], outs=[dst_d.ap().opt()]))


def emit_fp4_ffn(s, ps, X, mT, rT, idn_f, idn_b, q,
                 xtok, hfl, xq_tok, xqT, wgT_buf, wuT_buf, hq, hqT, wdT_buf,
                 gbuf, wgT_d, wuT_d, wdT_d, g_d):
    psA, psB, psT = ps[0], ps[1], ps[3]
    for g in range(NGRP):
        for tl in range(GT // 128):
            tt = g * (GT // 128) + tl
            for c in range(DCH):
                s.add("tensor", lambda e, c=c, tt=tt: e.matmul(
                    psT[0:128, 0:128], X[:, c * T + tt * 128:c * T + (tt + 1) * 128],
                    idn_f[:, :], start=True, stop=True))
                s.add("vector", lambda e, c=c, tt=tt: e.tensor_scalar(
                    out=xtok[:, c * 128:(c + 1) * 128], in0=psT[0:128, 0:128],
                    scalar1=mT[:, tt:tt + 1], scalar2=rT[:, tt:tt + 1],
                    op0=ALU.subtract, op1=ALU.mult))
            emit_quant(s, q, 128, D, 16, 6.0, 1e-12, 0.5, xtok[0:128, 0:D], xq_tok)
            for c in range(DCH):
                s.add("tensor", lambda e, c=c: e.matmul(
                    psT[0:128, 0:128], xq_tok[0:128, c * 128:(c + 1) * 128],
                    idn_b[:, :], start=True, stop=True))
                s.add("vector", lambda e, c=c, tl=tl: e.tensor_copy(
                    xqT[:, c * GT + tl * 128: c * GT + (tl + 1) * 128],
                    psT[0:128, 0:128]))
        for ncx in range(NCORES):
            s.dma(lambda e, ncx=ncx: e.dma_start(
                out=wgT_buf.rearrange("p (c n) -> p c n", n=FSL),
                in_=wgT_d.ap()[ncx * D:(ncx + 1) * D, :].rearrange(
                    "(c p) n -> p c n", p=128)))
            s.dma(lambda e, ncx=ncx: e.dma_start(
                out=wuT_buf.rearrange("p (c n) -> p c n", n=FSL),
                in_=wuT_d.ap()[ncx * D:(ncx + 1) * D, :].rearrange(
                    "(c p) n -> p c n", p=128)))
            for tl in range(GT // 128):
                def mm_gu(e, tl=tl, buf=None, dst=None):
                    for c in range(DCH):
                        last = e.matmul(
                            dst[0:128, 0:FSL],
                            xqT[:, c * GT + tl * 128: c * GT + (tl + 1) * 128],
                            buf[:, c * FSL:(c + 1) * FSL],
                            start=(c == 0), stop=(c == DCH - 1))
                    return last
                s.add("tensor", lambda e, tl=tl: mm_gu(e, tl, wgT_buf, psA))
                s.add("tensor", lambda e, tl=tl: mm_gu(e, tl, wuT_buf, psB))
                s.add("scalar", lambda e: e.activation(
                    out=hfl[0:128, 0:FSL], in_=psA[0:128, 0:FSL], func=AF.Silu))
                s.add("vector", lambda e: e.tensor_tensor(
                    hfl[0:128, 0:FSL], hfl[0:128, 0:FSL], psB[0:128, 0:FSL],
                    op=ALU.mult))
                emit_quant(s, q, 128, FSL, 16, 6.0, 1e-12, 0.5,
                           hfl[0:128, 0:FSL], hq)
                for sj, (off, w) in enumerate(((0, 128), (128, 128), (256, 96))):
                    sub = ncx * 3 + sj
                    s.add("tensor", lambda e, off=off, w=w: e.matmul(
                        psT[0:w, 0:128], hq[0:128, off:off + w],
                        idn_b[:, :], start=True, stop=True))
                    s.add("vector", lambda e, w=w, sub=sub, tl=tl: e.tensor_copy(
                        hqT[0:w, sub * GT + tl * 128: sub * GT + (tl + 1) * 128],
                        psT[0:w, 0:128]))
        for sub in range(24):
            ncx, sj = sub // 3, sub % 3
            w = 96 if sj == 2 else 128
            nglob = ncx * FSL + sj * 128
            s.dma(lambda e, nglob=nglob, w=w: e.dma_start(
                out=wdT_buf[0:w, :].rearrange("p (c n) -> p c n", n=WDSL),
                in_=bass.AP(wdT_d, nglob * WDSL,
                            [[WDSL, w], [FF * WDSL, 8], [1, WDSL]])))

            def mm_down(e, sub=sub, w=w):
                for c2 in range(DCH):
                    last = e.matmul(ps[c2][0:128, 0:GT],
                                    wdT_buf[0:w, c2 * 128:(c2 + 1) * 128],
                                    hqT[0:w, sub * GT:(sub + 1) * GT],
                                    start=(sub == 0), stop=(sub == 23))
                return last
            s.add("tensor", mm_down)
        for c2 in range(DCH):
            s.add("scalar", lambda e, c2=c2: e.activation(
                out=gbuf[0:128, 0:GT], in_=ps[c2][0:128, 0:GT], func=AF.Silu))
            s.dma(lambda e, g=g, c2=c2: e.dma_start(
                out=bass.AP(g_d, c2 * 128 * T + g * GT, [[T, 128], [1, GT]]),
                in_=gbuf[0:128, 0:GT]))


def build_l0():
    nc = bass.Bass("TRN2", target_bir_lowering=False)
    xT = nc.dram_tensor("xT", [D, T], F32, kind="ExternalInput").ap()
    wqT = nc.dram_tensor("wqT", [D, D], F32, kind="ExternalInput").ap()
    wkT = nc.dram_tensor("wkT", [D, D], F32, kind="ExternalInput").ap()
    wvT = nc.dram_tensor("wvT", [D, D], F32, kind="ExternalInput").ap()
    bq_in = nc.dram_tensor("bq", [128, DCH], F32, kind="ExternalInput").ap()
    bk_in = nc.dram_tensor("bk", [128, DCH], F32, kind="ExternalInput").ap()
    bv_in = nc.dram_tensor("bv", [128, DCH], F32, kind="ExternalInput").ap()
    wg_sl = nc.dram_tensor("wg_sl", [FSL, D], F32, kind="ExternalInput").ap()
    wu_sl = nc.dram_tensor("wu_sl", [FSL, D], F32, kind="ExternalInput").ap()
    wd_sl = nc.dram_tensor("wd_sl", [WDSL, FF], F32, kind="ExternalInput").ap()
    idn_in = nc.dram_tensor("idn", [128, 128], F32, kind="ExternalInput").ap()
    onesc_in = nc.dram_tensor("onesc", [128, 1], F32, kind="ExternalInput").ap()
    onesr_in = nc.dram_tensor("onesr", [1, 128], F32, kind="ExternalInput").ap()
    yT = nc.dram_tensor("yT", [D, T], F32, kind="ExternalOutput").ap()

    wgT_sl_d = nc.dram_tensor("wgT_sl_d", [D, FSL], BF16)
    wuT_sl_d = nc.dram_tensor("wuT_sl_d", [D, FSL], BF16)
    wdT_sl_d = nc.dram_tensor("wdT_sl_d", [FF, WDSL], BF16)
    wgT_d = nc.dram_tensor("wgT_d", [NCORES * D, FSL], BF16)
    wuT_d = nc.dram_tensor("wuT_d", [NCORES * D, FSL], BF16)
    wdT_d = nc.dram_tensor("wdT_d", [NCORES * FF, WDSL], BF16)
    g_d = nc.dram_tensor("g_d", [D, T], F32)

    with ExitStack() as st:
        def sb(name, shape, dt=F32):
            return st.enter_context(nc.sbuf_tensor(name, shape, dt))

        X = sb("X", [128, DCH * T])
        idn_f = sb("idn_f", [128, 128])
        idn_b = sb("idn_b", [128, 128], BF16)
        ones_c = sb("ones_c", [128, 1])
        ones_r = sb("ones_r", [1, 128])
        bq = sb("bq_s", [128, DCH]); bk = sb("bk_s", [128, DCH]); bv = sb("bv_s", [128, DCH])
        arow = sb("arow", [1, T]); brow = sb("brow", [1, T])
        mT = sb("mT", [128, NTILE]); rT = sb("rT", [128, NTILE])
        gbuf = sb("gbuf", [128, DCH * GT])
        tcomb = sb("tcomb", [128, 128])
        nmax = sb("nmax", [128, 1]); ssum = sb("ssum", [128, 1]); rsum = sb("rsum", [128, 1])
        AFT = sb("arenaF", [128, 9216])
        ABT = sb("arenaB", [128, 29184], BF16)
        aF = Arena(AFT, 9216)
        aB = Arena(ABT, 29184)

        ps = [st.enter_context(nc.psum_tensor(f"ps{i}", [128, 512], F32)) for i in range(8)]
        psA, psB, psQ, psT = ps[0], ps[1], ps[2], ps[3]

        step = st.enter_context(nc.semaphore("step"))
        block = st.enter_context(nc.Block())
        s = Sched()

        s.dma(lambda e: e.dma_start(
            out=X.ap().rearrange("p (c t) -> p c t", t=T),
            in_=xT.rearrange("(c p) t -> p c t", p=128)))
        s.dma(lambda e: e.dma_start(out=idn_f[:, :], in_=idn_in))
        s.gdma(lambda e: e.dma_start(out=idn_b[:, :], in_=idn_in))
        s.dma(lambda e: e.dma_start(out=ones_c[:, :], in_=onesc_in))
        s.dma(lambda e: e.dma_start(out=ones_r[:, :], in_=onesr_in))
        s.dma(lambda e: e.dma_start(out=bq[:, :], in_=bq_in))
        s.dma(lambda e: e.dma_start(out=bk[:, :], in_=bk_in))
        s.dma(lambda e: e.dma_start(out=bv[:, :], in_=bv_in))

        q = quant_scratch(aF)
        qin = aF.alloc(QW)
        qout = aB.alloc(QW)
        ev16 = aB.alloc(128)

        def quant_weight_slice(src_dram, dst_dram, rows, cols):
            nrt = (rows + 127) // 128
            for i in range(nrt):
                r0 = i * 128
                pr = min(rows, r0 + 128) - r0
                for h0 in range(0, cols, QW):
                    hw = min(cols - h0, QW)
                    s.dma(lambda e, r0=r0, pr=pr, h0=h0, hw=hw: e.dma_start(
                        out=qin[0:pr, 0:hw], in_=src_dram[r0:r0 + pr, h0:h0 + hw]))
                    emit_quant(s, q, pr, hw, 16, 6.0, 1e-12, 0.5,
                               qin[0:pr, 0:hw], qout)
                    for c in range(hw // 128):
                        k0 = h0 + c * 128
                        s.add("tensor", lambda e, c=c, pr=pr: e.matmul(
                            psT[0:128, 0:pr], qout[0:pr, c * 128:(c + 1) * 128],
                            idn_b[0:pr, 0:pr], start=True, stop=True))
                        s.add("vector", lambda e, pr=pr: e.tensor_copy(
                            ev16[0:128, 0:pr], psT[0:128, 0:pr]))
                        s.dma(lambda e, k0=k0, r0=r0, pr=pr: e.dma_start(
                            out=dst_dram[k0:k0 + 128, r0:r0 + pr],
                            in_=ev16[0:128, 0:pr]))

        quant_weight_slice(wg_sl, wgT_sl_d.ap(), FSL, D)
        quant_weight_slice(wu_sl, wuT_sl_d.ap(), FSL, D)
        quant_weight_slice(wd_sl, wdT_sl_d.ap(), WDSL, FF)
        for src_d, dst_d in ((wgT_sl_d, wgT_d), (wuT_sl_d, wuT_d), (wdT_sl_d, wdT_d)):
            s.add("gpsimd", lambda e, src_d=src_d, dst_d=dst_d: e.collective_compute(
                "AllGather", ALU.bypass, replica_groups=[list(range(NCORES))],
                ins=[src_d.ap().opt()], outs=[dst_d.ap().opt()]))
        aF.reset(); aB.reset()

        emit_ln(s, nc, aF, X, ones_c, ones_r, arow, brow, mT, rT, psA, psB, psT)
        aF.reset(); aB.reset()

        q = quant_scratch(aF)
        xtok = aF.alloc(D)
        hfl = aF.alloc(FSL)
        xq_tok = aB.alloc(D)
        xqT = aB.alloc(DCH * GT)
        wgT_buf = aB.alloc(DCH * FSL)
        wuT_buf = aB.alloc(DCH * FSL)
        hq = aB.alloc(FSL)
        hqT = aB.alloc(24 * GT)
        wdT_buf = aB.alloc(D)

        for g in range(NGRP):
            for tl in range(GT // 128):
                tt = g * (GT // 128) + tl
                for c in range(DCH):
                    s.add("tensor", lambda e, c=c, tt=tt: e.matmul(
                        psT[0:128, 0:128], X[:, c * T + tt * 128:c * T + (tt + 1) * 128],
                        idn_f[:, :], start=True, stop=True))
                    s.add("vector", lambda e, c=c, tt=tt: e.tensor_scalar(
                        out=xtok[:, c * 128:(c + 1) * 128], in0=psT[0:128, 0:128],
                        scalar1=mT[:, tt:tt + 1], scalar2=rT[:, tt:tt + 1],
                        op0=ALU.subtract, op1=ALU.mult))
                emit_quant(s, q, 128, D, 16, 6.0, 1e-12, 0.5, xtok[0:128, 0:D], xq_tok)
                for c in range(DCH):
                    s.add("tensor", lambda e, c=c: e.matmul(
                        psT[0:128, 0:128], xq_tok[0:128, c * 128:(c + 1) * 128],
                        idn_b[:, :], start=True, stop=True))
                    s.add("vector", lambda e, c=c, tl=tl: e.tensor_copy(
                        xqT[:, c * GT + tl * 128: c * GT + (tl + 1) * 128],
                        psT[0:128, 0:128]))
            for ncx in range(NCORES):
                s.dma(lambda e, ncx=ncx: e.dma_start(
                    out=wgT_buf.rearrange("p (c n) -> p c n", n=FSL),
                    in_=wgT_d.ap()[ncx * D:(ncx + 1) * D, :].rearrange(
                        "(c p) n -> p c n", p=128)))
                s.dma(lambda e, ncx=ncx: e.dma_start(
                    out=wuT_buf.rearrange("p (c n) -> p c n", n=FSL),
                    in_=wuT_d.ap()[ncx * D:(ncx + 1) * D, :].rearrange(
                        "(c p) n -> p c n", p=128)))
                for tl in range(GT // 128):
                    def mm_gu(e, tl=tl, buf=None, dst=None):
                        for c in range(DCH):
                            last = e.matmul(
                                dst[0:128, 0:FSL],
                                xqT[:, c * GT + tl * 128: c * GT + (tl + 1) * 128],
                                buf[:, c * FSL:(c + 1) * FSL],
                                start=(c == 0), stop=(c == DCH - 1))
                        return last
                    s.add("tensor", lambda e, tl=tl: mm_gu(e, tl, wgT_buf, psA))
                    s.add("tensor", lambda e, tl=tl: mm_gu(e, tl, wuT_buf, psB))
                    s.add("scalar", lambda e: e.activation(
                        out=hfl[0:128, 0:FSL], in_=psA[0:128, 0:FSL], func=AF.Silu))
                    s.add("vector", lambda e: e.tensor_tensor(
                        hfl[0:128, 0:FSL], hfl[0:128, 0:FSL], psB[0:128, 0:FSL],
                        op=ALU.mult))
                    emit_quant(s, q, 128, FSL, 16, 6.0, 1e-12, 0.5,
                               hfl[0:128, 0:FSL], hq)
                    for sj, (off, w) in enumerate(((0, 128), (128, 128), (256, 96))):
                        sub = ncx * 3 + sj
                        s.add("tensor", lambda e, off=off, w=w: e.matmul(
                            psT[0:w, 0:128], hq[0:128, off:off + w],
                            idn_b[:, :], start=True, stop=True))
                        s.add("vector", lambda e, w=w, sub=sub, tl=tl: e.tensor_copy(
                            hqT[0:w, sub * GT + tl * 128: sub * GT + (tl + 1) * 128],
                            psT[0:w, 0:128]))
            for sub in range(24):
                ncx, sj = sub // 3, sub % 3
                w = 96 if sj == 2 else 128
                nglob = ncx * FSL + sj * 128
                s.dma(lambda e, nglob=nglob, w=w: e.dma_start(
                    out=wdT_buf[0:w, :].rearrange("p (c n) -> p c n", n=WDSL),
                    in_=bass.AP(wdT_d, nglob * WDSL,
                                [[WDSL, w], [FF * WDSL, 8], [1, WDSL]])))

                def mm_down(e, sub=sub, w=w):
                    for c2 in range(DCH):
                        last = e.matmul(ps[c2][0:128, 0:GT],
                                        wdT_buf[0:w, c2 * 128:(c2 + 1) * 128],
                                        hqT[0:w, sub * GT:(sub + 1) * GT],
                                        start=(sub == 0), stop=(sub == 23))
                    return last
                s.add("tensor", mm_down)
            for c2 in range(DCH):
                s.add("scalar", lambda e, c2=c2: e.activation(
                    out=gbuf[:, c2 * GT:(c2 + 1) * GT], in_=ps[c2][0:128, 0:GT],
                    func=AF.Silu))
            s.dma(lambda e, g=g: e.dma_start(
                out=bass.AP(g_d, g * GT, [[T, 128], [128 * T, DCH], [1, GT]]),
                in_=gbuf.ap().rearrange("p (c t) -> p c t", t=GT)))
        aF.reset(); aB.reset()

        wq_b = aB.alloc(DCH * D)
        wk_b = aB.alloc(DCH * D)
        wv_b = aB.alloc(DCH * D)
        xnb = aB.alloc(DCH * 128)
        qTt = aB.alloc(DCH * 128)
        kTt = aB.alloc(DCH * 128)
        vtok = aB.alloc(D)
        probs = aB.alloc(128)
        aTt = aB.alloc(128)
        for wbuf, wsrc in ((wq_b, wqT), (wk_b, wkT), (wv_b, wvT)):
            s.gdma(lambda e, wbuf=wbuf, wsrc=wsrc: e.dma_start(
                out=wbuf.rearrange("p (c n) -> p c n", n=D),
                in_=wsrc.rearrange("(c p) n -> p c n", p=128)))

        for ch in range(NTILE):
            if ch % (GT // 128) == 0:
                g = ch // (GT // 128)
                s.dma(lambda e, g=g: e.dma_start(
                    out=gbuf.ap().rearrange("p (c t) -> p c t", t=GT),
                    in_=bass.AP(g_d, g * GT, [[T, 128], [128 * T, DCH], [1, GT]])))
            s.add("tensor", lambda e, ch=ch: e.matmul(
                psA[0:128, 0:128], ones_r[:, :], arow[0:1, ch * 128:(ch + 1) * 128],
                start=True, stop=True))
            s.add("tensor", lambda e, ch=ch: e.matmul(
                psB[0:128, 0:128], ones_r[:, :], brow[0:1, ch * 128:(ch + 1) * 128],
                start=True, stop=True))
            for c in range(DCH):
                s.add("vector", lambda e, c=c, ch=ch: e.tensor_tensor(
                    tcomb[:, :], X[:, c * T + ch * 128:c * T + (ch + 1) * 128],
                    psA[0:128, 0:128], op=ALU.mult))
                s.add("vector", lambda e, c=c: e.tensor_tensor(
                    xnb[:, c * 128:(c + 1) * 128], tcomb[:, :], psB[0:128, 0:128],
                    op=ALU.add))
            for wbuf, dstT, bias in ((wq_b, qTt, bq), (wk_b, kTt, bk)):
                for ec in range(DCH):
                    def mm_p(e, wbuf=wbuf, ec=ec):
                        for c in range(DCH):
                            last = e.matmul(
                                psQ[0:128, 0:128],
                                wbuf[:, c * D + ec * 128: c * D + (ec + 1) * 128],
                                xnb[:, c * 128:(c + 1) * 128],
                                start=(c == 0), stop=(c == DCH - 1))
                        return last
                    s.add("tensor", mm_p)
                    s.add("vector", lambda e, dstT=dstT, bias=bias, ec=ec: e.tensor_scalar(
                        out=dstT[:, ec * 128:(ec + 1) * 128], in0=psQ[0:128, 0:128],
                        scalar1=bias[:, ec:ec + 1], scalar2=None, op0=ALU.add))
            for en in range(2):
                def mm_v(e, en=en):
                    for c in range(DCH):
                        last = e.matmul(
                            psA[0:128, 0:512], xnb[:, c * 128:(c + 1) * 128],
                            wv_b[:, c * D + en * 512: c * D + en * 512 + 512],
                            start=(c == 0), stop=(c == DCH - 1))
                    return last
                s.add("tensor", mm_v)
                s.add("vector", lambda e, en=en: e.tensor_copy(
                    vtok[:, en * 512:en * 512 + 512], psA[0:128, 0:512]))

            def mm_l(e):
                for ec in range(DCH):
                    last = e.matmul(psQ[0:128, 0:128], qTt[:, ec * 128:(ec + 1) * 128],
                                    kTt[:, ec * 128:(ec + 1) * 128],
                                    start=(ec == 0), stop=(ec == DCH - 1))
                return last
            s.add("tensor", mm_l)
            s.add("vector", lambda e: e.reduce_max(
                nmax[:, :], psQ[0:128, 0:128], axis=AX.X, negate=True))
            s.add("scalar", lambda e: e.activation(
                out=probs[:, :], in_=psQ[0:128, 0:128], func=AF.Exp,
                bias=nmax[:, :], accum_out=ssum[:, :]))
            s.add("vector", lambda e: e.reciprocal(rsum[:, :], ssum[:, :]))
            s.add("vector", lambda e: e.tensor_scalar(
                out=probs[:, :], in0=probs[:, :], scalar1=rsum[:, :],
                scalar2=None, op0=ALU.mult))
            s.add("tensor", lambda e: e.matmul(
                psT[0:128, 0:128], probs[:, :], idn_b[:, :], start=True, stop=True))
            s.add("vector", lambda e: e.tensor_copy(aTt[:, :], psT[0:128, 0:128]))
            for ec in range(DCH):
                s.add("tensor", lambda e, ec=ec: e.matmul(
                    psQ[0:128, 0:128], vtok[:, ec * 128:(ec + 1) * 128],
                    aTt[:, :], start=True, stop=True))
                s.add("vector", lambda e, ec=ec: e.tensor_scalar(
                    out=tcomb[:, :], in0=psQ[0:128, 0:128],
                    scalar1=bv[:, ec:ec + 1], scalar2=None, op0=ALU.add))
                s.add("vector", lambda e, ec=ec, ch=ch: e.tensor_tensor(
                    tcomb[:, :], tcomb[:, :],
                    gbuf[:, ec * GT + (ch % (GT // 128)) * 128:
                         ec * GT + (ch % (GT // 128) + 1) * 128], op=ALU.mult))
                s.add("vector", lambda e, ec=ec, ch=ch: e.tensor_tensor(
                    X[:, ec * T + ch * 128:ec * T + (ch + 1) * 128],
                    X[:, ec * T + ch * 128:ec * T + (ch + 1) * 128],
                    tcomb[:, :], op=ALU.add))

        s.dma(lambda e: e.dma_start(
            out=yT.rearrange("(c p) t -> p c t", p=128),
            in_=X.ap().rearrange("p (c t) -> p c t", t=T)))

        s.emit(block, step)
    return nc


def emit_ln(s, nc, aF, X, ones_c, ones_r, arow, brow, mT, rT, psA, psB, psT):
    sqs = aF.alloc(512)
    mrow = aF.alloc(512)[0:1, :]
    vrow = aF.alloc(512)[0:1, :]
    for j in range(T // 512):
        sl = slice(j * 512, (j + 1) * 512)

        def mm_mean(e, j=j):
            for c in range(DCH):
                last = e.matmul(psA[0:1, 0:512], ones_c[:, :],
                                X[:, c * T + j * 512: c * T + (j + 1) * 512],
                                start=(c == 0), stop=(c == DCH - 1))
            return last
        s.add("tensor", mm_mean)
        for c in range(DCH):
            s.add("scalar", lambda e, c=c, j=j: e.activation(
                out=sqs[0:128, 0:512], in_=X[:, c * T + j * 512: c * T + (j + 1) * 512],
                func=AF.Square))
            s.add("tensor", lambda e, c=c: e.matmul(
                psB[0:1, 0:512], ones_c[:, :], sqs[0:128, 0:512],
                start=(c == 0), stop=(c == DCH - 1)))
        s.add("vector", lambda e: e.tensor_scalar(
            out=mrow[0:1, 0:512], in0=psA[0:1, 0:512], scalar1=1.0 / D,
            scalar2=None, op0=ALU.mult))
        s.add("vector", lambda e: e.tensor_scalar(
            out=vrow[0:1, 0:512], in0=psB[0:1, 0:512], scalar1=1.0 / D,
            scalar2=EPS, op0=ALU.mult, op1=ALU.add))
        s.add("vector", lambda e, sl=sl: e.tensor_tensor(
            brow[0:1, sl], mrow[0:1, 0:512], mrow[0:1, 0:512], op=ALU.mult))
        s.add("vector", lambda e, sl=sl: e.tensor_tensor(
            vrow[0:1, 0:512], vrow[0:1, 0:512], brow[0:1, sl], op=ALU.subtract))
        s.add("scalar", lambda e: e.activation(
            out=vrow[0:1, 0:512], in_=vrow[0:1, 0:512], func=AF.Ln))
        s.add("scalar", lambda e, sl=sl: e.activation(
            out=arow[0:1, sl], in_=vrow[0:1, 0:512], func=AF.Exp, scale=-0.5))
        s.add("vector", lambda e, sl=sl: e.scalar_tensor_tensor(
            out=brow[0:1, sl], in0=mrow[0:1, 0:512], scalar=-1.0,
            in1=arow[0:1, sl], op0=ALU.mult, op1=ALU.mult))
        for tl in range(4):
            t = j * 4 + tl
            s.add("tensor", lambda e, tl=tl: e.matmul(
                psT[0:128, 0:1], mrow[0:1, tl * 128:(tl + 1) * 128],
                ones_r[0:1, 0:1], start=True, stop=True))
            s.add("vector", lambda e, t=t: e.tensor_copy(mT[:, t:t + 1], psT[0:128, 0:1]))
            s.add("tensor", lambda e, t=t: e.matmul(
                psT[0:128, 0:1], arow[0:1, t * 128:(t + 1) * 128],
                ones_r[0:1, 0:1], start=True, stop=True))
            s.add("vector", lambda e, t=t: e.tensor_copy(rT[:, t:t + 1], psT[0:128, 0:1]))


def _f(a):
    return np.ascontiguousarray(a, dtype=np.float32)


def _bias128(b):
    return _f(np.asarray(b).reshape(DCH, 128).T)


_CONSTS = {
    "idn": np.eye(128, dtype=np.float32),
    "onesc": np.ones((128, 1), dtype=np.float32),
    "onesr": np.ones((1, 128), dtype=np.float32),
}


def l0_in_maps(inputs, xT_list):
    wqT = _f(np.asarray(inputs["L0_attn_Wq"]).T)
    wkT = _f(np.asarray(inputs["L0_attn_Wk"]).T)
    wvT = _f(np.asarray(inputs["L0_attn_Wv"]).T)
    bq = _bias128(inputs["L0_attn_bq"])
    bk = _bias128(inputs["L0_attn_bk"])
    bv = _bias128(inputs["L0_attn_bv"])
    wg = _f(inputs["L0_ffn_w_gate"]); wu = _f(inputs["L0_ffn_w_up"])
    wd = _f(inputs["L0_ffn_w_down"])
    maps = []
    for c in range(NCORES):
        m = dict(_CONSTS)
        m.update({
            "xT": xT_list[c],
            "wqT": wqT, "wkT": wkT, "wvT": wvT,
            "bq": bq, "bk": bk, "bv": bv,
            "wg_sl": _f(wg[c * FSL:(c + 1) * FSL]),
            "wu_sl": _f(wu[c * FSL:(c + 1) * FSL]),
            "wd_sl": _f(wd[c * WDSL:(c + 1) * WDSL]),
        })
        maps.append(m)
    return maps


def run_l0(inputs, xT_list):
    nc = build_l0()
    res = run_bass_kernel_spmd(nc, l0_in_maps(inputs, xT_list),
                               core_ids=list(range(NCORES)))
    return [np.asarray(res.results[c]["yT"]) for c in range(NCORES)]


def kernel(**inputs):
    x = np.asarray(inputs["L0_attn_x"])
    xT_list = [_f(x[c].T) for c in range(NCORES)]
    xT_list = run_l0(inputs, xT_list)
    xT_list = run_l1(inputs, xT_list)
    xT_list = run_l2(inputs, xT_list)
    xT_list = run_l3(inputs, xT_list)
    out = np.stack([np.asarray(y, np.float32).T for y in xT_list])
    return out


W6SL = D // NCORES
H1 = 16
DH1 = 64
HG = 2
NHG = H1 // HG


def build_l1():
    nc = bass.Bass("TRN2", target_bir_lowering=False)
    xT = nc.dram_tensor("xT", [D, T], F32, kind="ExternalInput").ap()
    wqkvT = nc.dram_tensor("wqkvT", [D, 3 * D], F32, kind="ExternalInput").ap()
    bqkv_in = nc.dram_tensor("bqkv", [128, 24], F32, kind="ExternalInput").ap()
    w1wT = nc.dram_tensor("w1wT", [D, 64], F32, kind="ExternalInput").ap()
    w2wT = nc.dram_tensor("w2wT", [D, 64], F32, kind="ExternalInput").ap()
    w1rT = nc.dram_tensor("w1rT", [D, 64], F32, kind="ExternalInput").ap()
    w2rT = nc.dram_tensor("w2rT", [D, 64], F32, kind="ExternalInput").ap()
    bs_in = nc.dram_tensor("bsrow", [1, 16], F32, kind="ExternalInput").ap()
    woT = nc.dram_tensor("woT", [D, D], F32, kind="ExternalInput").ap()
    bo_in = nc.dram_tensor("bo", [128, DCH], F32, kind="ExternalInput").ap()
    w6_sl = nc.dram_tensor("w6_sl", [W6SL, D], F32, kind="ExternalInput").ap()
    b6_in = nc.dram_tensor("b6", [128, DCH], F32, kind="ExternalInput").ap()
    idn_in = nc.dram_tensor("idn", [128, 128], F32, kind="ExternalInput").ap()
    onesc_in = nc.dram_tensor("onesc", [128, 1], F32, kind="ExternalInput").ap()
    onesr_in = nc.dram_tensor("onesr", [1, 128], F32, kind="ExternalInput").ap()
    triu_in = nc.dram_tensor("triu01", [128, 128], F32, kind="ExternalInput").ap()
    cmask_in = nc.dram_tensor("cmask", [128, 128], F32, kind="ExternalInput").ap()
    yT = nc.dram_tensor("yT", [D, T], F32, kind="ExternalOutput").ap()

    w6T_sl_d = nc.dram_tensor("w6T_sl_d", [D, W6SL], BF16)
    w6T_d = nc.dram_tensor("w6T_d", [NCORES * D, W6SL], BF16)
    g_d = nc.dram_tensor("g_d", [D, T], F32)

    with ExitStack() as st:
        def sb(name, shape, dt=F32):
            return st.enter_context(nc.sbuf_tensor(name, shape, dt))

        X = sb("X", [128, DCH * T])
        idn_f = sb("idn_f", [128, 128])
        idn_b = sb("idn_b", [128, 128], BF16)
        triu = sb("triu", [128, 128])
        cmask = sb("cmask_s", [128, 128])
        ones_c = sb("ones_c", [128, 1])
        ones_r = sb("ones_r", [1, 128])
        bqkv = sb("bqkv_s", [128, 24]); bo = sb("bo_s", [128, DCH]); b6 = sb("b6_s", [128, DCH])
        mT = sb("mT", [128, NTILE]); rT = sb("rT", [128, NTILE])
        bs_full = sb("bs_full", [128, 16]); sgn_full = sb("sgn_full", [128, 16])
        bs_row = sb("bs_row", [1, 16])
        nmax = sb("nmax", [128, 1]); ssum = sb("ssum", [128, 1]); rsum = sb("rsum", [128, 1])
        tcomb = sb("tcomb", [128, 128])
        AFT = sb("arenaF", [128, 11520])
        ABT = sb("arenaB", [128, 48640], BF16)
        aF = Arena(AFT, 11520)
        aB = Arena(ABT, 48640)

        ps = [st.enter_context(nc.psum_tensor(f"ps{i}", [128, 512], F32)) for i in range(8)]

        step = st.enter_context(nc.semaphore("step"))
        block = st.enter_context(nc.Block())
        s = Sched()

        arow = aF.alloc(T)[0:1, :]
        brow = aF.alloc(T)[0:1, :]
        zoneF = aF.off

        def resetF():
            aF.off = zoneF

        s.dma(lambda e: e.dma_start(
            out=X.ap().rearrange("p (c t) -> p c t", t=T),
            in_=xT.rearrange("(c p) t -> p c t", p=128)))
        s.dma(lambda e: e.dma_start(out=idn_f[:, :], in_=idn_in))
        s.gdma(lambda e: e.dma_start(out=idn_b[:, :], in_=idn_in))
        s.dma(lambda e: e.dma_start(out=triu[:, :], in_=triu_in))
        s.dma(lambda e: e.dma_start(out=cmask[:, :], in_=cmask_in))
        s.dma(lambda e: e.dma_start(out=ones_c[:, :], in_=onesc_in))
        s.dma(lambda e: e.dma_start(out=ones_r[:, :], in_=onesr_in))
        s.dma(lambda e: e.dma_start(out=bqkv[:, :], in_=bqkv_in))
        s.dma(lambda e: e.dma_start(out=bo[:, :], in_=bo_in))
        s.dma(lambda e: e.dma_start(out=b6[:, :], in_=b6_in))

        q = quant_scratch(aF)
        qin = aF.alloc(D)
        qout = aB.alloc(D)
        ev16 = aB.alloc(128)
        s.dma(lambda e: e.dma_start(out=qin[0:128, 0:D], in_=w6_sl))
        emit_quant(s, q, 128, D, 32, 7.5, 1e-30, 0.125, qin[0:128, 0:D], qout)
        for c in range(DCH):
            s.add("tensor", lambda e, c=c: e.matmul(
                ps[2][0:128, 0:128], qout[0:128, c * 128:(c + 1) * 128],
                idn_b[:, :], start=True, stop=True))
            s.add("vector", lambda e: e.tensor_copy(ev16[0:128, 0:128], ps[2][0:128, 0:128]))
            s.dma(lambda e, c=c: e.dma_start(
                out=w6T_sl_d.ap()[c * 128:(c + 1) * 128, :], in_=ev16[0:128, 0:128]))
        s.add("gpsimd", lambda e: e.collective_compute(
            "AllGather", ALU.bypass, replica_groups=[list(range(NCORES))],
            ins=[w6T_sl_d.ap().opt()], outs=[w6T_d.ap().opt()]))
        resetF(); aB.reset()

        emit_ln(s, nc, aF, X, ones_c, ones_r, arow, brow, mT, rT, ps[0], ps[1], ps[2])
        resetF(); aB.reset()

        s.dma(lambda e: e.dma_start(out=bs_row[:, :], in_=bs_in))
        s.add("tensor", lambda e: e.matmul(
            ps[2][0:128, 0:16], ones_r[:, :], bs_row[:, :], start=True, stop=True))
        s.add("vector", lambda e: e.tensor_copy(bs_full[:, :], ps[2][0:128, 0:16]))
        s.add("scalar", lambda e: e.activation(
            out=sgn_full[:, :], in_=bs_full[:, :], func=AF.Sign))

        q = quant_scratch(aF)
        xtok = aF.alloc(D)
        gsl = aF.alloc(512)
        w6_b = aB.alloc(DCH * D)
        xq_tok = aB.alloc(D)
        xqT = aB.alloc(DCH * GT)
        for kc in range(DCH):
            s.dma(lambda e, kc=kc: e.dma_start(
                out=w6_b[:, kc * D:(kc + 1) * D].rearrange("p (c2 nl) -> p c2 nl", nl=W6SL),
                in_=bass.AP(w6T_d, kc * 128 * W6SL,
                            [[W6SL, 128], [D * W6SL, NCORES], [1, W6SL]])))
        for g in range(NGRP):
            for tl in range(GT // 128):
                tt = g * (GT // 128) + tl
                for c in range(DCH):
                    s.add("tensor", lambda e, c=c, tt=tt: e.matmul(
                        ps[2][0:128, 0:128], X[:, c * T + tt * 128:c * T + (tt + 1) * 128],
                        idn_f[:, :], start=True, stop=True))
                    s.add("vector", lambda e, c=c, tt=tt: e.tensor_scalar(
                        out=xtok[:, c * 128:(c + 1) * 128], in0=ps[2][0:128, 0:128],
                        scalar1=mT[:, tt:tt + 1], scalar2=rT[:, tt:tt + 1],
                        op0=ALU.subtract, op1=ALU.mult))
                emit_quant(s, q, 128, D, 32, 7.5, 1e-30, 0.125, xtok[0:128, 0:D], xq_tok)
                for c in range(DCH):
                    s.add("tensor", lambda e, c=c: e.matmul(
                        ps[2][0:128, 0:128], xq_tok[0:128, c * 128:(c + 1) * 128],
                        idn_b[:, :], start=True, stop=True))
                    s.add("vector", lambda e, c=c, tl=tl: e.tensor_copy(
                        xqT[:, c * GT + tl * 128: c * GT + (tl + 1) * 128],
                        ps[2][0:128, 0:128]))
            for nc2 in range(DCH):
                def mm6(e, nc2=nc2):
                    for kc in range(DCH):
                        last = e.matmul(
                            ps[0][0:128, 0:GT],
                            w6_b[:, kc * D + nc2 * W6SL: kc * D + (nc2 + 1) * W6SL],
                            xqT[:, kc * GT:(kc + 1) * GT],
                            start=(kc == 0), stop=(kc == DCH - 1))
                    return last
                s.add("tensor", mm6)
                s.add("scalar", lambda e, nc2=nc2: e.activation(
                    out=gsl[0:128, 0:GT], in_=ps[0][0:128, 0:GT], func=AF.Silu,
                    bias=b6[:, nc2:nc2 + 1]))
                s.dma(lambda e, nc2=nc2, g=g: e.dma_start(
                    out=bass.AP(g_d, nc2 * 128 * T + g * GT, [[T, 128], [1, GT]]),
                    in_=gsl[0:128, 0:GT]))
        resetF(); aB.reset()

        xnf = aB.alloc(DCH * T)
        jw_tok = aB.alloc(NTILE * 96)
        rd_tok = aB.alloc(NTILE * 96)
        attn_oT = aB.alloc(DCH * T)
        wstr = aB.alloc(2048)

        for ch in range(NTILE):
            s.add("tensor", lambda e, ch=ch: e.matmul(
                ps[0][0:128, 0:128], ones_r[:, :], arow[0:1, ch * 128:(ch + 1) * 128],
                start=True, stop=True))
            s.add("tensor", lambda e, ch=ch: e.matmul(
                ps[1][0:128, 0:128], ones_r[:, :], brow[0:1, ch * 128:(ch + 1) * 128],
                start=True, stop=True))
            for c in range(DCH):
                s.add("vector", lambda e, c=c, ch=ch: e.tensor_tensor(
                    tcomb[:, :], X[:, c * T + ch * 128:c * T + (ch + 1) * 128],
                    ps[0][0:128, 0:128], op=ALU.mult))
                s.add("vector", lambda e, c=c, ch=ch: e.tensor_tensor(
                    xnf[:, c * T + ch * 128:c * T + (ch + 1) * 128],
                    tcomb[:, :], ps[1][0:128, 0:128], op=ALU.add))

        w1T = aF.alloc(2112)
        w2tok = aF.alloc(64); r1tok = aF.alloc(64); r2tok = aF.alloc(64)
        w1tok = aF.alloc(64)
        lines = aF.alloc(96)
        sq = aF.alloc(96)
        nn = aF.alloc(16); rsn = aF.alloc(16)
        s.add("vector", lambda e: e.memset(w1T[0:64, 60:64], 0.0))
        s.gdma(lambda e: e.dma_start(
            out=wstr[:, 0:512].rearrange("p (c n) -> p c n", n=64),
            in_=w1wT.rearrange("(c p) n -> p c n", p=128)))
        for tc in range(4):
            def mmw1(e, tc=tc):
                for c in range(DCH):
                    last = e.matmul(
                        ps[0][0:64, 0:512], wstr[:, c * 64:(c + 1) * 64],
                        xnf[:, c * T + tc * 512:c * T + (tc + 1) * 512],
                        start=(c == 0), stop=(c == DCH - 1))
                return last
            s.add("tensor", mmw1)
            s.add("vector", lambda e, tc=tc: e.tensor_copy(
                w1T[0:64, 64 + tc * 512: 64 + (tc + 1) * 512], ps[0][0:64, 0:512]))
        for wi, wsrc in enumerate((w2wT, w1rT, w2rT)):
            s.gdma(lambda e, wi=wi, wsrc=wsrc: e.dma_start(
                out=wstr[:, 512 + wi * 512:512 + (wi + 1) * 512].rearrange(
                    "p (c n) -> p c n", n=64),
                in_=wsrc.rearrange("(c p) n -> p c n", p=128)))

        PAIRS = [(0, 1), (0, 2), (0, 3), (1, 2), (1, 3), (2, 3)]
        JSG = [1.0, -1.0, 1.0, 1.0, -1.0, 1.0]
        for ch in range(NTILE):
            for wi, dstt in enumerate((w2tok, r1tok, r2tok)):
                def mmrs(e, wi=wi, ch=ch):
                    for c in range(DCH):
                        last = e.matmul(
                            ps[0][0:128, 0:64],
                            xnf[:, c * T + ch * 128:c * T + (ch + 1) * 128],
                            wstr[:, 512 + wi * 512 + c * 64:512 + wi * 512 + (c + 1) * 64],
                            start=(c == 0), stop=(c == DCH - 1))
                    return last
                s.add("tensor", mmrs)
                s.add("vector", lambda e, dstt=dstt: e.tensor_copy(
                    dstt[0:128, 0:64], ps[0][0:128, 0:64]))
            s.add("tensor", lambda e, ch=ch: e.matmul(
                ps[0][0:128, 0:64], w1T[0:64, 63 + ch * 128: 63 + (ch + 1) * 128],
                idn_f[0:64, 0:64], start=True, stop=True))
            s.add("vector", lambda e: e.tensor_copy(w1tok[0:128, 0:64], ps[0][0:128, 0:64]))
            for (p1, p2, dst_tok, permute) in (
                    (w1tok, w2tok, jw_tok, True), (r1tok, r2tok, rd_tok, False)):
                for cix, (i, jx) in enumerate(PAIRS):
                    va = p1.rearrange("p (h f) -> p h f", f=4)
                    vb = p2.rearrange("p (h f) -> p h f", f=4)
                    s.add("vector", lambda e, cix=cix, i=i, jx=jx, va=va, vb=vb: e.tensor_tensor(
                        lines[:, cix * 16:(cix + 1) * 16],
                        va[:, 0:16, i:i + 1].squeeze(2), vb[:, 0:16, jx:jx + 1].squeeze(2),
                        op=ALU.mult))
                    s.add("vector", lambda e, cix=cix, i=i, jx=jx, va=va, vb=vb: e.tensor_tensor(
                        sq[:, cix * 16:(cix + 1) * 16],
                        va[:, 0:16, jx:jx + 1].squeeze(2), vb[:, 0:16, i:i + 1].squeeze(2),
                        op=ALU.mult))
                    s.add("vector", lambda e, cix=cix: e.tensor_tensor(
                        lines[:, cix * 16:(cix + 1) * 16],
                        lines[:, cix * 16:(cix + 1) * 16],
                        sq[:, cix * 16:(cix + 1) * 16], op=ALU.subtract))
                s.add("vector", lambda e: e.tensor_tensor(
                    sq[:, 0:96], lines[:, 0:96], lines[:, 0:96], op=ALU.mult))
                s.add("vector", lambda e: e.tensor_reduce(
                    nn[:, 0:16], sq[:, 0:96].rearrange("p (c h) -> p h c", h=16),
                    axis=AX.X, op=ALU.add))
                s.add("vector", lambda e: e.tensor_scalar(
                    out=nn[:, 0:16], in0=nn[:, 0:16], scalar1=1e-24, scalar2=None,
                    op0=ALU.max))
                s.add("scalar", lambda e: e.activation(
                    out=rsn[:, 0:16], in_=nn[:, 0:16], func=AF.Ln))
                s.add("scalar", lambda e: e.activation(
                    out=rsn[:, 0:16], in_=rsn[:, 0:16], func=AF.Exp, scale=-0.5))
                if permute:
                    for jx in range(6):
                        s.add("vector", lambda e, jx=jx, ch=ch, dst_tok=dst_tok: e.scalar_tensor_tensor(
                            out=dst_tok[:, ch * 96 + jx * 16: ch * 96 + (jx + 1) * 16],
                            in0=lines[:, (5 - jx) * 16:(5 - jx + 1) * 16],
                            scalar=JSG[jx], in1=rsn[:, 0:16],
                            op0=ALU.mult, op1=ALU.mult))
                else:
                    s.add("vector", lambda e, ch=ch, dst_tok=dst_tok: e.tensor_tensor(
                        dst_tok[:, ch * 96:(ch + 1) * 96].rearrange(
                            "p (c h) -> p c h", h=16),
                        lines[:, 0:96].rearrange("p (c h) -> p c h", h=16),
                        rsn[:, 0:16].unsqueeze(1).broadcast_to([128, 6, 16]),
                        op=ALU.mult))
        resetF()

        markB = aB.off
        QTh = aB.alloc(T)
        KTh = aB.alloc(T)
        Vh = aB.alloc(NTILE * HG * DH1)
        jwTh = aB.alloc(T)
        probs = aB.alloc(T)
        probsT1 = aB.alloc(128)
        rdt1 = aB.alloc(128)
        amask = probsT1
        ctb = aB.alloc(128)
        rdmp = aB.alloc(128)
        sprev = probsT1
        L_sb = aF.alloc(T)
        babs = aF.alloc(512)
        c_sb = aF.alloc(8)

        psS = ps[7]
        for hg in range(NHG):
            for which, dstT in ((0, QTh), (1, KTh)):
                base_row = which * D + hg * 128
                s.gdma(lambda e, base_row=base_row: e.dma_start(
                    out=wstr[:, 0:1024].rearrange("p (c n) -> p c n", n=128),
                    in_=bass.AP(wqkvT.tensor, base_row,
                                [[3 * D, 128], [128 * 3 * D, DCH], [1, 128]])))
                bcol = base_row // 128
                for tc in range(4):
                    def mmqk(e, tc=tc):
                        for c in range(DCH):
                            last = e.matmul(
                                ps[0][0:128, 0:512], wstr[:, c * 128:(c + 1) * 128],
                                xnf[:, c * T + tc * 512:c * T + (tc + 1) * 512],
                                start=(c == 0), stop=(c == DCH - 1))
                        return last
                    s.add("tensor", mmqk)
                    if which == 0:
                        s.add("vector", lambda e, tc=tc, bcol=bcol, dstT=dstT: e.tensor_scalar(
                            out=dstT[:, tc * 512:(tc + 1) * 512], in0=ps[0][0:128, 0:512],
                            scalar1=bqkv[:, bcol:bcol + 1], scalar2=0.125,
                            op0=ALU.add, op1=ALU.mult))
                    else:
                        s.add("vector", lambda e, tc=tc, bcol=bcol, dstT=dstT: e.tensor_scalar(
                            out=dstT[:, tc * 512:(tc + 1) * 512], in0=ps[0][0:128, 0:512],
                            scalar1=bqkv[:, bcol:bcol + 1], scalar2=None, op0=ALU.add))
            vbase = 2 * D + hg * 128
            s.gdma(lambda e, vbase=vbase: e.dma_start(
                out=wstr[:, 0:1024].rearrange("p (c n) -> p c n", n=128),
                in_=bass.AP(wqkvT.tensor, vbase,
                            [[3 * D, 128], [128 * 3 * D, DCH], [1, 128]])))
            for ch in range(NTILE):
                def mmv(e, ch=ch):
                    for c in range(DCH):
                        last = e.matmul(
                            ps[0][0:128, 0:128],
                            xnf[:, c * T + ch * 128:c * T + (ch + 1) * 128],
                            wstr[:, c * 128:(c + 1) * 128],
                            start=(c == 0), stop=(c == DCH - 1))
                    return last
                s.add("tensor", mmv)
                s.add("vector", lambda e, ch=ch: e.tensor_copy(
                    Vh[:, ch * 128:(ch + 1) * 128], ps[0][0:128, 0:128]))

            for hl in range(HG):
                h = hg * HG + hl

                def head_slice(src_tok, tile):
                    return src_tok[:, tile * 96:(tile + 1) * 96].rearrange(
                        "p (c h2) -> p c h2", h2=16)[:, :, h:h + 1].squeeze(2)
                for tile in range(NTILE):
                    ap_str = head_slice(jw_tok, tile)
                    s.add("tensor", lambda e, ap_str=ap_str: e.matmul(
                        ps[2][0:6, 0:128], ap_str, idn_b[:, :], start=True, stop=True))
                    s.add("vector", lambda e, tile=tile: e.tensor_copy(
                        jwTh[0:6, tile * 128:(tile + 1) * 128], ps[2][0:6, 0:128]))
                for qb in range(NTILE):
                    W = (qb + 1) * 128
                    jw_str = head_slice(jw_tok, qb)
                    s.add("tensor", lambda e, ap_str=head_slice(rd_tok, qb): e.matmul(
                        ps[2][0:6, 0:128], ap_str, idn_b[:, :], start=True, stop=True))
                    s.add("vector", lambda e: e.tensor_copy(
                        rdt1[0:6, 0:128], ps[2][0:6, 0:128]))
                    if qb > 0:
                        s.add("vector", lambda e, h=h: e.tensor_copy(
                            sprev[0:6, 0:6], psS[0:6, h * 8:h * 8 + 6]))
                        s.add("tensor", lambda e: e.matmul(
                            ps[2][0:6, 0:128], sprev[0:6, 0:6],
                            rdt1[0:6, 0:128], start=True, stop=True))
                        s.add("vector", lambda e: e.tensor_copy(
                            rdmp[0:6, 0:128], ps[2][0:6, 0:128]))
                    s.add("tensor", lambda e, jw_str=jw_str, qb=qb, h=h: e.matmul(
                        psS[0:6, h * 8:h * 8 + 6], jw_str, jw_str,
                        start=(qb == 0), stop=(qb == NTILE - 1)))
                    s.add("tensor", lambda e, qb=qb: e.matmul(
                        ps[4][0:128, 0:128], jwTh[0:6, qb * 128:(qb + 1) * 128],
                        rdt1[0:6, 0:128], start=True, stop=True))
                    s.add("vector", lambda e: e.tensor_tensor(
                        amask[0:128, 0:128], ps[4][0:128, 0:128], triu[:, :], op=ALU.mult))
                    s.add("tensor", lambda e, jw_str=jw_str: e.matmul(
                        ps[2][0:128, 0:6], amask[0:128, 0:128], jw_str,
                        start=True, stop=True))
                    s.add("vector", lambda e: e.tensor_copy(c_sb[0:128, 0:6], ps[2][0:128, 0:6]))
                    s.add("tensor", lambda e: e.matmul(
                        ps[2][0:6, 0:128], c_sb[0:128, 0:6], idn_f[:, :],
                        start=True, stop=True))
                    s.add("vector", lambda e: e.tensor_copy(ctb[0:6, 0:128], ps[2][0:6, 0:128]))
                    nsc = (W + 511) // 512
                    for sc in range(nsc):
                        s0 = sc * 512
                        wdt = min(512, W - s0)

                        def mmstd(e, qb=qb, hl=hl, s0=s0, wdt=wdt):
                            return e.matmul(
                                ps[0][0:128, 0:wdt],
                                QTh[hl * 64:(hl + 1) * 64, qb * 128:(qb + 1) * 128],
                                KTh[hl * 64:(hl + 1) * 64, s0:s0 + wdt],
                                start=True, stop=True)
                        s.add("tensor", mmstd)
                        s.add("tensor", lambda e, qb=qb, s0=s0, wdt=wdt: e.matmul(
                            ps[1][0:128, 0:wdt], ctb[0:6, 0:128],
                            jwTh[0:6, s0:s0 + wdt], start=True, stop=(qb == 0)))
                        if qb > 0:
                            s.add("tensor", lambda e, s0=s0, wdt=wdt: e.matmul(
                                ps[1][0:128, 0:wdt], rdmp[0:6, 0:128],
                                jwTh[0:6, s0:s0 + wdt], start=False, stop=True))
                        s.add("scalar", lambda e, h=h, wdt=wdt: e.activation(
                            out=babs[0:128, 0:wdt], in_=ps[1][0:128, 0:wdt],
                            func=AF.Abs, scale=bs_full[:, h:h + 1]))
                        s.add("vector", lambda e, h=h, s0=s0, wdt=wdt: e.scalar_tensor_tensor(
                            out=L_sb[0:128, s0:s0 + wdt], in0=babs[0:128, 0:wdt],
                            scalar=sgn_full[:, h:h + 1], in1=ps[0][0:128, 0:wdt],
                            op0=ALU.mult, op1=ALU.add))
                    s.add("vector", lambda e, qb=qb: e.tensor_tensor(
                        L_sb[0:128, qb * 128:(qb + 1) * 128],
                        L_sb[0:128, qb * 128:(qb + 1) * 128],
                        cmask[:, :], op=ALU.add))
                    s.add("vector", lambda e, W=W: e.reduce_max(
                        nmax[:, :], L_sb[0:128, 0:W], axis=AX.X, negate=True))
                    s.add("scalar", lambda e, W=W: e.activation(
                        out=probs[0:128, 0:W], in_=L_sb[0:128, 0:W], func=AF.Exp,
                        bias=nmax[:, :], accum_out=ssum[:, :]))
                    s.add("vector", lambda e: e.reciprocal(rsum[:, :], ssum[:, :]))
                    s.add("vector", lambda e, W=W: e.tensor_scalar(
                        out=probs[0:128, 0:W], in0=probs[0:128, 0:W],
                        scalar1=rsum[:, :], scalar2=None, op0=ALU.mult))
                    for kb in range(qb + 1):
                        s.add("tensor", lambda e, kb=kb: e.matmul(
                            ps[2][0:128, 0:128], probs[:, kb * 128:(kb + 1) * 128],
                            idn_b[:, :], start=True, stop=True))
                        s.add("vector", lambda e: e.tensor_copy(
                            probsT1[:, 0:128], ps[2][0:128, 0:128]))
                        s.add("tensor", lambda e, kb=kb, qb=qb, hl=hl: e.matmul(
                            ps[3][0:64, 0:128],
                            Vh[:, kb * 128 + hl * 64: kb * 128 + hl * 64 + 64],
                            probsT1[:, 0:128],
                            start=(kb == 0), stop=(kb == qb)))
                    vcol = 16 + h // 2
                    vpart = (h % 2) * 64
                    oc = h // 2
                    s.add("vector", lambda e, qb=qb, vcol=vcol, vpart=vpart, oc=oc: e.tensor_scalar(
                        out=attn_oT[vpart:vpart + 64, oc * T + qb * 128: oc * T + (qb + 1) * 128],
                        in0=ps[3][0:64, 0:128], scalar1=bqkv[vpart:vpart + 64, vcol:vcol + 1],
                        scalar2=None, op0=ALU.add))

        aB.off = markB
        wo_b = aB.alloc(DCH * D)
        gslb = aF.alloc(DCH * 128)
        s.gdma(lambda e: e.dma_start(
            out=wo_b.rearrange("p (c n) -> p c n", n=D),
            in_=woT.rearrange("(c p) n -> p c n", p=128)))
        for qb in range(NTILE):
            s.dma(lambda e, qb=qb: e.dma_start(
                out=gslb.rearrange("p (c t) -> p c t", t=128),
                in_=bass.AP(g_d, qb * 128, [[T, 128], [128 * T, DCH], [1, 128]])))
            for d2 in range(DCH):
                def mmwo(e, d2=d2, qb=qb):
                    for ic in range(DCH):
                        last = e.matmul(
                            ps[0][0:128, 0:128],
                            wo_b[:, ic * D + d2 * 128: ic * D + (d2 + 1) * 128],
                            attn_oT[:, ic * T + qb * 128: ic * T + (qb + 1) * 128],
                            start=(ic == 0), stop=(ic == DCH - 1))
                    return last
                s.add("tensor", mmwo)
                s.add("vector", lambda e, d2=d2: e.tensor_scalar(
                    out=tcomb[:, :], in0=ps[0][0:128, 0:128],
                    scalar1=bo[:, d2:d2 + 1], scalar2=None, op0=ALU.add))
                s.add("vector", lambda e, d2=d2: e.tensor_tensor(
                    tcomb[:, :], tcomb[:, :], gslb[:, d2 * 128:(d2 + 1) * 128],
                    op=ALU.mult))
                s.add("vector", lambda e, d2=d2, qb=qb: e.tensor_tensor(
                    X[:, d2 * T + qb * 128:d2 * T + (qb + 1) * 128],
                    X[:, d2 * T + qb * 128:d2 * T + (qb + 1) * 128],
                    tcomb[:, :], op=ALU.add))

        s.dma(lambda e: e.dma_start(
            out=yT.rearrange("(c p) t -> p c t", p=128),
            in_=X.ap().rearrange("p (c t) -> p c t", t=T)))
        s.emit(block, step)
    return nc


def l1_in_maps(inputs, xT_list):
    wqkvT = _f(np.asarray(inputs["L1_attn_W_qkv"]).T)
    bqkv = _f(np.asarray(inputs["L1_attn_b_qkv"]).reshape(24, 128).T)
    w1wT = _f(np.asarray(inputs["L1_attn_W1_write"]).T)
    w2wT = _f(np.asarray(inputs["L1_attn_W2_write"]).T)
    w1rT = _f(np.asarray(inputs["L1_attn_W1_read"]).T)
    w2rT = _f(np.asarray(inputs["L1_attn_W2_read"]).T)
    bsrow = _f(np.asarray(inputs["L1_attn_bias_scale"]).reshape(1, 16))
    woT = _f(np.asarray(inputs["L1_attn_W_out"]).T)
    bo = _bias128(inputs["L1_attn_b_out"])
    w6 = _f(inputs["L1_ffn_weight"])
    b6 = _bias128(inputs["L1_ffn_bias"])
    tri = np.zeros((128, 128), np.float32)
    tri[np.triu_indices(128, 0)] = 1.0
    cm = np.where(np.arange(128)[None, :] > np.arange(128)[:, None],
                  np.float32(-1e38), np.float32(0.0)).astype(np.float32)
    maps = []
    for c in range(NCORES):
        m = dict(_CONSTS)
        m.update({
            "xT": xT_list[c], "wqkvT": wqkvT, "bqkv": bqkv,
            "w1wT": w1wT, "w2wT": w2wT, "w1rT": w1rT, "w2rT": w2rT,
            "bsrow": bsrow, "woT": woT, "bo": bo,
            "w6_sl": _f(w6[c * W6SL:(c + 1) * W6SL]), "b6": b6,
            "triu01": tri, "cmask": cm,
        })
        maps.append(m)
    return maps


def run_l1(inputs, xT_list):
    nc = build_l1()
    res = run_bass_kernel_spmd(nc, l1_in_maps(inputs, xT_list),
                               core_ids=list(range(NCORES)))
    return [np.asarray(res.results[c]["yT"]) for c in range(NCORES)]


NE = 8
CAP = 2048


def build_l2():
    nc = bass.Bass("TRN2", target_bir_lowering=False)
    xT = nc.dram_tensor("xT", [D, T], F32, kind="ExternalInput").ap()
    wq_in = nc.dram_tensor("wq", [D, D], F32, kind="ExternalInput").ap()
    wk_in = nc.dram_tensor("wk", [D, 256], F32, kind="ExternalInput").ap()
    wv_in = nc.dram_tensor("wv", [D, 256], F32, kind="ExternalInput").ap()
    wo_in = nc.dram_tensor("wo", [D, D], F32, kind="ExternalInput").ap()
    wr_in = nc.dram_tensor("wr", [D, NE], F32, kind="ExternalInput").ap()
    br_in = nc.dram_tensor("br", [NE, 1], F32, kind="ExternalInput").ap()
    w1_in = nc.dram_tensor("w1e", [NE, D, D], F32, kind="ExternalInput").ap()
    w2_in = nc.dram_tensor("w2e", [NE, D, D], F32, kind="ExternalInput").ap()
    b1_in = nc.dram_tensor("b1e", [128, NE * DCH], F32, kind="ExternalInput").ap()
    b2_in = nc.dram_tensor("b2e", [NE, D], F32, kind="ExternalInput").ap()
    pref_in = nc.dram_tensor("prefmask", [NE, 1], F32, kind="ExternalInput").ap()
    idn_in = nc.dram_tensor("idn", [128, 128], F32, kind="ExternalInput").ap()
    onesc_in = nc.dram_tensor("onesc", [128, 1], F32, kind="ExternalInput").ap()
    onesr_in = nc.dram_tensor("onesr", [1, 128], F32, kind="ExternalInput").ap()
    yT = nc.dram_tensor("yT", [D, T], F32, kind="ExternalOutput").ap()

    cnt_sl_d = nc.dram_tensor("cnt_sl_d", [1, NE], F32)
    cnt_d = nc.dram_tensor("cnt_d", [NCORES, NE], F32)
    gacc_d = nc.dram_tensor("gacc_d", [D, T], F32)
    g_d = nc.dram_tensor("g_d", [D, T], F32)

    with ExitStack() as st:
        def sb(name, shape, dt=F32):
            return st.enter_context(nc.sbuf_tensor(name, shape, dt))

        X = sb("X", [128, DCH * T])
        idn_f = sb("idn_f", [128, 128])
        idn_b = sb("idn_b", [128, 128], BF16)
        ones_c = sb("ones_c", [128, 1])
        ones_r = sb("ones_r", [1, 128])
        ones_b = sb("ones_b", [128, 1], BF16)
        ones_rb = sb("ones_rb", [1, 128], BF16)
        mT = sb("mT", [128, NTILE]); rT = sb("rT", [128, NTILE])
        b1s = sb("b1s", [128, NE * DCH]); b2s = sb("b2s", [NE, D], BF16)
        brs = sb("brs", [NE, 1])
        prefm = sb("prefm", [NE, 1])
        nmax = sb("nmax", [128, 1]); ssum = sb("ssum", [128, 1]); rsum = sb("rsum", [128, 1])
        tcomb = sb("tcomb", [128, 128])
        AFT = sb("arenaF", [128, 11520])
        ABT = sb("arenaB", [128, 47104], BF16)
        aF = Arena(AFT, 11520)
        aB = Arena(ABT, 47104)

        ps = [st.enter_context(nc.psum_tensor(f"ps{i}", [128, 512], F32)) for i in range(8)]

        step = st.enter_context(nc.semaphore("step"))
        block = st.enter_context(nc.Block())
        s = Sched()

        arow = aF.alloc(T)[0:1, :]
        brow = aF.alloc(T)[0:1, :]
        zoneF = aF.off

        def resetF():
            aF.off = zoneF

        s.dma(lambda e: e.dma_start(
            out=X.ap().rearrange("p (c t) -> p c t", t=T),
            in_=xT.rearrange("(c p) t -> p c t", p=128)))
        s.dma(lambda e: e.dma_start(out=idn_f[:, :], in_=idn_in))
        s.gdma(lambda e: e.dma_start(out=idn_b[:, :], in_=idn_in))
        s.dma(lambda e: e.dma_start(out=ones_c[:, :], in_=onesc_in))
        s.dma(lambda e: e.dma_start(out=ones_r[:, :], in_=onesr_in))
        s.gdma(lambda e: e.dma_start(out=ones_b[:, :], in_=onesc_in))
        s.gdma(lambda e: e.dma_start(out=ones_rb[:, :], in_=onesr_in))
        s.dma(lambda e: e.dma_start(out=b1s[:, :], in_=b1_in))
        s.gdma(lambda e: e.dma_start(out=b2s[:, :], in_=b2_in))
        s.dma(lambda e: e.dma_start(out=brs[:, :], in_=br_in))
        s.dma(lambda e: e.dma_start(out=prefm[:, :], in_=pref_in))

        emit_ln(s, nc, aF, X, ones_c, ones_r, arow, brow, mT, rT, ps[0], ps[1], ps[2])
        resetF()

        def emit_xnf(xnf):
            for ch in range(NTILE):
                s.add("tensor", lambda e, ch=ch: e.matmul(
                    ps[0][0:128, 0:128], ones_r[:, :], arow[0:1, ch * 128:(ch + 1) * 128],
                    start=True, stop=True))
                s.add("tensor", lambda e, ch=ch: e.matmul(
                    ps[1][0:128, 0:128], ones_r[:, :], brow[0:1, ch * 128:(ch + 1) * 128],
                    start=True, stop=True))
                for c in range(DCH):
                    s.add("vector", lambda e, c=c, ch=ch: e.tensor_tensor(
                        tcomb[:, :], X[:, c * T + ch * 128:c * T + (ch + 1) * 128],
                        ps[0][0:128, 0:128], op=ALU.mult))
                    s.add("vector", lambda e, c=c, ch=ch, xnf=xnf: e.tensor_tensor(
                        xnf[:, c * T + ch * 128:c * T + (ch + 1) * 128],
                        tcomb[:, :], ps[1][0:128, 0:128], op=ALU.add))

        xnf = aB.alloc(DCH * T)
        wstr = aB.alloc(DCH * NE)
        emit_xnf(xnf)
        logit_sb = aF.alloc(T)
        mask_T = aF.alloc(T)
        pos_T = aF.alloc(T)
        gate_row = logit_sb[0:1, :]
        ltok = aF.alloc(8)
        maskt = aF.alloc(8)
        junk8 = aF.alloc(8)
        keept = aF.alloc(512)
        cnt8 = aF.alloc(8)
        off8 = aF.alloc(8)
        sel_b = aB.alloc(T)
        selrow = aB.alloc(T)
        s.gdma(lambda e: e.dma_start(
            out=wstr[:, 0:NE * DCH].rearrange("p (c n) -> p c n", n=NE),
            in_=wr_in.rearrange("(c p) n -> p c n", p=128)))
        for tc2 in range(4):
            def mmr(e, tc2=tc2):
                for c in range(DCH):
                    last = e.matmul(
                        ps[0][0:NE, 0:512], wstr[:, c * NE:(c + 1) * NE],
                        xnf[:, c * T + tc2 * 512:c * T + (tc2 + 1) * 512],
                        start=(c == 0), stop=(c == DCH - 1))
                return last
            s.add("tensor", mmr)
            s.add("vector", lambda e, tc2=tc2: e.tensor_scalar(
                out=logit_sb[0:NE, tc2 * 512:(tc2 + 1) * 512], in0=ps[0][0:NE, 0:512],
                scalar1=brs[0:NE, 0:1], scalar2=None, op0=ALU.add))
        for tl in range(NTILE):
            s.add("tensor", lambda e, tl=tl: e.matmul(
                ps[2][0:128, 0:NE], logit_sb[0:NE, tl * 128:(tl + 1) * 128],
                idn_f[0:NE, 0:NE], start=True, stop=True))
            s.add("vector", lambda e: e.tensor_copy(ltok[0:128, 0:NE], ps[2][0:128, 0:NE]))
            s.add("vector", lambda e: e.reduce_max(
                nmax[:, :], ltok[0:128, 0:NE], axis=AX.X, negate=True))
            s.add("vector", lambda e: e.tensor_scalar(
                out=maskt[0:128, 0:NE], in0=ltok[0:128, 0:NE],
                scalar1=nmax[:, :], scalar2=-1.0, op0=ALU.add, op1=ALU.mult))
            s.add("vector", lambda e: e.tensor_scalar(
                out=maskt[0:128, 0:NE], in0=maskt[0:128, 0:NE],
                scalar1=0.0, scalar2=None, op0=ALU.is_le))
            s.add("scalar", lambda e: e.activation(
                out=junk8[0:128, 0:NE], in_=ltok[0:128, 0:NE], func=AF.Exp,
                bias=nmax[:, :], accum_out=ssum[:, :]))
            s.add("vector", lambda e: e.reciprocal(rsum[:, :], ssum[:, :]))
            s.add("tensor", lambda e: e.matmul(
                ps[2][0:NE, 0:128], maskt[0:128, 0:NE], idn_f[:, :],
                start=True, stop=True))
            s.add("vector", lambda e, tl=tl: e.tensor_copy(
                mask_T[0:NE, tl * 128:(tl + 1) * 128], ps[2][0:NE, 0:128]))
            s.add("tensor", lambda e: e.matmul(
                ps[2][0:1, 0:128], rsum[:, :], idn_f[:, :], start=True, stop=True))
            s.add("vector", lambda e, tl=tl: e.tensor_copy(
                gate_row[0:1, tl * 128:(tl + 1) * 128], ps[2][0:1, 0:128]))
        s.add("vector", lambda e: e.tensor_tensor_scan(
            out=pos_T[0:NE, 0:T],
            data0=ones_c[0:NE, 0:1].broadcast_to([NE, T]),
            data1=mask_T[0:NE, 0:T],
            initial=0.0, op0=ALU.mult, op1=ALU.add))
        s.add("vector", lambda e: e.tensor_copy(cnt8[0:NE, 0:1], pos_T[0:NE, T - 1:T]))
        s.add("tensor", lambda e: e.matmul(
            ps[2][0:1, 0:NE], cnt8[0:NE, 0:1], idn_f[0:NE, 0:NE], start=True, stop=True))
        s.add("vector", lambda e: e.tensor_copy(junk8[0:1, 0:NE], ps[2][0:1, 0:NE]))
        s.gdma(lambda e: e.dma_start(out=cnt_sl_d.ap(), in_=junk8[0:1, 0:NE]))
        s.add("gpsimd", lambda e: e.collective_compute(
            "AllGather", ALU.bypass, replica_groups=[list(range(NCORES))],
            ins=[cnt_sl_d.ap().opt()], outs=[cnt_d.ap().opt()]))
        s.gdma(lambda e: e.dma_start(out=ltok[0:NCORES, 0:NE], in_=cnt_d.ap()))
        s.add("tensor", lambda e: e.matmul(
            ps[2][0:1, 0:NE], prefm[0:NCORES, 0:1], ltok[0:NCORES, 0:NE],
            start=True, stop=True))
        s.add("vector", lambda e: e.tensor_copy(junk8[0:1, 0:NE], ps[2][0:1, 0:NE]))
        s.add("tensor", lambda e: e.matmul(
            ps[2][0:NE, 0:1], junk8[0:1, 0:NE], ones_r[0:1, 0:1], start=True, stop=True))
        s.add("vector", lambda e: e.tensor_copy(off8[0:NE, 0:1], ps[2][0:NE, 0:1]))
        s.add("vector", lambda e: e.tensor_scalar(
            out=pos_T[0:NE, 0:T], in0=pos_T[0:NE, 0:T], scalar1=off8[0:NE, 0:1],
            scalar2=None, op0=ALU.add))
        for tc2 in range(4):
            sl = slice(tc2 * 512, (tc2 + 1) * 512)
            s.add("tensor", lambda e, tc2=tc2: e.matmul(
                ps[1][0:NE, 0:512], ones_r[0:1, 0:NE],
                gate_row[0:1, tc2 * 512:(tc2 + 1) * 512], start=True, stop=True))
            s.add("vector", lambda e, sl=sl: e.tensor_scalar(
                out=keept[0:NE, 0:512], in0=pos_T[0:NE, sl], scalar1=float(CAP),
                scalar2=None, op0=ALU.is_lt))
            s.add("vector", lambda e, sl=sl: e.tensor_tensor(
                keept[0:NE, 0:512], keept[0:NE, 0:512], mask_T[0:NE, sl], op=ALU.mult))
            s.add("vector", lambda e, sl=sl: e.tensor_tensor(
                sel_b[0:NE, sl], keept[0:NE, 0:512], ps[1][0:NE, 0:512], op=ALU.mult))

        for tc2 in range(4):
            s.add("tensor", lambda e, tc2=tc2: e.matmul(
                ps[1][0:128, 0:512], ones_r[:, :],
                gate_row[0:1, tc2 * 512:(tc2 + 1) * 512], start=True, stop=True))
            for c in range(DCH):
                s.add("vector", lambda e, c=c, tc2=tc2: e.tensor_tensor(
                    xnf[:, c * T + tc2 * 512:c * T + (tc2 + 1) * 512],
                    xnf[:, c * T + tc2 * 512:c * T + (tc2 + 1) * 512],
                    ps[1][0:128, 0:512], op=ALU.mult))

        w1b = aB.alloc(DCH * D)
        w2b = aB.alloc(DCH * D)
        hT = aB.alloc(DCH * 512)
        resetF()
        hrelu = aF.alloc(512)
        yda = aF.alloc(512)
        for e_i in range(NE):
            s.gdma(lambda e, e_i=e_i: e.dma_start(
                out=selrow[0:1, 0:T], in_=sel_b[e_i:e_i + 1, 0:T]))
            s.gdma(lambda e, e_i=e_i: e.dma_start(
                out=w1b.rearrange("p (c n) -> p c n", n=D),
                in_=w1_in[e_i].rearrange("(c p) n -> p c n", p=128)))
            s.gdma(lambda e, e_i=e_i: e.dma_start(
                out=w2b.rearrange("p (c n) -> p c n", n=D),
                in_=w2_in[e_i].rearrange("(c p) n -> p c n", p=128)))
            for tc2 in range(4):
                s.add("tensor", lambda e, tc2=tc2: e.matmul(
                    ps[1][0:128, 0:512], ones_rb[:, :],
                    selrow[0:1, tc2 * 512:(tc2 + 1) * 512],
                    start=True, stop=True))
                for fc in range(DCH):
                    def mmw1(e, fc=fc, tc2=tc2):
                        for dc in range(DCH):
                            last = e.matmul(
                                ps[0][0:128, 0:512],
                                w1b[:, dc * D + fc * 128: dc * D + (fc + 1) * 128],
                                xnf[:, dc * T + tc2 * 512: dc * T + (tc2 + 1) * 512],
                                start=(dc == 0), stop=(dc == DCH - 1))
                        return last
                    s.add("tensor", mmw1)
                    s.add("scalar", lambda e, e_i=e_i, fc=fc: e.activation(
                        out=hrelu[0:128, 0:512], in_=ps[0][0:128, 0:512], func=AF.Relu,
                        bias=b1s[:, e_i * DCH + fc: e_i * DCH + fc + 1]))
                    s.add("vector", lambda e, fc=fc: e.tensor_tensor(
                        hT[:, fc * 512:(fc + 1) * 512], hrelu[0:128, 0:512],
                        ps[1][0:128, 0:512], op=ALU.mult))
                for d2 in range(DCH):
                    def mmw2(e, d2=d2):
                        for fc in range(DCH):
                            last = e.matmul(
                                ps[2][0:128, 0:512],
                                w2b[:, fc * D + d2 * 128: fc * D + (d2 + 1) * 128],
                                hT[:, fc * 512:(fc + 1) * 512],
                                start=(fc == 0), stop=(fc == DCH - 1))
                        return last
                    s.add("tensor", mmw2)
                    s.add("vector", lambda e: e.tensor_copy(
                        yda[0:128, 0:512], ps[2][0:128, 0:512]))
                    if e_i == 0:
                        s.gdma(lambda e, d2=d2, tc2=tc2: e.dma_start(
                            out=bass.AP(gacc_d, d2 * 128 * T + tc2 * 512,
                                        [[T, 128], [1, 512]]),
                            in_=yda[0:128, 0:512]))
                    else:
                        s.gdma(lambda e, d2=d2, tc2=tc2: e.dma_start(
                            out=bass.AP(gacc_d, d2 * 128 * T + tc2 * 512,
                                        [[T, 128], [1, 512]]),
                            in_=yda[0:128, 0:512], accum_op=ALU.add))
        for tc2 in range(4):
            for d2 in range(DCH):
                s.add("tensor", lambda e, d2=d2, tc2=tc2: e.matmul(
                    ps[0][0:128, 0:512], b2s[0:NE, d2 * 128:(d2 + 1) * 128],
                    sel_b[0:NE, tc2 * 512:(tc2 + 1) * 512], start=True, stop=True))
                s.dma(lambda e, d2=d2, tc2=tc2: e.dma_start(
                    out=hrelu[0:128, 0:512],
                    in_=bass.AP(gacc_d, d2 * 128 * T + tc2 * 512, [[T, 128], [1, 512]])))
                s.add("vector", lambda e: e.tensor_tensor(
                    hrelu[0:128, 0:512], hrelu[0:128, 0:512], ps[0][0:128, 0:512],
                    op=ALU.add))
                s.add("scalar", lambda e: e.activation(
                    out=yda[0:128, 0:512], in_=hrelu[0:128, 0:512], func=AF.Silu))
                s.dma(lambda e, d2=d2, tc2=tc2: e.dma_start(
                    out=bass.AP(g_d, d2 * 128 * T + tc2 * 512, [[T, 128], [1, 512]]),
                    in_=yda[0:128, 0:512]))
        aB.reset(); resetF()

        QT = aB.alloc(DCH * T)
        KT = aB.alloc(2 * T)
        VT = aB.alloc(NTILE * 256)
        markA = aB.off
        xnf = aB.alloc(DCH * T)
        wstr = aB.alloc(DCH * 256)
        emit_xnf(xnf)
        for ec in range(DCH):
            s.gdma(lambda e, ec=ec: e.dma_start(
                out=wstr[:, 0:DCH * 128].rearrange("p (c n) -> p c n", n=128),
                in_=bass.AP(wq_in.tensor, ec * 128, [[D, 128], [128 * D, DCH], [1, 128]])))
            for tc2 in range(4):
                def mmq(e, tc2=tc2):
                    for c in range(DCH):
                        last = e.matmul(
                            ps[0][0:128, 0:512], wstr[:, c * 128:(c + 1) * 128],
                            xnf[:, c * T + tc2 * 512:c * T + (tc2 + 1) * 512],
                            start=(c == 0), stop=(c == DCH - 1))
                    return last
                s.add("tensor", mmq)
                s.add("vector", lambda e, ec=ec, tc2=tc2: e.tensor_scalar(
                    out=QT[:, ec * T + tc2 * 512: ec * T + (tc2 + 1) * 512],
                    in0=ps[0][0:128, 0:512], scalar1=0.125, scalar2=None, op0=ALU.mult))
        for ec in range(2):
            s.gdma(lambda e, ec=ec: e.dma_start(
                out=wstr[:, 0:DCH * 128].rearrange("p (c n) -> p c n", n=128),
                in_=bass.AP(wk_in.tensor, ec * 128, [[256, 128], [128 * 256, DCH], [1, 128]])))
            for tc2 in range(4):
                def mmk(e, tc2=tc2):
                    for c in range(DCH):
                        last = e.matmul(
                            ps[0][0:128, 0:512], wstr[:, c * 128:(c + 1) * 128],
                            xnf[:, c * T + tc2 * 512:c * T + (tc2 + 1) * 512],
                            start=(c == 0), stop=(c == DCH - 1))
                    return last
                s.add("tensor", mmk)
                s.add("vector", lambda e, ec=ec, tc2=tc2: e.tensor_copy(
                    KT[:, ec * T + tc2 * 512: ec * T + (tc2 + 1) * 512],
                    ps[0][0:128, 0:512]))
        s.gdma(lambda e: e.dma_start(
            out=wstr[:, 0:DCH * 256].rearrange("p (c n) -> p c n", n=256),
            in_=wv_in.rearrange("(c p) n -> p c n", p=128)))
        for ch in range(NTILE):
            def mmv(e, ch=ch):
                for c in range(DCH):
                    last = e.matmul(
                        ps[0][0:128, 0:256],
                        xnf[:, c * T + ch * 128:c * T + (ch + 1) * 128],
                        wstr[:, c * 256:(c + 1) * 256],
                        start=(c == 0), stop=(c == DCH - 1))
                return last
            s.add("tensor", mmv)
            s.add("vector", lambda e, ch=ch: e.tensor_copy(
                VT[:, ch * 256:(ch + 1) * 256], ps[0][0:128, 0:256]))

        aB.off = markA
        ksl = aB.alloc(T)
        ET = aB.alloc(NTILE * 512)
        soT = aB.alloc(DCH * 512)
        wob = aB.alloc(DCH * D)
        rdeb = aF.alloc(512)
        den_r = aF.alloc(512)[0:1, :]
        gslb = aF.alloc(DCH * 512)
        att_t = aF.alloc(512)
        s.gdma(lambda e: e.dma_start(
            out=wob.rearrange("p (c n) -> p c n", n=D),
            in_=wo_in.rearrange("(c p) n -> p c n", p=128)))
        for qc in range(4):
            for h in range(16):
                g2 = h // 4
                hb = (h % 2) * 64
                s.add("vector", lambda e, g2=g2, hb=hb: e.tensor_copy(
                    ksl[hb:hb + 64, 0:T],
                    KT[(g2 % 2) * 64:(g2 % 2) * 64 + 64, (g2 // 2) * T:(g2 // 2) * T + T]))
                for kb in range(NTILE):
                    s.add("tensor", lambda e, h=h, qc=qc, kb=kb, hb=hb: e.matmul(
                        ps[0][0:128, 0:512],
                        ksl[hb:hb + 64, kb * 128:(kb + 1) * 128],
                        QT[hb:hb + 64,
                           (h // 2) * T + qc * 512:(h // 2) * T + (qc + 1) * 512],
                        start=True, stop=True))
                    s.add("scalar", lambda e, kb=kb: e.activation(
                        out=ET[:, kb * 512:(kb + 1) * 512], in_=ps[0][0:128, 0:512],
                        func=AF.Exp))

                def mmpv(e, h=h):
                    g2 = h // 4
                    for kb in range(NTILE):
                        e.matmul(ps[2][0:64, 0:512],
                                 VT[:, kb * 256 + g2 * 64: kb * 256 + g2 * 64 + 64],
                                 ET[:, kb * 512:(kb + 1) * 512],
                                 start=(kb == 0), stop=(kb == NTILE - 1))
                    for kb in range(NTILE):
                        last = e.matmul(ps[2][64:65, 0:512], ones_b[:, :],
                                        ET[:, kb * 512:(kb + 1) * 512],
                                        start=(kb == 0), stop=(kb == NTILE - 1))
                    return last
                s.add("tensor", mmpv)
                s.add("vector", lambda e: e.tensor_copy(den_r[0:1, 0:512], ps[2][64:65, 0:512]))
                s.add("scalar", lambda e: e.activation(
                    out=den_r[0:1, 0:512], in_=den_r[0:1, 0:512], func=AF.Ln))
                s.add("scalar", lambda e: e.activation(
                    out=den_r[0:1, 0:512], in_=den_r[0:1, 0:512], func=AF.Exp, scale=-1.0))
                s.add("tensor", lambda e: e.matmul(
                    ps[1][0:64, 0:512], ones_r[0:1, 0:64], den_r[0:1, 0:512],
                    start=True, stop=True))
                s.add("vector", lambda e: e.tensor_copy(rdeb[0:64, 0:512], ps[1][0:64, 0:512]))
                s.add("vector", lambda e, h=h: e.tensor_tensor(
                    soT[(h % 2) * 64:(h % 2) * 64 + 64,
                        (h // 2) * 512:(h // 2) * 512 + 512],
                    ps[2][0:64, 0:512], rdeb[0:64, 0:512], op=ALU.mult))
            s.dma(lambda e, qc=qc: e.dma_start(
                out=gslb.rearrange("p (c t) -> p c t", t=512),
                in_=bass.AP(g_d, qc * 512, [[T, 128], [128 * T, DCH], [1, 512]])))
            for d2 in range(DCH):
                def mmwo(e, d2=d2):
                    for ic in range(DCH):
                        last = e.matmul(
                            ps[0][0:128, 0:512],
                            wob[:, ic * D + d2 * 128: ic * D + (d2 + 1) * 128],
                            soT[:, ic * 512:(ic + 1) * 512],
                            start=(ic == 0), stop=(ic == DCH - 1))
                    return last
                s.add("tensor", mmwo)
                s.add("vector", lambda e, d2=d2: e.tensor_tensor(
                    att_t[0:128, 0:512],
                    ps[0][0:128, 0:512], gslb[:, d2 * 512:(d2 + 1) * 512], op=ALU.mult))
                s.add("vector", lambda e, d2=d2, qc=qc: e.tensor_tensor(
                    X[:, d2 * T + qc * 512:d2 * T + (qc + 1) * 512],
                    X[:, d2 * T + qc * 512:d2 * T + (qc + 1) * 512],
                    att_t[0:128, 0:512], op=ALU.add))

        s.dma(lambda e: e.dma_start(
            out=yT.rearrange("(c p) t -> p c t", p=128),
            in_=X.ap().rearrange("p (c t) -> p c t", t=T)))
        s.emit(block, step)
    return nc


def l2_in_maps(inputs, xT_list):
    wq = _f(inputs["L2_attn_Wq"]); wk = _f(inputs["L2_attn_Wk"])
    wv = _f(inputs["L2_attn_Wv"]); wo = _f(inputs["L2_attn_Wo"])
    wr = _f(inputs["L2_ffn_Wr"])
    br = _f(np.asarray(inputs["L2_ffn_br"]).reshape(NE, 1))
    w1 = _f(inputs["L2_ffn_W1"]); w2 = _f(inputs["L2_ffn_W2"])
    b1 = _f(np.asarray(inputs["L2_ffn_b1"]).reshape(NE, DCH, 128)
            .transpose(2, 0, 1).reshape(128, NE * DCH))
    b2 = _f(inputs["L2_ffn_b2"])
    maps = []
    for c in range(NCORES):
        pref = np.zeros((NE, 1), np.float32)
        pref[:c] = 1.0
        m = dict(_CONSTS)
        m.update({
            "xT": xT_list[c], "wq": wq, "wk": wk, "wv": wv, "wo": wo,
            "wr": wr, "br": br, "w1e": w1, "w2e": w2, "b1e": b1, "b2e": b2,
            "prefmask": pref,
        })
        maps.append(m)
    return maps


def run_l2(inputs, xT_list):
    nc = build_l2()
    res = run_bass_kernel_spmd(nc, l2_in_maps(inputs, xT_list),
                               core_ids=list(range(NCORES)))
    return [np.asarray(res.results[c]["yT"]) for c in range(NCORES)]


class _IdCols:

    def __init__(self, col):
        self.col = col

    def __getitem__(self, key):
        return self.col[:, 0:1]


def build_l3():
    nc = bass.Bass("TRN2", target_bir_lowering=False)
    xT = nc.dram_tensor("xT", [D, T], F32, kind="ExternalInput").ap()
    vec_names = ["ln1w", "ln1b", "ln2w", "ln2b", "tdec", "tfst",
                 "tmk", "tmv", "tmr", "ftmk", "ftmr"]
    vins = {n: nc.dram_tensor(n, [128, DCH], F32, kind="ExternalInput").ap()
            for n in vec_names}
    kwT_in = nc.dram_tensor("kwT", [D, D], F32, kind="ExternalInput").ap()
    vwT_in = nc.dram_tensor("vwT", [D, D], F32, kind="ExternalInput").ap()
    rwT_in = nc.dram_tensor("rwT", [D, D], F32, kind="ExternalInput").ap()
    owT_in = nc.dram_tensor("owT", [D, D], F32, kind="ExternalInput").ap()
    frwT_in = nc.dram_tensor("frwT", [D, D], F32, kind="ExternalInput").ap()
    fkwT_in = nc.dram_tensor("fkwT", [D, 4 * D], F32, kind="ExternalInput").ap()
    fvwT_in = nc.dram_tensor("fvwT", [4 * D, D], F32, kind="ExternalInput").ap()
    wg_sl = nc.dram_tensor("wg_sl", [FSL, D], F32, kind="ExternalInput").ap()
    wu_sl = nc.dram_tensor("wu_sl", [FSL, D], F32, kind="ExternalInput").ap()
    wd_sl = nc.dram_tensor("wd_sl", [WDSL, FF], F32, kind="ExternalInput").ap()
    idn_in = nc.dram_tensor("idn", [128, 128], F32, kind="ExternalInput").ap()
    onesc_in = nc.dram_tensor("onesc", [128, 1], F32, kind="ExternalInput").ap()
    onesr_in = nc.dram_tensor("onesr", [1, 128], F32, kind="ExternalInput").ap()
    yT = nc.dram_tensor("yT", [D, T], F32, kind="ExternalOutput").ap()

    wgT_sl_d = nc.dram_tensor("wgT_sl_d", [D, FSL], BF16)
    wuT_sl_d = nc.dram_tensor("wuT_sl_d", [D, FSL], BF16)
    wdT_sl_d = nc.dram_tensor("wdT_sl_d", [FF, WDSL], BF16)
    wgT_d = nc.dram_tensor("wgT_d", [NCORES * D, FSL], BF16)
    wuT_d = nc.dram_tensor("wuT_d", [NCORES * D, FSL], BF16)
    wdT_d = nc.dram_tensor("wdT_d", [NCORES * FF, WDSL], BF16)
    g_d = nc.dram_tensor("g_d", [D, T], F32)

    with ExitStack() as st:
        def sb(name, shape, dt=F32):
            return st.enter_context(nc.sbuf_tensor(name, shape, dt))

        Z = sb("Z", [128, DCH * T])
        idn_f = sb("idn_f", [128, 128])
        idn_b = sb("idn_b", [128, 128], BF16)
        ones_c = sb("ones_c", [128, 1])
        ones_r = sb("ones_r", [1, 128])
        mT = sb("mT", [128, NTILE]); rT = sb("rT", [128, NTILE])
        vsb = {n: sb(n + "_s", [128, DCH]) for n in vec_names}
        adec = sb("adec", [128, DCH]); etf = sb("etf", [128, DCH])
        Ast = sb("Ast", [128, DCH]); Bst = sb("Bst", [128, DCH])
        hprev = sb("hprev", [128, 2 * DCH])
        zcol = sb("zcol", [128, 1])
        AFT = sb("arenaF", [128, 23296])
        ABT = sb("arenaB", [128, 24576], BF16)
        aF = Arena(AFT, 23296)
        aB = Arena(ABT, 24576)

        ps = [st.enter_context(nc.psum_tensor(f"ps{i}", [128, 512], F32)) for i in range(8)]
        psA, psB, psQ = ps[0], ps[1], ps[2]

        step = st.enter_context(nc.semaphore("step"))
        block = st.enter_context(nc.Block())
        s = Sched()

        s.dma(lambda e: e.dma_start(out=idn_f[:, :], in_=idn_in))
        s.gdma(lambda e: e.dma_start(out=idn_b[:, :], in_=idn_in))
        s.dma(lambda e: e.dma_start(out=ones_c[:, :], in_=onesc_in))
        s.dma(lambda e: e.dma_start(out=ones_r[:, :], in_=onesr_in))
        for n in vec_names:
            s.dma(lambda e, n=n: e.dma_start(out=vsb[n][:, :], in_=vins[n]))
        s.add("scalar", lambda e: e.activation(
            out=adec[:, :], in_=vsb["tdec"][:, :], func=AF.Exp))
        s.add("vector", lambda e: e.tensor_scalar(
            out=adec[:, :], in0=adec[:, :], scalar1=-1.0, scalar2=None, op0=ALU.mult))
        s.add("scalar", lambda e: e.activation(
            out=adec[:, :], in_=adec[:, :], func=AF.Exp))
        s.add("scalar", lambda e: e.activation(
            out=etf[:, :], in_=vsb["tfst"][:, :], func=AF.Exp))
        s.add("vector", lambda e: e.memset(Ast[:, :], 0.0))
        s.add("vector", lambda e: e.memset(Bst[:, :], 0.0))
        s.add("vector", lambda e: e.memset(hprev[:, :], 0.0))
        s.add("vector", lambda e: e.memset(zcol[:, :], 0.0))

        q = quant_scratch(aF)
        qin = aF.alloc(QW)
        qout = aB.alloc(QW)
        ev16 = aB.alloc(128)
        emit_fp4_wq(s, ps, idn_b, q, qin, qout, ev16,
                    wg_sl, wu_sl, wd_sl, wgT_sl_d, wuT_sl_d, wdT_sl_d,
                    wgT_d, wuT_d, wdT_d)
        aF.reset(); aB.reset()

        X = aF.alloc(DCH * T)
        s.dma(lambda e: e.dma_start(
            out=X.rearrange("p (c t) -> p c t", t=T),
            in_=xT.rearrange("(c p) t -> p c t", p=128)))
        arow = aF.alloc(T)[0:1, :]
        brow = aF.alloc(T)[0:1, :]
        emit_ln(s, nc, aF, X, ones_c, ones_r, arow, brow, mT, rT, psA, psB, ps[3])
        for ch in range(NTILE):
            s.add("tensor", lambda e, ch=ch: e.matmul(
                psA[0:128, 0:128], ones_r[:, :], arow[0:1, ch * 128:(ch + 1) * 128],
                start=True, stop=True))
            s.add("tensor", lambda e, ch=ch: e.matmul(
                psB[0:128, 0:128], ones_r[:, :], brow[0:1, ch * 128:(ch + 1) * 128],
                start=True, stop=True))
            for c in range(DCH):
                s.add("vector", lambda e, c=c, ch=ch: e.tensor_tensor(
                    Z[:, c * T + ch * 128:c * T + (ch + 1) * 128],
                    X[:, c * T + ch * 128:c * T + (ch + 1) * 128],
                    psA[0:128, 0:128], op=ALU.mult))
                s.add("vector", lambda e, c=c, ch=ch: e.tensor_tensor(
                    Z[:, c * T + ch * 128:c * T + (ch + 1) * 128],
                    Z[:, c * T + ch * 128:c * T + (ch + 1) * 128],
                    psB[0:128, 0:128], op=ALU.add))

        aF.off = 0
        q = quant_scratch(aF)
        xtok = aF.alloc(D)
        hfl = aF.alloc(FSL)
        gbufF = aF.alloc(GT)
        xq_tok = aB.alloc(D)
        xqT = aB.alloc(DCH * GT)
        wgT_buf = aB.alloc(DCH * FSL)
        wuT_buf = aB.alloc(DCH * FSL)
        hq = aB.alloc(FSL)
        hqT = aB.alloc(24 * GT)
        wdT_buf = aB.alloc(D)
        emit_fp4_ffn(s, ps, Z, _IdCols(zcol), _IdCols(ones_c), idn_f, idn_b, q,
                     xtok, hfl, xq_tok, xqT, wgT_buf, wuT_buf, hq, hqT, wdT_buf,
                     gbufF, wgT_d, wuT_d, wdT_d, g_d)

        aF.off = 0
        aB.reset()
        h1 = aF.alloc(DCH * GT)
        Cf = aF.alloc(DCH * GT)
        Vf = aF.alloc(DCH * GT)
        SRf = aF.alloc(DCH * GT)
        cvs = aF.alloc(GT)
        SA = aF.alloc(GT); SB = aF.alloc(GT)
        numb = aF.alloc(GT); denb = aF.alloc(GT)
        scr1 = aF.alloc(GT)
        grow = {}
        for nm in ("m1", "a1", "b1"):
            grow[nm] = aF.alloc(GT)[0:1, :]
        wstr = aB.alloc(4096)
        XK = aB.alloc(DCH * GT)
        XV = aB.alloc(DCH * GT)
        XR = aB.alloc(DCH * GT)
        RWb = aB.alloc(DCH * GT)
        wstr2 = wstr
        KK = aB.alloc(32 * 128)

        def ln_inner(gi, wname, bname, dsth):
            t0 = gi * GT

            def mm_mean(e, t0=t0):
                for c in range(DCH):
                    last = e.matmul(psA[0:1, 0:512], ones_c[:, :],
                                    Z[:, c * T + t0: c * T + t0 + 512],
                                    start=(c == 0), stop=(c == DCH - 1))
                return last
            s.add("tensor", mm_mean)
            for c in range(DCH):
                s.add("scalar", lambda e, c=c, t0=t0: e.activation(
                    out=cvs[0:128, 0:512], in_=Z[:, c * T + t0: c * T + t0 + 512],
                    func=AF.Square))
                s.add("tensor", lambda e, c=c: e.matmul(
                    psB[0:1, 0:512], ones_c[:, :], cvs[0:128, 0:512],
                    start=(c == 0), stop=(c == DCH - 1)))
            s.add("vector", lambda e: e.tensor_scalar(
                out=grow["m1"][0:1, 0:512], in0=psA[0:1, 0:512], scalar1=1.0 / D,
                scalar2=None, op0=ALU.mult))
            s.add("vector", lambda e: e.tensor_scalar(
                out=grow["a1"][0:1, 0:512], in0=psB[0:1, 0:512], scalar1=1.0 / D,
                scalar2=EPS, op0=ALU.mult, op1=ALU.add))
            s.add("vector", lambda e: e.tensor_tensor(
                grow["b1"][0:1, 0:512], grow["m1"][0:1, 0:512], grow["m1"][0:1, 0:512],
                op=ALU.mult))
            s.add("vector", lambda e: e.tensor_tensor(
                grow["a1"][0:1, 0:512], grow["a1"][0:1, 0:512], grow["b1"][0:1, 0:512],
                op=ALU.subtract))
            s.add("scalar", lambda e: e.activation(
                out=grow["a1"][0:1, 0:512], in_=grow["a1"][0:1, 0:512], func=AF.Ln))
            s.add("scalar", lambda e: e.activation(
                out=grow["a1"][0:1, 0:512], in_=grow["a1"][0:1, 0:512],
                func=AF.Exp, scale=-0.5))
            s.add("vector", lambda e: e.scalar_tensor_tensor(
                out=grow["b1"][0:1, 0:512], in0=grow["m1"][0:1, 0:512], scalar=-1.0,
                in1=grow["a1"][0:1, 0:512], op0=ALU.mult, op1=ALU.mult))
            s.add("tensor", lambda e: e.matmul(
                psA[0:128, 0:512], ones_r[:, :], grow["a1"][0:1, 0:512],
                start=True, stop=True))
            s.add("tensor", lambda e: e.matmul(
                psB[0:128, 0:512], ones_r[:, :], grow["b1"][0:1, 0:512],
                start=True, stop=True))
            for c in range(DCH):
                s.add("vector", lambda e, c=c, t0=t0: e.tensor_tensor(
                    scr1[0:128, 0:512], Z[:, c * T + t0: c * T + t0 + 512],
                    psA[0:128, 0:512], op=ALU.mult))
                s.add("vector", lambda e, c=c: e.tensor_tensor(
                    scr1[0:128, 0:512], scr1[0:128, 0:512], psB[0:128, 0:512],
                    op=ALU.add))
                s.add("vector", lambda e, c=c, wname=wname, bname=bname, dsth=dsth:
                      e.tensor_scalar(
                          out=dsth[:, c * 512:(c + 1) * 512], in0=scr1[0:128, 0:512],
                          scalar1=vsb[wname][:, c:c + 1], scalar2=vsb[bname][:, c:c + 1],
                          op0=ALU.mult, op1=ALU.add))

        def mixes(mix_list, hpoff):
            for c in range(DCH):
                for tm, dst in mix_list:
                    tcol = vsb[tm][:, c:c + 1]
                    s.add("vector", lambda e, c=c: e.tensor_tensor(
                        scr1[0:128, 0:511], h1[:, c * 512 + 1:(c + 1) * 512],
                        h1[:, c * 512:c * 512 + 511], op=ALU.subtract))
                    s.add("vector", lambda e, c=c, tcol=tcol, dst=dst:
                          e.scalar_tensor_tensor(
                              out=dst[:, c * 512 + 1:(c + 1) * 512],
                              in0=scr1[0:128, 0:511], scalar=tcol,
                              in1=h1[:, c * 512:c * 512 + 511],
                              op0=ALU.mult, op1=ALU.add))
                    s.add("vector", lambda e, c=c, hpoff=hpoff: e.tensor_tensor(
                        scr1[0:128, 511:512], h1[:, c * 512:c * 512 + 1],
                        hprev[:, hpoff + c:hpoff + c + 1], op=ALU.subtract))
                    s.add("vector", lambda e, c=c, tcol=tcol, dst=dst, hpoff=hpoff:
                          e.scalar_tensor_tensor(
                              out=dst[:, c * 512:c * 512 + 1],
                              in0=scr1[0:128, 511:512], scalar=tcol,
                              in1=hprev[:, hpoff + c:hpoff + c + 1],
                              op0=ALU.mult, op1=ALU.add))
                s.add("vector", lambda e, c=c, hpoff=hpoff: e.tensor_copy(
                    hprev[:, hpoff + c:hpoff + c + 1],
                    h1[:, c * 512 + 511:(c + 1) * 512]))

        def stream_w(src_ap, half):
            s.gdma(lambda e, src_ap=src_ap, half=half: e.dma_start(
                out=wstr.rearrange("p (c n) -> p c n", n=512),
                in_=bass.AP(src_ap.tensor, half * 512,
                            [[D, 128], [128 * D, DCH], [1, 512]])))

        for gi in range(NGRP):
            t0 = gi * GT
            ln_inner(gi, "ln1w", "ln1b", h1)
            mixes([("tmk", XK), ("tmv", XV), ("tmr", XR)], 0)
            for wsrc, fn in ((kwT_in, "exp"), (vwT_in, "copy"), (rwT_in, "sigm")):
                for half in range(2):
                    stream_w(wsrc, half)
                    for dcl in range(4):
                        dc = half * 4 + dcl
                        src = {"exp": XK, "copy": XV, "sigm": XR}[fn]

                        def mmp(e, dcl=dcl, src=src):
                            for c in range(DCH):
                                last = e.matmul(
                                    psQ[0:128, 0:512],
                                    wstr[:, c * 512 + dcl * 128:
                                         c * 512 + (dcl + 1) * 128],
                                    src[:, c * 512:(c + 1) * 512],
                                    start=(c == 0), stop=(c == DCH - 1))
                            return last
                        s.add("tensor", mmp)
                        if fn == "exp":
                            s.add("scalar", lambda e, dc=dc: e.activation(
                                out=Cf[:, dc * 512:(dc + 1) * 512],
                                in_=psQ[0:128, 0:512], func=AF.Exp))
                        elif fn == "copy":
                            s.add("vector", lambda e, dc=dc: e.tensor_copy(
                                Vf[:, dc * 512:(dc + 1) * 512], psQ[0:128, 0:512]))
                        else:
                            s.add("scalar", lambda e, dc=dc: e.activation(
                                out=SRf[:, dc * 512:(dc + 1) * 512],
                                in_=psQ[0:128, 0:512], func=AF.Sigmoid))
            for dc in range(DCH):
                dsl = slice(dc * 512, (dc + 1) * 512)
                s.add("vector", lambda e, dsl=dsl: e.tensor_tensor(
                    cvs[0:128, 0:512], Cf[:, dsl], Vf[:, dsl], op=ALU.mult))
                s.add("vector", lambda e, dc=dc: e.tensor_tensor_scan(
                    out=SA[0:128, 0:512],
                    data0=adec[:, dc:dc + 1].broadcast_to([128, 512]),
                    data1=cvs[0:128, 0:512],
                    initial=Ast[:, dc:dc + 1], op0=ALU.mult, op1=ALU.add))
                s.add("vector", lambda e, dc=dc, dsl=dsl: e.tensor_tensor_scan(
                    out=SB[0:128, 0:512],
                    data0=adec[:, dc:dc + 1].broadcast_to([128, 512]),
                    data1=Cf[:, dsl],
                    initial=Bst[:, dc:dc + 1], op0=ALU.mult, op1=ALU.add))
                s.add("vector", lambda e, dc=dc: e.scalar_tensor_tensor(
                    out=numb[0:128, 0:1], in0=cvs[0:128, 0:1], scalar=etf[:, dc:dc + 1],
                    in1=Ast[:, dc:dc + 1], op0=ALU.mult, op1=ALU.add))
                s.add("vector", lambda e, dc=dc: e.scalar_tensor_tensor(
                    out=numb[0:128, 1:512], in0=cvs[0:128, 1:512],
                    scalar=etf[:, dc:dc + 1], in1=SA[0:128, 0:511],
                    op0=ALU.mult, op1=ALU.add))
                s.add("vector", lambda e, dc=dc: e.scalar_tensor_tensor(
                    out=denb[0:128, 0:1], in0=Cf[:, dc * 512:dc * 512 + 1],
                    scalar=etf[:, dc:dc + 1], in1=Bst[:, dc:dc + 1],
                    op0=ALU.mult, op1=ALU.add))
                s.add("vector", lambda e, dc=dc: e.scalar_tensor_tensor(
                    out=denb[0:128, 1:512], in0=Cf[:, dc * 512 + 1:(dc + 1) * 512],
                    scalar=etf[:, dc:dc + 1], in1=SB[0:128, 0:511],
                    op0=ALU.mult, op1=ALU.add))
                s.add("vector", lambda e, dc=dc: e.tensor_copy(
                    Ast[:, dc:dc + 1], SA[0:128, 511:512]))
                s.add("vector", lambda e, dc=dc: e.tensor_copy(
                    Bst[:, dc:dc + 1], SB[0:128, 511:512]))
                s.add("scalar", lambda e: e.activation(
                    out=denb[0:128, 0:512], in_=denb[0:128, 0:512], func=AF.Ln))
                s.add("scalar", lambda e: e.activation(
                    out=denb[0:128, 0:512], in_=denb[0:128, 0:512],
                    func=AF.Exp, scale=-1.0))
                s.add("vector", lambda e: e.tensor_tensor(
                    numb[0:128, 0:512], numb[0:128, 0:512], denb[0:128, 0:512],
                    op=ALU.mult))
                s.add("vector", lambda e, dc=dc, dsl=dsl: e.tensor_tensor(
                    RWb[:, dsl], numb[0:128, 0:512], SRf[:, dsl], op=ALU.mult))
            for half in range(2):
                stream_w(owT_in, half)
                for c2l in range(4):
                    c2 = half * 4 + c2l

                    def mmo(e, c2l=c2l):
                        for dc in range(DCH):
                            last = e.matmul(
                                psQ[0:128, 0:512],
                                wstr[:, dc * 512 + c2l * 128:
                                     dc * 512 + (c2l + 1) * 128],
                                RWb[:, dc * 512:(dc + 1) * 512],
                                start=(dc == 0), stop=(dc == DCH - 1))
                        return last
                    s.add("tensor", mmo)
                    s.add("vector", lambda e, c2=c2, t0=t0: e.tensor_tensor(
                        Z[:, c2 * T + t0: c2 * T + t0 + 512],
                        Z[:, c2 * T + t0: c2 * T + t0 + 512],
                        psQ[0:128, 0:512], op=ALU.add))
            ln_inner(gi, "ln2w", "ln2b", h1)
            mixes([("ftmk", XK), ("ftmr", XR)], DCH)
            for half in range(2):
                stream_w(frwT_in, half)
                for dcl in range(4):
                    dc = half * 4 + dcl

                    def mmr2(e, dcl=dcl):
                        for c in range(DCH):
                            last = e.matmul(
                                psQ[0:128, 0:512],
                                wstr[:, c * 512 + dcl * 128: c * 512 + (dcl + 1) * 128],
                                XR[:, c * 512:(c + 1) * 512],
                                start=(c == 0), stop=(c == DCH - 1))
                        return last
                    s.add("tensor", mmr2)
                    s.add("scalar", lambda e, dc=dc: e.activation(
                        out=SRf[:, dc * 512:(dc + 1) * 512], in_=psQ[0:128, 0:512],
                        func=AF.Sigmoid))
            for qtr in range(4):
                q0 = qtr * 128
                for fc in range(32):
                    s.gdma(lambda e, fc=fc: e.dma_start(
                        out=wstr2[:, 0:1024].rearrange("p (c n) -> p c n", n=128),
                        in_=bass.AP(fkwT_in.tensor, fc * 128,
                                    [[4 * D, 128], [128 * 4 * D, DCH], [1, 128]])))

                    def mmkk(e, fc=fc, q0=q0):
                        for c in range(DCH):
                            last = e.matmul(
                                psQ[0:128, 0:128],
                                wstr2[:, c * 128:(c + 1) * 128],
                                XK[:, c * 512 + q0: c * 512 + q0 + 128],
                                start=(c == 0), stop=(c == DCH - 1))
                        return last
                    s.add("tensor", mmkk)
                    s.add("scalar", lambda e: e.activation(
                        out=cvs[0:128, 0:128], in_=psQ[0:128, 0:128], func=AF.Relu))
                    s.add("vector", lambda e, fc=fc: e.tensor_tensor(
                        KK[:, fc * 128:(fc + 1) * 128], cvs[0:128, 0:128],
                        cvs[0:128, 0:128], op=ALU.mult))
                for c2 in range(DCH):
                    s.gdma(lambda e, c2=c2: e.dma_start(
                        out=wstr2[:, :].rearrange("p (c n) -> p c n", n=128),
                        in_=bass.AP(fvwT_in.tensor, c2 * 128,
                                    [[D, 128], [128 * D, 4 * DCH], [1, 128]])))

                    def mmkv(e, c2=c2):
                        for fc in range(32):
                            last = e.matmul(
                                psQ[0:128, 0:128],
                                wstr2[:, fc * 128:(fc + 1) * 128],
                                KK[:, fc * 128:(fc + 1) * 128],
                                start=(fc == 0), stop=(fc == 31))
                        return last
                    s.add("tensor", mmkv)
                    s.add("vector", lambda e, c2=c2, q0=q0: e.tensor_tensor(
                        cvs[0:128, 0:128], SRf[:, c2 * 512 + q0: c2 * 512 + q0 + 128],
                        psQ[0:128, 0:128], op=ALU.mult))
                    s.add("vector", lambda e, c2=c2, t0=t0, q0=q0: e.tensor_tensor(
                        Z[:, c2 * T + t0 + q0: c2 * T + t0 + q0 + 128],
                        Z[:, c2 * T + t0 + q0: c2 * T + t0 + q0 + 128],
                        cvs[0:128, 0:128], op=ALU.add))

        aF.off = 0
        xck = aF.alloc(DCH * 512)
        gck = aF.alloc(DCH * 512)
        for tc2 in range(4):
            s.dma(lambda e, tc2=tc2: e.dma_start(
                out=xck.rearrange("p (c t) -> p c t", t=512),
                in_=bass.AP(xT.tensor, tc2 * 512, [[T, 128], [128 * T, DCH], [1, 512]])))
            s.dma(lambda e, tc2=tc2: e.dma_start(
                out=gck.rearrange("p (c t) -> p c t", t=512),
                in_=bass.AP(g_d, tc2 * 512, [[T, 128], [128 * T, DCH], [1, 512]])))
            for c in range(DCH):
                s.add("vector", lambda e, c=c, tc2=tc2: e.tensor_tensor(
                    gck[:, c * 512:(c + 1) * 512],
                    Z[:, c * T + tc2 * 512: c * T + (tc2 + 1) * 512],
                    gck[:, c * 512:(c + 1) * 512], op=ALU.mult))
                s.add("vector", lambda e, c=c: e.tensor_tensor(
                    xck[:, c * 512:(c + 1) * 512], xck[:, c * 512:(c + 1) * 512],
                    gck[:, c * 512:(c + 1) * 512], op=ALU.add))
            s.dma(lambda e, tc2=tc2: e.dma_start(
                out=bass.AP(yT.tensor, tc2 * 512, [[T, 128], [128 * T, DCH], [1, 512]]),
                in_=xck.rearrange("p (c t) -> p c t", t=512)))

        s.emit(block, step)
    return nc


def l3_in_maps(inputs, xT_list):
    def v128(name):
        return _bias128(inputs[name])
    wg = _f(inputs["L3_ffn_w_gate"]); wu = _f(inputs["L3_ffn_w_up"])
    wd = _f(inputs["L3_ffn_w_down"])
    common = {
        "ln1w": v128("L3_attn_ln1_w"), "ln1b": v128("L3_attn_ln1_b"),
        "ln2w": v128("L3_attn_ln2_w"), "ln2b": v128("L3_attn_ln2_b"),
        "tdec": v128("L3_attn_time_decay"), "tfst": v128("L3_attn_time_first"),
        "tmk": v128("L3_attn_tmk"), "tmv": v128("L3_attn_tmv"),
        "tmr": v128("L3_attn_tmr"),
        "ftmk": v128("L3_attn_f_tmk"), "ftmr": v128("L3_attn_f_tmr"),
        "kwT": _f(np.asarray(inputs["L3_attn_att_kw"]).T),
        "vwT": _f(np.asarray(inputs["L3_attn_att_vw"]).T),
        "rwT": _f(np.asarray(inputs["L3_attn_att_rw"]).T),
        "owT": _f(np.asarray(inputs["L3_attn_att_ow"]).T),
        "frwT": _f(np.asarray(inputs["L3_attn_f_rw"]).T),
        "fkwT": _f(np.asarray(inputs["L3_attn_f_kw"]).T),
        "fvwT": _f(np.asarray(inputs["L3_attn_f_vw"]).T),
    }
    maps = []
    for c in range(NCORES):
        m = dict(_CONSTS)
        m.update(common)
        m.update({
            "xT": xT_list[c],
            "wg_sl": _f(wg[c * FSL:(c + 1) * FSL]),
            "wu_sl": _f(wu[c * FSL:(c + 1) * FSL]),
            "wd_sl": _f(wd[c * WDSL:(c + 1) * WDSL]),
        })
        maps.append(m)
    return maps


def run_l3(inputs, xT_list):
    nc = build_l3()
    res = run_bass_kernel_spmd(nc, l3_in_maps(inputs, xT_list),
                               core_ids=list(range(NCORES)))
    return [np.asarray(res.results[c]["yT"]) for c in range(NCORES)]


# version 02
# speedup vs baseline: 1.0216x; 1.0216x over previous
import numpy as np
import ml_dtypes
from contextlib import ExitStack

import concourse.bass as bass
import concourse.mybir as mybir
from concourse.bass_utils import run_bass_kernel_spmd

F32 = mybir.dt.float32
BF16 = mybir.dt.bfloat16
AF = mybir.ActivationFunctionType
ALU = mybir.AluOpType
AX = mybir.AxisListType

NCORES = 8
T = 2048
D = 1024
NTILE = T // 128
DCH = D // 128
EPS = 1e-5
M0 = 1.5 * 2.0 ** 23
BF = ml_dtypes.bfloat16

FF = 2816
FSL = FF // NCORES
WDSL = D // NCORES
NGRP = 4
GT = T // NGRP


class Sched:

    def __init__(self):
        self.ops = []

    def add(self, eng, fn, inc=1):
        self.ops.append((eng, inc, fn))

    def dma(self, fn):
        self.add("sync", fn, 16)

    def gdma(self, fn):
        self.add("gpsimd", fn, 16)

    def emit(self, block, step):
        totals = []
        t = 0
        for _, inc, _ in self.ops:
            totals.append(t)
            t += inc
        final_total = t

        def emit_for(name):
            def _(eng):
                for (en, inc, fn), before in zip(self.ops, totals):
                    if en != name:
                        continue
                    if before > 0:
                        eng.wait_ge(step, before)
                    fn(eng).then_inc(step, inc)
                if name == "sync":
                    eng.wait_ge(step, final_total)
            return _

        block.sync(emit_for("sync"))
        block.gpsimd(emit_for("gpsimd"))
        block.vector(emit_for("vector"))
        block.scalar(emit_for("scalar"))
        block.tensor(emit_for("tensor"))


class Arena:

    def __init__(self, tensor, cols):
        self.t = tensor
        self.cols = cols
        self.off = 0

    def alloc(self, cols):
        cols = (cols + 63) & ~63
        assert self.off + cols <= self.cols, \
            f"arena overflow: {self.off}+{cols} > {self.cols}"
        v = self.t[:, self.off:self.off + cols]
        self.off += cols
        return v

    def reset(self):
        self.off = 0


def emit_quant(s, q, pr, cols, blk, qmax, qeps, base_step, src, dst):
    nb = cols // blk
    bm, sc, a, c2, c4, ms = q["bm"], q["sc"], q["a"], q["c2"], q["c4"], q["ms"]
    v = "vector"
    s.add(v, lambda e: e.tensor_reduce(
        bm[0:pr, 0:nb], src.rearrange("p (b k) -> p b k", k=blk),
        axis=AX.X, op=ALU.max, apply_absolute_value=True))
    s.add(v, lambda e: e.tensor_scalar(
        out=sc[0:pr, 0:nb], in0=bm[0:pr, 0:nb], scalar1=1.0 / qmax,
        scalar2=qeps, op0=ALU.mult, op1=ALU.max))
    s.add(v, lambda e: e.reciprocal(bm[0:pr, 0:nb], sc[0:pr, 0:nb]))
    scb = sc[0:pr, 0:nb].unsqueeze(2).broadcast_to([pr, nb, blk])
    rsb = bm[0:pr, 0:nb].unsqueeze(2).broadcast_to([pr, nb, blk])
    s.add(v, lambda e: e.tensor_tensor(
        a[0:pr, 0:cols].rearrange("p (b k) -> p b k", k=blk),
        src.rearrange("p (b k) -> p b k", k=blk), rsb, op=ALU.mult))
    s.add("scalar", lambda e: e.activation(
        out=ms[0:pr, 0:cols], in_=a[0:pr, 0:cols], func=AF.Abs))
    s.add(v, lambda e: e.tensor_scalar(
        out=c2[0:pr, 0:cols], in0=ms[0:pr, 0:cols], scalar1=2.0,
        scalar2=base_step * M0, op0=ALU.is_ge, op1=ALU.mult))
    s.add(v, lambda e: e.tensor_scalar(
        out=c4[0:pr, 0:cols], in0=ms[0:pr, 0:cols], scalar1=4.0,
        scalar2=2.0 * base_step * M0, op0=ALU.is_ge, op1=ALU.mult))
    s.add(v, lambda e: e.scalar_tensor_tensor(
        out=ms[0:pr, 0:cols], in0=c2[0:pr, 0:cols], scalar=base_step * M0,
        in1=c4[0:pr, 0:cols], op0=ALU.add, op1=ALU.add))
    s.add(v, lambda e: e.tensor_tensor(
        c2[0:pr, 0:cols], a[0:pr, 0:cols], ms[0:pr, 0:cols], op=ALU.add))
    s.add(v, lambda e: e.tensor_tensor(
        c4[0:pr, 0:cols], c2[0:pr, 0:cols], ms[0:pr, 0:cols], op=ALU.subtract))
    s.add(v, lambda e: e.tensor_tensor(
        dst[0:pr, 0:cols].rearrange("p (b k) -> p b k", k=blk),
        c4[0:pr, 0:cols].rearrange("p (b k) -> p b k", k=blk), scb, op=ALU.mult))


QW = 1408


def quant_scratch(aF, cols=QW):
    return {"bm": aF.alloc(cols // 16), "sc": aF.alloc(cols // 16),
            "a": aF.alloc(cols), "c2": aF.alloc(cols),
            "c4": aF.alloc(cols), "ms": aF.alloc(cols)}


POOL_BYTES = 143360


class Ctx:
    def __init__(self, nc, st, pool_bytes=POOL_BYTES):
        self.nc = nc
        self.st = st
        self.prefix = ""
        self.s = Sched()
        self.ps = [st.enter_context(nc.psum_tensor(f"ps{i}", [128, 512], F32))
                   for i in range(8)]
        self.X = st.enter_context(nc.sbuf_tensor("X", [128, DCH * T], F32))
        self.idn_f = st.enter_context(nc.sbuf_tensor("idn_f", [128, 128], F32))
        self.idn_b = st.enter_context(nc.sbuf_tensor("idn_b", [128, 128], BF16))
        self.ones_c = st.enter_context(nc.sbuf_tensor("ones_c", [128, 1], F32))
        self.ones_r = st.enter_context(nc.sbuf_tensor("ones_r", [1, 128], F32))
        self.mT = st.enter_context(nc.sbuf_tensor("mT", [128, NTILE], F32))
        self.rT = st.enter_context(nc.sbuf_tensor("rT", [128, NTILE], F32))
        self.nmax = st.enter_context(nc.sbuf_tensor("nmax", [128, 1], F32))
        self.ssum = st.enter_context(nc.sbuf_tensor("ssum", [128, 1], F32))
        self.rsum = st.enter_context(nc.sbuf_tensor("rsum", [128, 1], F32))
        self.tcomb = st.enter_context(nc.sbuf_tensor("tcomb", [128, 128], F32))
        self.ones_b = st.enter_context(nc.sbuf_tensor("ones_b", [128, 1], BF16))
        self.ones_rb = st.enter_context(nc.sbuf_tensor("ones_rb", [1, 128], BF16))
        nb = pool_bytes // 2
        self.poolB = st.enter_context(nc.sbuf_tensor("pool", [128, nb], BF16))
        self.poolF = self.poolB.ap().bitcast(F32)
        self.pool_bytes = pool_bytes
        self.aF = PoolArenaF(self)
        self.aB = PoolArenaB(self)
        self.xT = nc.dram_tensor("xT", [D, T], F32, kind="ExternalInput").ap()
        self.yT = nc.dram_tensor("yT", [D, T], F32, kind="ExternalOutput").ap()
        self.g_d = nc.dram_tensor("g_d", [D, T], F32)
        self.step = st.enter_context(nc.semaphore("step"))
        sch = self.s
        idn_in = nc.dram_tensor("idn", [128, 128], F32, kind="ExternalInput").ap()
        onesc_in = nc.dram_tensor("onesc", [128, 1], F32, kind="ExternalInput").ap()
        onesr_in = nc.dram_tensor("onesr", [1, 128], F32, kind="ExternalInput").ap()
        self.onesc_src = onesc_in
        self.onesr_src = onesr_in
        sch.dma(lambda e, a=idn_in: e.dma_start(out=self.idn_f[:, :], in_=a))
        sch.gdma(lambda e, a=idn_in: e.dma_start(out=self.idn_b[:, :], in_=a))
        sch.dma(lambda e, a=onesc_in: e.dma_start(out=self.ones_c[:, :], in_=a))
        sch.dma(lambda e, a=onesr_in: e.dma_start(out=self.ones_r[:, :], in_=a))
        sch.gdma(lambda e, a=onesc_in: e.dma_start(out=self.ones_b[:, :], in_=a))
        sch.gdma(lambda e, a=onesr_in: e.dma_start(out=self.ones_rb[:, :], in_=a))

    def sb(self, name, shape, dt=F32):
        return self.st.enter_context(
            self.nc.sbuf_tensor(self.prefix + name, shape, dt))

    def din(self, name, shape):
        return self.nc.dram_tensor(self.prefix + name, shape, F32,
                                   kind="ExternalInput").ap()

    def dint(self, name, shape, dt):
        return self.nc.dram_tensor(self.prefix + name, shape, dt)

    def finish(self, block):
        self.s.emit(block, self.step)


class PoolArenaF:

    def __init__(self, C):
        self.C = C
        self.t = C.poolF
        self.off = 0

    def alloc(self, cols):
        cols = (cols + 63) & ~63
        assert self.off * 4 + cols * 4 + self.C.aB.off * 2 <= self.C.pool_bytes,             f"pool overflow F: {self.off}+{cols} f32 vs B {self.C.aB.off}"
        v = self.t[:, self.off:self.off + cols]
        self.off += cols
        return v

    def reset(self):
        self.off = 0


class PoolArenaB:

    def __init__(self, C):
        self.C = C
        self.t = C.poolB
        self.top = C.pool_bytes // 2
        self.off = 0

    def alloc(self, cols):
        cols = (cols + 63) & ~63
        assert self.C.aF.off * 4 + (self.off + cols) * 2 <= self.C.pool_bytes,             f"pool overflow B: {self.off}+{cols} bf16 vs F {self.C.aF.off}"
        v = self.t[:, self.top - self.off - cols: self.top - self.off]
        self.off += cols
        return v

    def reset(self):
        self.off = 0


def emit_fp4_wq(s, ps, idn_b, q, qin, qout, ev16,
                wg_sl, wu_sl, wd_sl, wgT_sl_d, wuT_sl_d, wdT_sl_d,
                wgT_d, wuT_d, wdT_d):
    psT = ps[3]

    def quant_weight_slice(src_dram, dst_dram, rows, cols):
        nrt = (rows + 127) // 128
        for i in range(nrt):
            r0 = i * 128
            pr = min(rows, r0 + 128) - r0
            for h0 in range(0, cols, QW):
                hw = min(cols - h0, QW)
                s.dma(lambda e, r0=r0, pr=pr, h0=h0, hw=hw: e.dma_start(
                    out=qin[0:pr, 0:hw], in_=src_dram[r0:r0 + pr, h0:h0 + hw]))
                emit_quant(s, q, pr, hw, 16, 6.0, 1e-12, 0.5,
                           qin[0:pr, 0:hw], qout)
                for c in range(hw // 128):
                    k0 = h0 + c * 128
                    s.add("tensor", lambda e, c=c, pr=pr: e.matmul(
                        psT[0:128, 0:pr], qout[0:pr, c * 128:(c + 1) * 128],
                        idn_b[0:pr, 0:pr], start=True, stop=True))
                    s.add("vector", lambda e, pr=pr: e.tensor_copy(
                        ev16[0:128, 0:pr], psT[0:128, 0:pr]))
                    s.dma(lambda e, k0=k0, r0=r0, pr=pr: e.dma_start(
                        out=dst_dram[k0:k0 + 128, r0:r0 + pr],
                        in_=ev16[0:128, 0:pr]))

    quant_weight_slice(wg_sl, wgT_sl_d.ap(), FSL, D)
    quant_weight_slice(wu_sl, wuT_sl_d.ap(), FSL, D)
    quant_weight_slice(wd_sl, wdT_sl_d.ap(), WDSL, FF)
    for src_d, dst_d in ((wgT_sl_d, wgT_d), (wuT_sl_d, wuT_d), (wdT_sl_d, wdT_d)):
        s.add("gpsimd", lambda e, src_d=src_d, dst_d=dst_d: e.collective_compute(
            "AllGather", ALU.bypass, replica_groups=[list(range(NCORES))],
            ins=[src_d.ap().opt()], outs=[dst_d.ap().opt()]))


def emit_fp4_ffn(s, ps, X, mT, rT, idn_f, idn_b, q,
                 xtok, hfl, xq_tok, xqT, wgT_buf, wuT_buf, hq, hqT, wdT_buf,
                 gbuf, wgT_d, wuT_d, wdT_d, g_d):
    psA, psB, psT = ps[0], ps[1], ps[3]
    for g in range(NGRP):
        for tl in range(GT // 128):
            tt = g * (GT // 128) + tl
            for c in range(DCH):
                s.add("tensor", lambda e, c=c, tt=tt: e.matmul(
                    psT[0:128, 0:128], X[:, c * T + tt * 128:c * T + (tt + 1) * 128],
                    idn_f[:, :], start=True, stop=True))
                s.add("vector", lambda e, c=c, tt=tt: e.tensor_scalar(
                    out=xtok[:, c * 128:(c + 1) * 128], in0=psT[0:128, 0:128],
                    scalar1=mT[:, tt:tt + 1], scalar2=rT[:, tt:tt + 1],
                    op0=ALU.subtract, op1=ALU.mult))
            emit_quant(s, q, 128, D, 16, 6.0, 1e-12, 0.5, xtok[0:128, 0:D], xq_tok)
            for c in range(DCH):
                s.add("tensor", lambda e, c=c: e.matmul(
                    psT[0:128, 0:128], xq_tok[0:128, c * 128:(c + 1) * 128],
                    idn_b[:, :], start=True, stop=True))
                s.add("vector", lambda e, c=c, tl=tl: e.tensor_copy(
                    xqT[:, c * GT + tl * 128: c * GT + (tl + 1) * 128],
                    psT[0:128, 0:128]))
        for ncx in range(NCORES):
            s.dma(lambda e, ncx=ncx: e.dma_start(
                out=wgT_buf.rearrange("p (c n) -> p c n", n=FSL),
                in_=wgT_d.ap()[ncx * D:(ncx + 1) * D, :].rearrange(
                    "(c p) n -> p c n", p=128)))
            s.dma(lambda e, ncx=ncx: e.dma_start(
                out=wuT_buf.rearrange("p (c n) -> p c n", n=FSL),
                in_=wuT_d.ap()[ncx * D:(ncx + 1) * D, :].rearrange(
                    "(c p) n -> p c n", p=128)))
            for tl in range(GT // 128):
                def mm_gu(e, tl=tl, buf=None, dst=None):
                    for c in range(DCH):
                        last = e.matmul(
                            dst[0:128, 0:FSL],
                            xqT[:, c * GT + tl * 128: c * GT + (tl + 1) * 128],
                            buf[:, c * FSL:(c + 1) * FSL],
                            start=(c == 0), stop=(c == DCH - 1))
                    return last
                s.add("tensor", lambda e, tl=tl: mm_gu(e, tl, wgT_buf, psA))
                s.add("tensor", lambda e, tl=tl: mm_gu(e, tl, wuT_buf, psB))
                s.add("scalar", lambda e: e.activation(
                    out=hfl[0:128, 0:FSL], in_=psA[0:128, 0:FSL], func=AF.Silu))
                s.add("vector", lambda e: e.tensor_tensor(
                    hfl[0:128, 0:FSL], hfl[0:128, 0:FSL], psB[0:128, 0:FSL],
                    op=ALU.mult))
                emit_quant(s, q, 128, FSL, 16, 6.0, 1e-12, 0.5,
                           hfl[0:128, 0:FSL], hq)
                for sj, (off, w) in enumerate(((0, 128), (128, 128), (256, 96))):
                    sub = ncx * 3 + sj
                    s.add("tensor", lambda e, off=off, w=w: e.matmul(
                        psT[0:w, 0:128], hq[0:128, off:off + w],
                        idn_b[:, :], start=True, stop=True))
                    s.add("vector", lambda e, w=w, sub=sub, tl=tl: e.tensor_copy(
                        hqT[0:w, sub * GT + tl * 128: sub * GT + (tl + 1) * 128],
                        psT[0:w, 0:128]))
        for sub in range(24):
            ncx, sj = sub // 3, sub % 3
            w = 96 if sj == 2 else 128
            nglob = ncx * FSL + sj * 128
            s.dma(lambda e, nglob=nglob, w=w: e.dma_start(
                out=wdT_buf[0:w, :].rearrange("p (c n) -> p c n", n=WDSL),
                in_=bass.AP(wdT_d, nglob * WDSL,
                            [[WDSL, w], [FF * WDSL, 8], [1, WDSL]])))

            def mm_down(e, sub=sub, w=w):
                for c2 in range(DCH):
                    last = e.matmul(ps[c2][0:128, 0:GT],
                                    wdT_buf[0:w, c2 * 128:(c2 + 1) * 128],
                                    hqT[0:w, sub * GT:(sub + 1) * GT],
                                    start=(sub == 0), stop=(sub == 23))
                return last
            s.add("tensor", mm_down)
        for c2 in range(DCH):
            s.add("scalar", lambda e, c2=c2: e.activation(
                out=gbuf[0:128, 0:GT], in_=ps[c2][0:128, 0:GT], func=AF.Silu))
            s.dma(lambda e, g=g, c2=c2: e.dma_start(
                out=bass.AP(g_d, c2 * 128 * T + g * GT, [[T, 128], [1, GT]]),
                in_=gbuf[0:128, 0:GT]))


def emit_l0(C, first=True, last=False):
    nc, s, ps, aF, aB = C.nc, C.s, C.ps, C.aF, C.aB
    X, idn_f, idn_b, ones_c, ones_r = C.X, C.idn_f, C.idn_b, C.ones_c, C.ones_r
    mT, rT, nmax, ssum, rsum, tcomb = C.mT, C.rT, C.nmax, C.ssum, C.rsum, C.tcomb
    xT, yT, g_d = C.xT, C.yT, C.g_d
    psA, psB, psQ, psT = ps[0], ps[1], ps[2], ps[3]
    sb = C.sb

    wqT = C.din("wqT", [D, D])
    wkT = C.din("wkT", [D, D])
    wvT = C.din("wvT", [D, D])
    bq_in = C.din("bq", [128, DCH])
    bk_in = C.din("bk", [128, DCH])
    bv_in = C.din("bv", [128, DCH])
    wg_sl = C.din("wg_sl", [FSL, D])
    wu_sl = C.din("wu_sl", [FSL, D])
    wd_sl = C.din("wd_sl", [WDSL, FF])
    wgT_sl_d = C.dint("wgT_sl_d", [D, FSL], BF16)
    wuT_sl_d = C.dint("wuT_sl_d", [D, FSL], BF16)
    wdT_sl_d = C.dint("wdT_sl_d", [FF, WDSL], BF16)
    wgT_d = C.dint("wgT_d", [NCORES * D, FSL], BF16)
    wuT_d = C.dint("wuT_d", [NCORES * D, FSL], BF16)
    wdT_d = C.dint("wdT_d", [NCORES * FF, WDSL], BF16)

    bq = sb("bq_s", [128, DCH]); bk = sb("bk_s", [128, DCH]); bv = sb("bv_s", [128, DCH])
    arow = aF.alloc(T)[0:1, :]
    brow = aF.alloc(T)[0:1, :]
    gbuf = aF.alloc(DCH * GT)
    zoneF0 = aF.off

    if first:
        s.dma(lambda e: e.dma_start(
            out=X.ap().rearrange("p (c t) -> p c t", t=T),
            in_=xT.rearrange("(c p) t -> p c t", p=128)))
    s.dma(lambda e: e.dma_start(out=bq[:, :], in_=bq_in))
    s.dma(lambda e: e.dma_start(out=bk[:, :], in_=bk_in))
    s.dma(lambda e: e.dma_start(out=bv[:, :], in_=bv_in))

    q = quant_scratch(aF)
    qin = aF.alloc(QW)
    qout = aB.alloc(QW)
    ev16 = aB.alloc(128)

    def quant_weight_slice(src_dram, dst_dram, rows, cols):
        nrt = (rows + 127) // 128
        for i in range(nrt):
            r0 = i * 128
            pr = min(rows, r0 + 128) - r0
            for h0 in range(0, cols, QW):
                hw = min(cols - h0, QW)
                s.dma(lambda e, r0=r0, pr=pr, h0=h0, hw=hw: e.dma_start(
                    out=qin[0:pr, 0:hw], in_=src_dram[r0:r0 + pr, h0:h0 + hw]))
                emit_quant(s, q, pr, hw, 16, 6.0, 1e-12, 0.5,
                           qin[0:pr, 0:hw], qout)
                for c in range(hw // 128):
                    k0 = h0 + c * 128
                    s.add("tensor", lambda e, c=c, pr=pr: e.matmul(
                        psT[0:128, 0:pr], qout[0:pr, c * 128:(c + 1) * 128],
                        idn_b[0:pr, 0:pr], start=True, stop=True))
                    s.add("vector", lambda e, pr=pr: e.tensor_copy(
                        ev16[0:128, 0:pr], psT[0:128, 0:pr]))
                    s.dma(lambda e, k0=k0, r0=r0, pr=pr: e.dma_start(
                        out=dst_dram[k0:k0 + 128, r0:r0 + pr],
                        in_=ev16[0:128, 0:pr]))

    quant_weight_slice(wg_sl, wgT_sl_d.ap(), FSL, D)
    quant_weight_slice(wu_sl, wuT_sl_d.ap(), FSL, D)
    quant_weight_slice(wd_sl, wdT_sl_d.ap(), WDSL, FF)
    for src_d, dst_d in ((wgT_sl_d, wgT_d), (wuT_sl_d, wuT_d), (wdT_sl_d, wdT_d)):
        s.add("gpsimd", lambda e, src_d=src_d, dst_d=dst_d: e.collective_compute(
            "AllGather", ALU.bypass, replica_groups=[list(range(NCORES))],
            ins=[src_d.ap().opt()], outs=[dst_d.ap().opt()]))
    aF.off = zoneF0; aB.reset()

    emit_ln(s, nc, aF, X, ones_c, ones_r, arow, brow, mT, rT, psA, psB, psT)
    aF.off = zoneF0; aB.reset()

    q = quant_scratch(aF)
    xtok = aF.alloc(D)
    hfl = aF.alloc(FSL)
    xq_tok = aB.alloc(D)
    xqT = aB.alloc(DCH * GT)
    wgT_buf = aB.alloc(DCH * FSL)
    wuT_buf = aB.alloc(DCH * FSL)
    hq = aB.alloc(FSL)
    hqT = aB.alloc(24 * GT)
    wdT_buf = aB.alloc(D)

    for g in range(NGRP):
        for tl in range(GT // 128):
            tt = g * (GT // 128) + tl
            for c in range(DCH):
                s.add("tensor", lambda e, c=c, tt=tt: e.matmul(
                    psT[0:128, 0:128], X[:, c * T + tt * 128:c * T + (tt + 1) * 128],
                    idn_f[:, :], start=True, stop=True))
                s.add("vector", lambda e, c=c, tt=tt: e.tensor_scalar(
                    out=xtok[:, c * 128:(c + 1) * 128], in0=psT[0:128, 0:128],
                    scalar1=mT[:, tt:tt + 1], scalar2=rT[:, tt:tt + 1],
                    op0=ALU.subtract, op1=ALU.mult))
            emit_quant(s, q, 128, D, 16, 6.0, 1e-12, 0.5, xtok[0:128, 0:D], xq_tok)
            for c in range(DCH):
                s.add("tensor", lambda e, c=c: e.matmul(
                    psT[0:128, 0:128], xq_tok[0:128, c * 128:(c + 1) * 128],
                    idn_b[:, :], start=True, stop=True))
                s.add("vector", lambda e, c=c, tl=tl: e.tensor_copy(
                    xqT[:, c * GT + tl * 128: c * GT + (tl + 1) * 128],
                    psT[0:128, 0:128]))
        for ncx in range(NCORES):
            s.dma(lambda e, ncx=ncx: e.dma_start(
                out=wgT_buf.rearrange("p (c n) -> p c n", n=FSL),
                in_=wgT_d.ap()[ncx * D:(ncx + 1) * D, :].rearrange(
                    "(c p) n -> p c n", p=128)))
            s.dma(lambda e, ncx=ncx: e.dma_start(
                out=wuT_buf.rearrange("p (c n) -> p c n", n=FSL),
                in_=wuT_d.ap()[ncx * D:(ncx + 1) * D, :].rearrange(
                    "(c p) n -> p c n", p=128)))
            for tl in range(GT // 128):
                def mm_gu(e, tl=tl, buf=None, dst=None):
                    for c in range(DCH):
                        last = e.matmul(
                            dst[0:128, 0:FSL],
                            xqT[:, c * GT + tl * 128: c * GT + (tl + 1) * 128],
                            buf[:, c * FSL:(c + 1) * FSL],
                            start=(c == 0), stop=(c == DCH - 1))
                    return last
                s.add("tensor", lambda e, tl=tl: mm_gu(e, tl, wgT_buf, psA))
                s.add("tensor", lambda e, tl=tl: mm_gu(e, tl, wuT_buf, psB))
                s.add("scalar", lambda e: e.activation(
                    out=hfl[0:128, 0:FSL], in_=psA[0:128, 0:FSL], func=AF.Silu))
                s.add("vector", lambda e: e.tensor_tensor(
                    hfl[0:128, 0:FSL], hfl[0:128, 0:FSL], psB[0:128, 0:FSL],
                    op=ALU.mult))
                emit_quant(s, q, 128, FSL, 16, 6.0, 1e-12, 0.5,
                           hfl[0:128, 0:FSL], hq)
                for sj, (off, w) in enumerate(((0, 128), (128, 128), (256, 96))):
                    sub = ncx * 3 + sj
                    s.add("tensor", lambda e, off=off, w=w: e.matmul(
                        psT[0:w, 0:128], hq[0:128, off:off + w],
                        idn_b[:, :], start=True, stop=True))
                    s.add("vector", lambda e, w=w, sub=sub, tl=tl: e.tensor_copy(
                        hqT[0:w, sub * GT + tl * 128: sub * GT + (tl + 1) * 128],
                        psT[0:w, 0:128]))
        for sub in range(24):
            ncx, sj = sub // 3, sub % 3
            w = 96 if sj == 2 else 128
            nglob = ncx * FSL + sj * 128
            s.dma(lambda e, nglob=nglob, w=w: e.dma_start(
                out=wdT_buf[0:w, :].rearrange("p (c n) -> p c n", n=WDSL),
                in_=bass.AP(wdT_d, nglob * WDSL,
                            [[WDSL, w], [FF * WDSL, 8], [1, WDSL]])))

            def mm_down(e, sub=sub, w=w):
                for c2 in range(DCH):
                    last = e.matmul(ps[c2][0:128, 0:GT],
                                    wdT_buf[0:w, c2 * 128:(c2 + 1) * 128],
                                    hqT[0:w, sub * GT:(sub + 1) * GT],
                                    start=(sub == 0), stop=(sub == 23))
                return last
            s.add("tensor", mm_down)
        for c2 in range(DCH):
            s.add("scalar", lambda e, c2=c2: e.activation(
                out=gbuf[:, c2 * GT:(c2 + 1) * GT], in_=ps[c2][0:128, 0:GT],
                func=AF.Silu))
        s.dma(lambda e, g=g: e.dma_start(
            out=bass.AP(g_d, g * GT, [[T, 128], [128 * T, DCH], [1, GT]]),
            in_=gbuf.rearrange("p (c t) -> p c t", t=GT)))
    aF.off = zoneF0; aB.reset()

    wq_b = aB.alloc(DCH * D)
    wk_b = aB.alloc(DCH * D)
    wv_b = aB.alloc(DCH * D)
    xnb = aB.alloc(DCH * 128)
    qTt = aB.alloc(DCH * 128)
    kTt = aB.alloc(DCH * 128)
    vtok = aB.alloc(D)
    probs = aB.alloc(128)
    aTt = aB.alloc(128)
    for wbuf, wsrc in ((wq_b, wqT), (wk_b, wkT), (wv_b, wvT)):
        s.gdma(lambda e, wbuf=wbuf, wsrc=wsrc: e.dma_start(
            out=wbuf.rearrange("p (c n) -> p c n", n=D),
            in_=wsrc.rearrange("(c p) n -> p c n", p=128)))

    for ch in range(NTILE):
        if ch % (GT // 128) == 0:
            g = ch // (GT // 128)
            s.dma(lambda e, g=g: e.dma_start(
                out=gbuf.rearrange("p (c t) -> p c t", t=GT),
                in_=bass.AP(g_d, g * GT, [[T, 128], [128 * T, DCH], [1, GT]])))
        s.add("tensor", lambda e, ch=ch: e.matmul(
            psA[0:128, 0:128], ones_r[:, :], arow[0:1, ch * 128:(ch + 1) * 128],
            start=True, stop=True))
        s.add("tensor", lambda e, ch=ch: e.matmul(
            psB[0:128, 0:128], ones_r[:, :], brow[0:1, ch * 128:(ch + 1) * 128],
            start=True, stop=True))
        for c in range(DCH):
            s.add("vector", lambda e, c=c, ch=ch: e.tensor_tensor(
                tcomb[:, :], X[:, c * T + ch * 128:c * T + (ch + 1) * 128],
                psA[0:128, 0:128], op=ALU.mult))
            s.add("vector", lambda e, c=c: e.tensor_tensor(
                xnb[:, c * 128:(c + 1) * 128], tcomb[:, :], psB[0:128, 0:128],
                op=ALU.add))
        for wbuf, dstT, bias in ((wq_b, qTt, bq), (wk_b, kTt, bk)):
            for ec in range(DCH):
                def mm_p(e, wbuf=wbuf, ec=ec):
                    for c in range(DCH):
                        last = e.matmul(
                            psQ[0:128, 0:128],
                            wbuf[:, c * D + ec * 128: c * D + (ec + 1) * 128],
                            xnb[:, c * 128:(c + 1) * 128],
                            start=(c == 0), stop=(c == DCH - 1))
                    return last
                s.add("tensor", mm_p)
                s.add("vector", lambda e, dstT=dstT, bias=bias, ec=ec: e.tensor_scalar(
                    out=dstT[:, ec * 128:(ec + 1) * 128], in0=psQ[0:128, 0:128],
                    scalar1=bias[:, ec:ec + 1], scalar2=None, op0=ALU.add))
        for en in range(2):
            def mm_v(e, en=en):
                for c in range(DCH):
                    last = e.matmul(
                        psA[0:128, 0:512], xnb[:, c * 128:(c + 1) * 128],
                        wv_b[:, c * D + en * 512: c * D + en * 512 + 512],
                        start=(c == 0), stop=(c == DCH - 1))
                return last
            s.add("tensor", mm_v)
            s.add("vector", lambda e, en=en: e.tensor_copy(
                vtok[:, en * 512:en * 512 + 512], psA[0:128, 0:512]))

        def mm_l(e):
            for ec in range(DCH):
                last = e.matmul(psQ[0:128, 0:128], qTt[:, ec * 128:(ec + 1) * 128],
                                kTt[:, ec * 128:(ec + 1) * 128],
                                start=(ec == 0), stop=(ec == DCH - 1))
            return last
        s.add("tensor", mm_l)
        s.add("vector", lambda e: e.reduce_max(
            nmax[:, :], psQ[0:128, 0:128], axis=AX.X, negate=True))
        s.add("scalar", lambda e: e.activation(
            out=probs[:, :], in_=psQ[0:128, 0:128], func=AF.Exp,
            bias=nmax[:, :], accum_out=ssum[:, :]))
        s.add("vector", lambda e: e.reciprocal(rsum[:, :], ssum[:, :]))
        s.add("vector", lambda e: e.tensor_scalar(
            out=probs[:, :], in0=probs[:, :], scalar1=rsum[:, :],
            scalar2=None, op0=ALU.mult))
        s.add("tensor", lambda e: e.matmul(
            psT[0:128, 0:128], probs[:, :], idn_b[:, :], start=True, stop=True))
        s.add("vector", lambda e: e.tensor_copy(aTt[:, :], psT[0:128, 0:128]))
        for ec in range(DCH):
            s.add("tensor", lambda e, ec=ec: e.matmul(
                psQ[0:128, 0:128], vtok[:, ec * 128:(ec + 1) * 128],
                aTt[:, :], start=True, stop=True))
            s.add("vector", lambda e, ec=ec: e.tensor_scalar(
                out=tcomb[:, :], in0=psQ[0:128, 0:128],
                scalar1=bv[:, ec:ec + 1], scalar2=None, op0=ALU.add))
            s.add("vector", lambda e, ec=ec, ch=ch: e.tensor_tensor(
                tcomb[:, :], tcomb[:, :],
                gbuf[:, ec * GT + (ch % (GT // 128)) * 128:
                     ec * GT + (ch % (GT // 128) + 1) * 128], op=ALU.mult))
            s.add("vector", lambda e, ec=ec, ch=ch: e.tensor_tensor(
                X[:, ec * T + ch * 128:ec * T + (ch + 1) * 128],
                X[:, ec * T + ch * 128:ec * T + (ch + 1) * 128],
                tcomb[:, :], op=ALU.add))


    if last:
        s.dma(lambda e: e.dma_start(
            out=yT.rearrange("(c p) t -> p c t", p=128),
            in_=X.ap().rearrange("p (c t) -> p c t", t=T)))
    aF.reset(); aB.reset()


def build_l0():
    nc = bass.Bass("TRN2", target_bir_lowering=False)
    with ExitStack() as st:
        C = Ctx(nc, st)
        emit_l0(C, first=True, last=True)
        block = st.enter_context(nc.Block())
        C.finish(block)
    return nc


def emit_ln(s, nc, aF, X, ones_c, ones_r, arow, brow, mT, rT, psA, psB, psT):
    sqs = aF.alloc(512)
    mrow = aF.alloc(512)[0:1, :]
    vrow = aF.alloc(512)[0:1, :]
    for j in range(T // 512):
        sl = slice(j * 512, (j + 1) * 512)

        def mm_mean(e, j=j):
            for c in range(DCH):
                last = e.matmul(psA[0:1, 0:512], ones_c[:, :],
                                X[:, c * T + j * 512: c * T + (j + 1) * 512],
                                start=(c == 0), stop=(c == DCH - 1))
            return last
        s.add("tensor", mm_mean)
        for c in range(DCH):
            s.add("scalar", lambda e, c=c, j=j: e.activation(
                out=sqs[0:128, 0:512], in_=X[:, c * T + j * 512: c * T + (j + 1) * 512],
                func=AF.Square))
            s.add("tensor", lambda e, c=c: e.matmul(
                psB[0:1, 0:512], ones_c[:, :], sqs[0:128, 0:512],
                start=(c == 0), stop=(c == DCH - 1)))
        s.add("vector", lambda e: e.tensor_scalar(
            out=mrow[0:1, 0:512], in0=psA[0:1, 0:512], scalar1=1.0 / D,
            scalar2=None, op0=ALU.mult))
        s.add("vector", lambda e: e.tensor_scalar(
            out=vrow[0:1, 0:512], in0=psB[0:1, 0:512], scalar1=1.0 / D,
            scalar2=EPS, op0=ALU.mult, op1=ALU.add))
        s.add("vector", lambda e, sl=sl: e.tensor_tensor(
            brow[0:1, sl], mrow[0:1, 0:512], mrow[0:1, 0:512], op=ALU.mult))
        s.add("vector", lambda e, sl=sl: e.tensor_tensor(
            vrow[0:1, 0:512], vrow[0:1, 0:512], brow[0:1, sl], op=ALU.subtract))
        s.add("scalar", lambda e: e.activation(
            out=vrow[0:1, 0:512], in_=vrow[0:1, 0:512], func=AF.Ln))
        s.add("scalar", lambda e, sl=sl: e.activation(
            out=arow[0:1, sl], in_=vrow[0:1, 0:512], func=AF.Exp, scale=-0.5))
        s.add("vector", lambda e, sl=sl: e.scalar_tensor_tensor(
            out=brow[0:1, sl], in0=mrow[0:1, 0:512], scalar=-1.0,
            in1=arow[0:1, sl], op0=ALU.mult, op1=ALU.mult))
        for tl in range(4):
            t = j * 4 + tl
            s.add("tensor", lambda e, tl=tl: e.matmul(
                psT[0:128, 0:1], mrow[0:1, tl * 128:(tl + 1) * 128],
                ones_r[0:1, 0:1], start=True, stop=True))
            s.add("vector", lambda e, t=t: e.tensor_copy(mT[:, t:t + 1], psT[0:128, 0:1]))
            s.add("tensor", lambda e, t=t: e.matmul(
                psT[0:128, 0:1], arow[0:1, t * 128:(t + 1) * 128],
                ones_r[0:1, 0:1], start=True, stop=True))
            s.add("vector", lambda e, t=t: e.tensor_copy(rT[:, t:t + 1], psT[0:128, 0:1]))


def _f(a):
    return np.ascontiguousarray(a, dtype=np.float32)


def _tile_half(w):
    return _f(np.asarray(w).T.reshape(DCH, 128, 2, 512).transpose(2, 0, 1, 3))


def _bias128(b):
    return _f(np.asarray(b).reshape(DCH, 128).T)


_CONSTS = {
    "idn": np.eye(128, dtype=np.float32),
    "onesc": np.ones((128, 1), dtype=np.float32),
    "onesr": np.ones((1, 128), dtype=np.float32),
}


def l0_in_maps(inputs, xT_list):
    wqT = _f(np.asarray(inputs["L0_attn_Wq"]).T)
    wkT = _f(np.asarray(inputs["L0_attn_Wk"]).T)
    wvT = _f(np.asarray(inputs["L0_attn_Wv"]).T)
    bq = _bias128(inputs["L0_attn_bq"])
    bk = _bias128(inputs["L0_attn_bk"])
    bv = _bias128(inputs["L0_attn_bv"])
    wg = _f(inputs["L0_ffn_w_gate"]); wu = _f(inputs["L0_ffn_w_up"])
    wd = _f(inputs["L0_ffn_w_down"])
    maps = []
    for c in range(NCORES):
        m = dict(_CONSTS)
        m.update({
            "xT": xT_list[c],
            "wqT": wqT, "wkT": wkT, "wvT": wvT,
            "bq": bq, "bk": bk, "bv": bv,
            "wg_sl": _f(wg[c * FSL:(c + 1) * FSL]),
            "wu_sl": _f(wu[c * FSL:(c + 1) * FSL]),
            "wd_sl": _f(wd[c * WDSL:(c + 1) * WDSL]),
        })
        maps.append(m)
    return maps


def run_l0(inputs, xT_list):
    nc = build_l0()
    res = run_bass_kernel_spmd(nc, l0_in_maps(inputs, xT_list),
                               core_ids=list(range(NCORES)))
    return [np.asarray(res.results[c]["yT"]) for c in range(NCORES)]


def build_fused(nlayers=4, trunc=None):
    nc = bass.Bass("TRN2", target_bir_lowering=False)
    with ExitStack() as st:
        C = Ctx(nc, st)
        C.prefix = "l0_"
        emit_l0(C, first=True, last=(nlayers == 1))
        if nlayers >= 2:
            C.prefix = "l1_"
            emit_l1(C, first=False, last=(nlayers == 2))
        if nlayers >= 3:
            C.prefix = "l2_"
            emit_l2(C, first=False, last=(nlayers == 3))
        if nlayers >= 4:
            C.prefix = "l3_"
            emit_l3(C, first=False, last=True)
        print(f"fused nlayers={nlayers}: entries={len(C.s.ops)} "
              f"total_inc={sum(i for _, i, _ in C.s.ops)} l3marks={L3MARKS}")
        if trunc is not None:
            C.s.ops = C.s.ops[:trunc]
        block = st.enter_context(nc.Block())
        C.finish(block)
    return nc


def fused_in_maps(inputs, xT_list):
    per_layer = [l0_in_maps(inputs, xT_list), l1_in_maps(inputs, xT_list),
                 l2_in_maps(inputs, xT_list), l3_in_maps(inputs, xT_list)]
    shared_keys = set(_CONSTS) | {"xT"}
    maps = []
    for c in range(NCORES):
        m = dict(_CONSTS)
        m["xT"] = xT_list[c]
        for li, pl in enumerate(per_layer):
            for k, v in pl[c].items():
                if k in shared_keys:
                    continue
                m[f"l{li}_{k}"] = v
        maps.append(m)
    return maps


def kernel(**inputs):
    x = np.asarray(inputs["L0_attn_x"])
    xT_list = [_f(x[c].T) for c in range(NCORES)]
    nc = build_fused()
    res = run_bass_kernel_spmd(nc, fused_in_maps(inputs, xT_list),
                               core_ids=list(range(NCORES)))
    out = np.stack([np.asarray(res.results[c]["yT"], np.float32).T
                    for c in range(NCORES)])
    return out


def kernel_unfused(**inputs):
    x = np.asarray(inputs["L0_attn_x"])
    xT_list = [_f(x[c].T) for c in range(NCORES)]
    xT_list = run_l0(inputs, xT_list)
    xT_list = run_l1(inputs, xT_list)
    xT_list = run_l2(inputs, xT_list)
    xT_list = run_l3(inputs, xT_list)
    out = np.stack([np.asarray(y, np.float32).T for y in xT_list])
    return out


W6SL = D // NCORES
H1 = 16
DH1 = 64
HG = 2
NHG = H1 // HG


def emit_l1(C, first=False, last=False):
    nc, s, ps, aF, aB = C.nc, C.s, C.ps, C.aF, C.aB
    X, idn_f, idn_b, ones_c, ones_r = C.X, C.idn_f, C.idn_b, C.ones_c, C.ones_r
    mT, rT, nmax, ssum, rsum, tcomb = C.mT, C.rT, C.nmax, C.ssum, C.rsum, C.tcomb
    xT, yT, g_d = C.xT, C.yT, C.g_d
    sb = C.sb

    wqkvT = C.din("wqkvT", [24, DCH, 128, 128])
    bqkv_in = C.din("bqkv", [128, 24])
    w1wT = C.din("w1wT", [D, 64])
    w2wT = C.din("w2wT", [D, 64])
    w1rT = C.din("w1rT", [D, 64])
    w2rT = C.din("w2rT", [D, 64])
    bs_in = C.din("bsrow", [1, 16])
    woT = C.din("woT", [D, D])
    bo_in = C.din("bo", [128, DCH])
    w6_sl = C.din("w6_sl", [W6SL, D])
    b6_in = C.din("b6", [128, DCH])
    triu_in = C.din("triu01", [128, 128])
    cmask_in = C.din("cmask", [128, 128])
    w6T_sl_d = C.dint("w6T_sl_d", [D, W6SL], BF16)
    w6T_d = C.dint("w6T_d", [NCORES * D, W6SL], BF16)

    triu = aF.alloc(128)
    cmask = aF.alloc(128)
    bqkv = aF.alloc(24)
    bo = aF.alloc(DCH); b6 = aF.alloc(DCH)
    bs_full = aF.alloc(16); sgn_full = aF.alloc(16)
    bs_row = aF.alloc(16)[0:1, :]

    arow = aF.alloc(T)[0:1, :]
    brow = aF.alloc(T)[0:1, :]
    zoneF = aF.off

    def resetF():
        aF.off = zoneF

    if first:
        s.dma(lambda e: e.dma_start(
            out=X.ap().rearrange("p (c t) -> p c t", t=T),
            in_=xT.rearrange("(c p) t -> p c t", p=128)))
    s.dma(lambda e: e.dma_start(out=triu[:, :], in_=triu_in))
    s.dma(lambda e: e.dma_start(out=cmask[:, :], in_=cmask_in))
    s.dma(lambda e: e.dma_start(out=bqkv[:, 0:24], in_=bqkv_in))
    s.dma(lambda e: e.dma_start(out=bo[:, 0:DCH], in_=bo_in))
    s.dma(lambda e: e.dma_start(out=b6[:, 0:DCH], in_=b6_in))

    q = quant_scratch(aF, 1024)
    qin = aF.alloc(D)
    qout = aB.alloc(D)
    ev16 = aB.alloc(128)
    s.dma(lambda e: e.dma_start(out=qin[0:128, 0:D], in_=w6_sl))
    emit_quant(s, q, 128, D, 32, 7.5, 1e-30, 0.125, qin[0:128, 0:D], qout)
    for c in range(DCH):
        s.add("tensor", lambda e, c=c: e.matmul(
            ps[2][0:128, 0:128], qout[0:128, c * 128:(c + 1) * 128],
            idn_b[:, :], start=True, stop=True))
        s.add("vector", lambda e: e.tensor_copy(ev16[0:128, 0:128], ps[2][0:128, 0:128]))
        s.dma(lambda e, c=c: e.dma_start(
            out=w6T_sl_d.ap()[c * 128:(c + 1) * 128, :], in_=ev16[0:128, 0:128]))
    s.add("gpsimd", lambda e: e.collective_compute(
        "AllGather", ALU.bypass, replica_groups=[list(range(NCORES))],
        ins=[w6T_sl_d.ap().opt()], outs=[w6T_d.ap().opt()]))
    resetF(); aB.reset()

    emit_ln(s, nc, aF, X, ones_c, ones_r, arow, brow, mT, rT, ps[0], ps[1], ps[2])
    resetF(); aB.reset()

    s.dma(lambda e: e.dma_start(out=bs_row[0:1, 0:16], in_=bs_in))
    s.add("tensor", lambda e: e.matmul(
        ps[2][0:128, 0:16], ones_r[:, :], bs_row[0:1, 0:16], start=True, stop=True))
    s.add("vector", lambda e: e.tensor_copy(bs_full[:, 0:16], ps[2][0:128, 0:16]))
    s.add("scalar", lambda e: e.activation(
        out=sgn_full[:, 0:16], in_=bs_full[:, 0:16], func=AF.Sign))

    q = quant_scratch(aF, 1024)
    xtok = aF.alloc(D)
    gsl = aF.alloc(512)
    w6_b = aB.alloc(DCH * D)
    xq_tok = aB.alloc(D)
    xqT = aB.alloc(DCH * GT)
    for kc in range(DCH):
        s.dma(lambda e, kc=kc: e.dma_start(
            out=w6_b[:, kc * D:(kc + 1) * D].rearrange("p (c2 nl) -> p c2 nl", nl=W6SL),
            in_=bass.AP(w6T_d, kc * 128 * W6SL,
                        [[W6SL, 128], [D * W6SL, NCORES], [1, W6SL]])))
    for g in range(NGRP):
        for tl in range(GT // 128):
            tt = g * (GT // 128) + tl
            for c in range(DCH):
                s.add("tensor", lambda e, c=c, tt=tt: e.matmul(
                    ps[2][0:128, 0:128], X[:, c * T + tt * 128:c * T + (tt + 1) * 128],
                    idn_f[:, :], start=True, stop=True))
                s.add("vector", lambda e, c=c, tt=tt: e.tensor_scalar(
                    out=xtok[:, c * 128:(c + 1) * 128], in0=ps[2][0:128, 0:128],
                    scalar1=mT[:, tt:tt + 1], scalar2=rT[:, tt:tt + 1],
                    op0=ALU.subtract, op1=ALU.mult))
            emit_quant(s, q, 128, D, 32, 7.5, 1e-30, 0.125, xtok[0:128, 0:D], xq_tok)
            for c in range(DCH):
                s.add("tensor", lambda e, c=c: e.matmul(
                    ps[2][0:128, 0:128], xq_tok[0:128, c * 128:(c + 1) * 128],
                    idn_b[:, :], start=True, stop=True))
                s.add("vector", lambda e, c=c, tl=tl: e.tensor_copy(
                    xqT[:, c * GT + tl * 128: c * GT + (tl + 1) * 128],
                    ps[2][0:128, 0:128]))
        for nc2 in range(DCH):
            def mm6(e, nc2=nc2):
                for kc in range(DCH):
                    last = e.matmul(
                        ps[0][0:128, 0:GT],
                        w6_b[:, kc * D + nc2 * W6SL: kc * D + (nc2 + 1) * W6SL],
                        xqT[:, kc * GT:(kc + 1) * GT],
                        start=(kc == 0), stop=(kc == DCH - 1))
                return last
            s.add("tensor", mm6)
            s.add("scalar", lambda e, nc2=nc2: e.activation(
                out=gsl[0:128, 0:GT], in_=ps[0][0:128, 0:GT], func=AF.Silu,
                bias=b6[:, nc2:nc2 + 1]))
            s.dma(lambda e, nc2=nc2, g=g: e.dma_start(
                out=bass.AP(g_d, nc2 * 128 * T + g * GT, [[T, 128], [1, GT]]),
                in_=gsl[0:128, 0:GT]))
    resetF(); aB.reset()

    xnf = aB.alloc(DCH * T)
    jw_tok = aB.alloc(NTILE * 96)
    rd_tok = aB.alloc(NTILE * 96)
    attn_oT = aB.alloc(DCH * T)
    wstr = aB.alloc(2048)

    for ch in range(NTILE):
        s.add("tensor", lambda e, ch=ch: e.matmul(
            ps[0][0:128, 0:128], ones_r[:, :], arow[0:1, ch * 128:(ch + 1) * 128],
            start=True, stop=True))
        s.add("tensor", lambda e, ch=ch: e.matmul(
            ps[1][0:128, 0:128], ones_r[:, :], brow[0:1, ch * 128:(ch + 1) * 128],
            start=True, stop=True))
        for c in range(DCH):
            s.add("vector", lambda e, c=c, ch=ch: e.tensor_tensor(
                tcomb[:, :], X[:, c * T + ch * 128:c * T + (ch + 1) * 128],
                ps[0][0:128, 0:128], op=ALU.mult))
            s.add("vector", lambda e, c=c, ch=ch: e.tensor_tensor(
                xnf[:, c * T + ch * 128:c * T + (ch + 1) * 128],
                tcomb[:, :], ps[1][0:128, 0:128], op=ALU.add))

    w1T = aF.alloc(2112)
    w2tok = aF.alloc(64); r1tok = aF.alloc(64); r2tok = aF.alloc(64)
    w1tok = aF.alloc(64)
    lines = aF.alloc(96)
    sq = aF.alloc(96)
    nn = aF.alloc(16); rsn = aF.alloc(16)
    s.add("vector", lambda e: e.memset(w1T[0:64, 60:64], 0.0))
    s.gdma(lambda e: e.dma_start(
        out=wstr[:, 0:512].rearrange("p (c n) -> p c n", n=64),
        in_=w1wT.rearrange("(c p) n -> p c n", p=128)))
    for tc in range(4):
        def mmw1(e, tc=tc):
            for c in range(DCH):
                last = e.matmul(
                    ps[0][0:64, 0:512], wstr[:, c * 64:(c + 1) * 64],
                    xnf[:, c * T + tc * 512:c * T + (tc + 1) * 512],
                    start=(c == 0), stop=(c == DCH - 1))
            return last
        s.add("tensor", mmw1)
        s.add("vector", lambda e, tc=tc: e.tensor_copy(
            w1T[0:64, 64 + tc * 512: 64 + (tc + 1) * 512], ps[0][0:64, 0:512]))
    for wi, wsrc in enumerate((w2wT, w1rT, w2rT)):
        s.gdma(lambda e, wi=wi, wsrc=wsrc: e.dma_start(
            out=wstr[:, 512 + wi * 512:512 + (wi + 1) * 512].rearrange(
                "p (c n) -> p c n", n=64),
            in_=wsrc.rearrange("(c p) n -> p c n", p=128)))

    PAIRS = [(0, 1), (0, 2), (0, 3), (1, 2), (1, 3), (2, 3)]
    JSG = [1.0, -1.0, 1.0, 1.0, -1.0, 1.0]
    for ch in range(NTILE):
        for wi, dstt in enumerate((w2tok, r1tok, r2tok)):
            def mmrs(e, wi=wi, ch=ch):
                for c in range(DCH):
                    last = e.matmul(
                        ps[0][0:128, 0:64],
                        xnf[:, c * T + ch * 128:c * T + (ch + 1) * 128],
                        wstr[:, 512 + wi * 512 + c * 64:512 + wi * 512 + (c + 1) * 64],
                        start=(c == 0), stop=(c == DCH - 1))
                return last
            s.add("tensor", mmrs)
            s.add("vector", lambda e, dstt=dstt: e.tensor_copy(
                dstt[0:128, 0:64], ps[0][0:128, 0:64]))
        s.add("tensor", lambda e, ch=ch: e.matmul(
            ps[0][0:128, 0:64], w1T[0:64, 63 + ch * 128: 63 + (ch + 1) * 128],
            idn_f[0:64, 0:64], start=True, stop=True))
        s.add("vector", lambda e: e.tensor_copy(w1tok[0:128, 0:64], ps[0][0:128, 0:64]))
        for (p1, p2, dst_tok, permute) in (
                (w1tok, w2tok, jw_tok, True), (r1tok, r2tok, rd_tok, False)):
            for cix, (i, jx) in enumerate(PAIRS):
                va = p1.rearrange("p (h f) -> p h f", f=4)
                vb = p2.rearrange("p (h f) -> p h f", f=4)
                s.add("vector", lambda e, cix=cix, i=i, jx=jx, va=va, vb=vb: e.tensor_tensor(
                    lines[:, cix * 16:(cix + 1) * 16],
                    va[:, 0:16, i:i + 1].squeeze(2), vb[:, 0:16, jx:jx + 1].squeeze(2),
                    op=ALU.mult))
                s.add("vector", lambda e, cix=cix, i=i, jx=jx, va=va, vb=vb: e.tensor_tensor(
                    sq[:, cix * 16:(cix + 1) * 16],
                    va[:, 0:16, jx:jx + 1].squeeze(2), vb[:, 0:16, i:i + 1].squeeze(2),
                    op=ALU.mult))
                s.add("vector", lambda e, cix=cix: e.tensor_tensor(
                    lines[:, cix * 16:(cix + 1) * 16],
                    lines[:, cix * 16:(cix + 1) * 16],
                    sq[:, cix * 16:(cix + 1) * 16], op=ALU.subtract))
            s.add("vector", lambda e: e.tensor_tensor(
                sq[:, 0:96], lines[:, 0:96], lines[:, 0:96], op=ALU.mult))
            s.add("vector", lambda e: e.tensor_reduce(
                nn[:, 0:16], sq[:, 0:96].rearrange("p (c h) -> p h c", h=16),
                axis=AX.X, op=ALU.add))
            s.add("vector", lambda e: e.tensor_scalar(
                out=nn[:, 0:16], in0=nn[:, 0:16], scalar1=1e-24, scalar2=None,
                op0=ALU.max))
            s.add("scalar", lambda e: e.activation(
                out=rsn[:, 0:16], in_=nn[:, 0:16], func=AF.Ln))
            s.add("scalar", lambda e: e.activation(
                out=rsn[:, 0:16], in_=rsn[:, 0:16], func=AF.Exp, scale=-0.5))
            if permute:
                for jx in range(6):
                    s.add("vector", lambda e, jx=jx, ch=ch, dst_tok=dst_tok: e.scalar_tensor_tensor(
                        out=dst_tok[:, ch * 96 + jx * 16: ch * 96 + (jx + 1) * 16],
                        in0=lines[:, (5 - jx) * 16:(5 - jx + 1) * 16],
                        scalar=JSG[jx], in1=rsn[:, 0:16],
                        op0=ALU.mult, op1=ALU.mult))
            else:
                s.add("vector", lambda e, ch=ch, dst_tok=dst_tok: e.tensor_tensor(
                    dst_tok[:, ch * 96:(ch + 1) * 96].rearrange(
                        "p (c h) -> p c h", h=16),
                    lines[:, 0:96].rearrange("p (c h) -> p c h", h=16),
                    rsn[:, 0:16].unsqueeze(1).broadcast_to([128, 6, 16]),
                    op=ALU.mult))
    resetF()

    markB = aB.off
    QTh = aB.alloc(T)
    KTh = aB.alloc(T)
    Vh = aB.alloc(NTILE * HG * DH1)
    jwTh = aB.alloc(T)
    probs = aB.alloc(T)
    probsT1 = aB.alloc(128)
    rdt1 = aB.alloc(128)
    amask = probsT1
    ctb = aB.alloc(128)
    rdmp = aB.alloc(128)
    sprev = probsT1
    L_sb = aF.alloc(T)
    babs = aF.alloc(512)
    c_sb = aF.alloc(8)

    psS = ps[7]
    for hg in range(NHG):
        for which, dstT in ((0, QTh), (1, KTh)):
            base_row = which * D + hg * 128
            bcol = base_row // 128
            s.gdma(lambda e, bcol=bcol: e.dma_start(
                out=wstr[:, 0:1024].rearrange("p (c n) -> p c n", n=128),
                in_=bass.AP(wqkvT.tensor, bcol * DCH * 128 * 128,
                            [[128, 128], [128 * 128, DCH], [1, 128]])))
            for tc in range(4):
                def mmqk(e, tc=tc):
                    for c in range(DCH):
                        last = e.matmul(
                            ps[0][0:128, 0:512], wstr[:, c * 128:(c + 1) * 128],
                            xnf[:, c * T + tc * 512:c * T + (tc + 1) * 512],
                            start=(c == 0), stop=(c == DCH - 1))
                    return last
                s.add("tensor", mmqk)
                if which == 0:
                    s.add("vector", lambda e, tc=tc, bcol=bcol, dstT=dstT: e.tensor_scalar(
                        out=dstT[:, tc * 512:(tc + 1) * 512], in0=ps[0][0:128, 0:512],
                        scalar1=bqkv[:, bcol:bcol + 1], scalar2=0.125,
                        op0=ALU.add, op1=ALU.mult))
                else:
                    s.add("vector", lambda e, tc=tc, bcol=bcol, dstT=dstT: e.tensor_scalar(
                        out=dstT[:, tc * 512:(tc + 1) * 512], in0=ps[0][0:128, 0:512],
                        scalar1=bqkv[:, bcol:bcol + 1], scalar2=None, op0=ALU.add))
        vcol0 = (2 * D + hg * 128) // 128
        s.gdma(lambda e, vcol0=vcol0: e.dma_start(
            out=wstr[:, 0:1024].rearrange("p (c n) -> p c n", n=128),
            in_=bass.AP(wqkvT.tensor, vcol0 * DCH * 128 * 128,
                        [[128, 128], [128 * 128, DCH], [1, 128]])))
        for ch in range(NTILE):
            def mmv(e, ch=ch):
                for c in range(DCH):
                    last = e.matmul(
                        ps[0][0:128, 0:128],
                        xnf[:, c * T + ch * 128:c * T + (ch + 1) * 128],
                        wstr[:, c * 128:(c + 1) * 128],
                        start=(c == 0), stop=(c == DCH - 1))
                return last
            s.add("tensor", mmv)
            s.add("vector", lambda e, ch=ch: e.tensor_copy(
                Vh[:, ch * 128:(ch + 1) * 128], ps[0][0:128, 0:128]))

        for hl in range(HG):
            h = hg * HG + hl

            def head_slice(src_tok, tile):
                return src_tok[:, tile * 96:(tile + 1) * 96].rearrange(
                    "p (c h2) -> p c h2", h2=16)[:, :, h:h + 1].squeeze(2)
            for tile in range(NTILE):
                ap_str = head_slice(jw_tok, tile)
                s.add("tensor", lambda e, ap_str=ap_str: e.matmul(
                    ps[2][0:6, 0:128], ap_str, idn_b[:, :], start=True, stop=True))
                s.add("vector", lambda e, tile=tile: e.tensor_copy(
                    jwTh[0:6, tile * 128:(tile + 1) * 128], ps[2][0:6, 0:128]))
            for qb in range(NTILE):
                W = (qb + 1) * 128
                jw_str = head_slice(jw_tok, qb)
                s.add("tensor", lambda e, ap_str=head_slice(rd_tok, qb): e.matmul(
                    ps[2][0:6, 0:128], ap_str, idn_b[:, :], start=True, stop=True))
                s.add("vector", lambda e: e.tensor_copy(
                    rdt1[0:6, 0:128], ps[2][0:6, 0:128]))
                if qb > 0:
                    s.add("vector", lambda e, h=h: e.tensor_copy(
                        sprev[0:6, 0:6], psS[0:6, h * 8:h * 8 + 6]))
                    s.add("tensor", lambda e: e.matmul(
                        ps[2][0:6, 0:128], sprev[0:6, 0:6],
                        rdt1[0:6, 0:128], start=True, stop=True))
                    s.add("vector", lambda e: e.tensor_copy(
                        rdmp[0:6, 0:128], ps[2][0:6, 0:128]))
                s.add("tensor", lambda e, jw_str=jw_str, qb=qb, h=h: e.matmul(
                    psS[0:6, h * 8:h * 8 + 6], jw_str, jw_str,
                    start=(qb == 0), stop=(qb == NTILE - 1)))
                s.add("tensor", lambda e, qb=qb: e.matmul(
                    ps[4][0:128, 0:128], jwTh[0:6, qb * 128:(qb + 1) * 128],
                    rdt1[0:6, 0:128], start=True, stop=True))
                s.add("vector", lambda e: e.tensor_tensor(
                    amask[0:128, 0:128], ps[4][0:128, 0:128], triu[:, :], op=ALU.mult))
                s.add("tensor", lambda e, jw_str=jw_str: e.matmul(
                    ps[2][0:128, 0:6], amask[0:128, 0:128], jw_str,
                    start=True, stop=True))
                s.add("vector", lambda e: e.tensor_copy(c_sb[0:128, 0:6], ps[2][0:128, 0:6]))
                s.add("tensor", lambda e: e.matmul(
                    ps[2][0:6, 0:128], c_sb[0:128, 0:6], idn_f[:, :],
                    start=True, stop=True))
                s.add("vector", lambda e: e.tensor_copy(ctb[0:6, 0:128], ps[2][0:6, 0:128]))
                nsc = (W + 511) // 512
                for sc in range(nsc):
                    s0 = sc * 512
                    wdt = min(512, W - s0)

                    def mmstd(e, qb=qb, hl=hl, s0=s0, wdt=wdt):
                        return e.matmul(
                            ps[0][0:128, 0:wdt],
                            QTh[hl * 64:(hl + 1) * 64, qb * 128:(qb + 1) * 128],
                            KTh[hl * 64:(hl + 1) * 64, s0:s0 + wdt],
                            start=True, stop=True)
                    s.add("tensor", mmstd)
                    s.add("tensor", lambda e, qb=qb, s0=s0, wdt=wdt: e.matmul(
                        ps[1][0:128, 0:wdt], ctb[0:6, 0:128],
                        jwTh[0:6, s0:s0 + wdt], start=True, stop=(qb == 0)))
                    if qb > 0:
                        s.add("tensor", lambda e, s0=s0, wdt=wdt: e.matmul(
                            ps[1][0:128, 0:wdt], rdmp[0:6, 0:128],
                            jwTh[0:6, s0:s0 + wdt], start=False, stop=True))
                    s.add("scalar", lambda e, h=h, wdt=wdt: e.activation(
                        out=babs[0:128, 0:wdt], in_=ps[1][0:128, 0:wdt],
                        func=AF.Abs, scale=bs_full[:, h:h + 1]))
                    s.add("vector", lambda e, h=h, s0=s0, wdt=wdt: e.scalar_tensor_tensor(
                        out=L_sb[0:128, s0:s0 + wdt], in0=babs[0:128, 0:wdt],
                        scalar=sgn_full[:, h:h + 1], in1=ps[0][0:128, 0:wdt],
                        op0=ALU.mult, op1=ALU.add))
                s.add("vector", lambda e, qb=qb: e.tensor_tensor(
                    L_sb[0:128, qb * 128:(qb + 1) * 128],
                    L_sb[0:128, qb * 128:(qb + 1) * 128],
                    cmask[:, :], op=ALU.add))
                s.add("vector", lambda e, W=W: e.reduce_max(
                    nmax[:, :], L_sb[0:128, 0:W], axis=AX.X, negate=True))
                s.add("scalar", lambda e, W=W: e.activation(
                    out=probs[0:128, 0:W], in_=L_sb[0:128, 0:W], func=AF.Exp,
                    bias=nmax[:, :], accum_out=ssum[:, :]))
                s.add("vector", lambda e: e.reciprocal(rsum[:, :], ssum[:, :]))
                s.add("vector", lambda e, W=W: e.tensor_scalar(
                    out=probs[0:128, 0:W], in0=probs[0:128, 0:W],
                    scalar1=rsum[:, :], scalar2=None, op0=ALU.mult))
                for kb in range(qb + 1):
                    s.add("tensor", lambda e, kb=kb: e.matmul(
                        ps[2][0:128, 0:128], probs[:, kb * 128:(kb + 1) * 128],
                        idn_b[:, :], start=True, stop=True))
                    s.add("vector", lambda e: e.tensor_copy(
                        probsT1[:, 0:128], ps[2][0:128, 0:128]))
                    s.add("tensor", lambda e, kb=kb, qb=qb, hl=hl: e.matmul(
                        ps[3][0:64, 0:128],
                        Vh[:, kb * 128 + hl * 64: kb * 128 + hl * 64 + 64],
                        probsT1[:, 0:128],
                        start=(kb == 0), stop=(kb == qb)))
                vcol = 16 + h // 2
                vpart = (h % 2) * 64
                oc = h // 2
                s.add("vector", lambda e, qb=qb, vcol=vcol, vpart=vpart, oc=oc: e.tensor_scalar(
                    out=attn_oT[vpart:vpart + 64, oc * T + qb * 128: oc * T + (qb + 1) * 128],
                    in0=ps[3][0:64, 0:128], scalar1=bqkv[vpart:vpart + 64, vcol:vcol + 1],
                    scalar2=None, op0=ALU.add))

    aB.off = markB
    wo_b = aB.alloc(DCH * D)
    gslb = aF.alloc(DCH * 128)
    s.gdma(lambda e: e.dma_start(
        out=wo_b.rearrange("p (c n) -> p c n", n=D),
        in_=woT.rearrange("(c p) n -> p c n", p=128)))
    for qb in range(NTILE):
        s.dma(lambda e, qb=qb: e.dma_start(
            out=gslb.rearrange("p (c t) -> p c t", t=128),
            in_=bass.AP(g_d, qb * 128, [[T, 128], [128 * T, DCH], [1, 128]])))
        for d2 in range(DCH):
            def mmwo(e, d2=d2, qb=qb):
                for ic in range(DCH):
                    last = e.matmul(
                        ps[0][0:128, 0:128],
                        wo_b[:, ic * D + d2 * 128: ic * D + (d2 + 1) * 128],
                        attn_oT[:, ic * T + qb * 128: ic * T + (qb + 1) * 128],
                        start=(ic == 0), stop=(ic == DCH - 1))
                return last
            s.add("tensor", mmwo)
            s.add("vector", lambda e, d2=d2: e.tensor_scalar(
                out=tcomb[:, :], in0=ps[0][0:128, 0:128],
                scalar1=bo[:, d2:d2 + 1], scalar2=None, op0=ALU.add))
            s.add("vector", lambda e, d2=d2: e.tensor_tensor(
                tcomb[:, :], tcomb[:, :], gslb[:, d2 * 128:(d2 + 1) * 128],
                op=ALU.mult))
            s.add("vector", lambda e, d2=d2, qb=qb: e.tensor_tensor(
                X[:, d2 * T + qb * 128:d2 * T + (qb + 1) * 128],
                X[:, d2 * T + qb * 128:d2 * T + (qb + 1) * 128],
                tcomb[:, :], op=ALU.add))

    if last:
        s.dma(lambda e: e.dma_start(
            out=yT.rearrange("(c p) t -> p c t", p=128),
            in_=X.ap().rearrange("p (c t) -> p c t", t=T)))
    aF.reset(); aB.reset()


def build_l1():
    nc = bass.Bass("TRN2", target_bir_lowering=False)
    with ExitStack() as st:
        C = Ctx(nc, st)
        emit_l1(C, first=True, last=True)
        block = st.enter_context(nc.Block())
        C.finish(block)
    return nc


def l1_in_maps(inputs, xT_list):
    wqkvT = _f(np.asarray(inputs["L1_attn_W_qkv"]).T
               .reshape(DCH, 128, 24, 128).transpose(2, 0, 1, 3))
    bqkv = _f(np.asarray(inputs["L1_attn_b_qkv"]).reshape(24, 128).T)
    w1wT = _f(np.asarray(inputs["L1_attn_W1_write"]).T)
    w2wT = _f(np.asarray(inputs["L1_attn_W2_write"]).T)
    w1rT = _f(np.asarray(inputs["L1_attn_W1_read"]).T)
    w2rT = _f(np.asarray(inputs["L1_attn_W2_read"]).T)
    bsrow = _f(np.asarray(inputs["L1_attn_bias_scale"]).reshape(1, 16))
    woT = _f(np.asarray(inputs["L1_attn_W_out"]).T)
    bo = _bias128(inputs["L1_attn_b_out"])
    w6 = _f(inputs["L1_ffn_weight"])
    b6 = _bias128(inputs["L1_ffn_bias"])
    tri = np.zeros((128, 128), np.float32)
    tri[np.triu_indices(128, 0)] = 1.0
    cm = np.where(np.arange(128)[None, :] > np.arange(128)[:, None],
                  np.float32(-1e38), np.float32(0.0)).astype(np.float32)
    maps = []
    for c in range(NCORES):
        m = dict(_CONSTS)
        m.update({
            "xT": xT_list[c], "wqkvT": wqkvT, "bqkv": bqkv,
            "w1wT": w1wT, "w2wT": w2wT, "w1rT": w1rT, "w2rT": w2rT,
            "bsrow": bsrow, "woT": woT, "bo": bo,
            "w6_sl": _f(w6[c * W6SL:(c + 1) * W6SL]), "b6": b6,
            "triu01": tri, "cmask": cm,
        })
        maps.append(m)
    return maps


def run_l1(inputs, xT_list):
    nc = build_l1()
    res = run_bass_kernel_spmd(nc, l1_in_maps(inputs, xT_list),
                               core_ids=list(range(NCORES)))
    return [np.asarray(res.results[c]["yT"]) for c in range(NCORES)]


NE = 8
CAP = 2048


def emit_l2(C, first=False, last=False):
    nc, s, ps, aF, aB = C.nc, C.s, C.ps, C.aF, C.aB
    X, idn_f, idn_b, ones_c, ones_r = C.X, C.idn_f, C.idn_b, C.ones_c, C.ones_r
    mT, rT, nmax, ssum, rsum, tcomb = C.mT, C.rT, C.nmax, C.ssum, C.rsum, C.tcomb
    xT, yT, g_d = C.xT, C.yT, C.g_d
    sb = C.sb

    wq_in = C.din("wq", [DCH, DCH, 128, 128])
    wk_in = C.din("wk", [2, DCH, 128, 128])
    wv_in = C.din("wv", [D, 256])
    wo_in = C.din("wo", [D, D])
    wr_in = C.din("wr", [D, NE])
    br_in = C.din("br", [NE, 1])
    w1_in = C.din("w1e", [NE, D, D])
    w2_in = C.din("w2e", [NE, D, D])
    b1_in = C.din("b1e", [128, NE * DCH])
    b2_in = C.din("b2e", [NE, D])
    pref_in = C.din("prefmask", [NE, 1])
    cnt_sl_d = C.dint("cnt_sl_d", [1, NE], F32)
    cnt_d = C.dint("cnt_d", [NCORES, NE], F32)
    gacc_d = C.dint("gacc_d", [D, T], F32)

    ones_b = C.ones_b
    ones_rb = C.ones_rb
    b1s = aF.alloc(NE * DCH)
    brs = aF.alloc(1)[0:NE, :]
    prefm = aF.alloc(1)[0:NE, :]
    b2s = aB.alloc(D)[0:NE, :]

    arow = aF.alloc(T)[0:1, :]
    brow = aF.alloc(T)[0:1, :]
    zoneF = aF.off

    def resetF():
        aF.off = zoneF

    if first:
        s.dma(lambda e: e.dma_start(
            out=X.ap().rearrange("p (c t) -> p c t", t=T),
            in_=xT.rearrange("(c p) t -> p c t", p=128)))
    s.dma(lambda e: e.dma_start(out=b1s[:, 0:NE * DCH], in_=b1_in))
    s.gdma(lambda e: e.dma_start(out=b2s[:, :], in_=b2_in))
    s.dma(lambda e: e.dma_start(out=brs[0:NE, 0:1], in_=br_in))
    s.dma(lambda e: e.dma_start(out=prefm[0:NE, 0:1], in_=pref_in))

    emit_ln(s, nc, aF, X, ones_c, ones_r, arow, brow, mT, rT, ps[0], ps[1], ps[2])
    resetF()

    def emit_xnf(xnf):
        for ch in range(NTILE):
            s.add("tensor", lambda e, ch=ch: e.matmul(
                ps[0][0:128, 0:128], ones_r[:, :], arow[0:1, ch * 128:(ch + 1) * 128],
                start=True, stop=True))
            s.add("tensor", lambda e, ch=ch: e.matmul(
                ps[1][0:128, 0:128], ones_r[:, :], brow[0:1, ch * 128:(ch + 1) * 128],
                start=True, stop=True))
            for c in range(DCH):
                s.add("vector", lambda e, c=c, ch=ch: e.tensor_tensor(
                    tcomb[:, :], X[:, c * T + ch * 128:c * T + (ch + 1) * 128],
                    ps[0][0:128, 0:128], op=ALU.mult))
                s.add("vector", lambda e, c=c, ch=ch, xnf=xnf: e.tensor_tensor(
                    xnf[:, c * T + ch * 128:c * T + (ch + 1) * 128],
                    tcomb[:, :], ps[1][0:128, 0:128], op=ALU.add))

    xnf = aB.alloc(DCH * T)
    wstr = aB.alloc(DCH * NE)
    emit_xnf(xnf)
    logit_sb = aF.alloc(T)
    mask_T = aF.alloc(T)
    pos_T = aF.alloc(T)
    gate_row = logit_sb[0:1, :]
    ltok = aF.alloc(8)
    maskt = aF.alloc(8)
    junk8 = aF.alloc(8)
    keept = aF.alloc(512)
    cnt8 = aF.alloc(8)
    off8 = aF.alloc(8)
    sel_b = aB.alloc(T)
    selrow = aB.alloc(T)
    s.gdma(lambda e: e.dma_start(
        out=wstr[:, 0:NE * DCH].rearrange("p (c n) -> p c n", n=NE),
        in_=wr_in.rearrange("(c p) n -> p c n", p=128)))
    for tc2 in range(4):
        def mmr(e, tc2=tc2):
            for c in range(DCH):
                last = e.matmul(
                    ps[0][0:NE, 0:512], wstr[:, c * NE:(c + 1) * NE],
                    xnf[:, c * T + tc2 * 512:c * T + (tc2 + 1) * 512],
                    start=(c == 0), stop=(c == DCH - 1))
            return last
        s.add("tensor", mmr)
        s.add("vector", lambda e, tc2=tc2: e.tensor_scalar(
            out=logit_sb[0:NE, tc2 * 512:(tc2 + 1) * 512], in0=ps[0][0:NE, 0:512],
            scalar1=brs[0:NE, 0:1], scalar2=None, op0=ALU.add))
    for tl in range(NTILE):
        s.add("tensor", lambda e, tl=tl: e.matmul(
            ps[2][0:128, 0:NE], logit_sb[0:NE, tl * 128:(tl + 1) * 128],
            idn_f[0:NE, 0:NE], start=True, stop=True))
        s.add("vector", lambda e: e.tensor_copy(ltok[0:128, 0:NE], ps[2][0:128, 0:NE]))
        s.add("vector", lambda e: e.reduce_max(
            nmax[:, :], ltok[0:128, 0:NE], axis=AX.X, negate=True))
        s.add("vector", lambda e: e.tensor_scalar(
            out=maskt[0:128, 0:NE], in0=ltok[0:128, 0:NE],
            scalar1=nmax[:, :], scalar2=-1.0, op0=ALU.add, op1=ALU.mult))
        s.add("vector", lambda e: e.tensor_scalar(
            out=maskt[0:128, 0:NE], in0=maskt[0:128, 0:NE],
            scalar1=0.0, scalar2=None, op0=ALU.is_le))
        s.add("scalar", lambda e: e.activation(
            out=junk8[0:128, 0:NE], in_=ltok[0:128, 0:NE], func=AF.Exp,
            bias=nmax[:, :], accum_out=ssum[:, :]))
        s.add("vector", lambda e: e.reciprocal(rsum[:, :], ssum[:, :]))
        s.add("tensor", lambda e: e.matmul(
            ps[2][0:NE, 0:128], maskt[0:128, 0:NE], idn_f[:, :],
            start=True, stop=True))
        s.add("vector", lambda e, tl=tl: e.tensor_copy(
            mask_T[0:NE, tl * 128:(tl + 1) * 128], ps[2][0:NE, 0:128]))
        s.add("tensor", lambda e: e.matmul(
            ps[2][0:1, 0:128], rsum[:, :], idn_f[:, :], start=True, stop=True))
        s.add("vector", lambda e, tl=tl: e.tensor_copy(
            gate_row[0:1, tl * 128:(tl + 1) * 128], ps[2][0:1, 0:128]))
    s.add("vector", lambda e: e.tensor_tensor_scan(
        out=pos_T[0:NE, 0:T],
        data0=ones_c[0:NE, 0:1].broadcast_to([NE, T]),
        data1=mask_T[0:NE, 0:T],
        initial=0.0, op0=ALU.mult, op1=ALU.add))
    s.add("vector", lambda e: e.tensor_copy(cnt8[0:NE, 0:1], pos_T[0:NE, T - 1:T]))
    s.add("tensor", lambda e: e.matmul(
        ps[2][0:1, 0:NE], cnt8[0:NE, 0:1], idn_f[0:NE, 0:NE], start=True, stop=True))
    s.add("vector", lambda e: e.tensor_copy(junk8[0:1, 0:NE], ps[2][0:1, 0:NE]))
    s.gdma(lambda e: e.dma_start(out=cnt_sl_d.ap(), in_=junk8[0:1, 0:NE]))
    s.add("gpsimd", lambda e: e.collective_compute(
        "AllGather", ALU.bypass, replica_groups=[list(range(NCORES))],
        ins=[cnt_sl_d.ap().opt()], outs=[cnt_d.ap().opt()]))
    s.gdma(lambda e: e.dma_start(out=ltok[0:NCORES, 0:NE], in_=cnt_d.ap()))
    s.add("tensor", lambda e: e.matmul(
        ps[2][0:1, 0:NE], prefm[0:NCORES, 0:1], ltok[0:NCORES, 0:NE],
        start=True, stop=True))
    s.add("vector", lambda e: e.tensor_copy(junk8[0:1, 0:NE], ps[2][0:1, 0:NE]))
    s.add("tensor", lambda e: e.matmul(
        ps[2][0:NE, 0:1], junk8[0:1, 0:NE], ones_r[0:1, 0:1], start=True, stop=True))
    s.add("vector", lambda e: e.tensor_copy(off8[0:NE, 0:1], ps[2][0:NE, 0:1]))
    s.add("vector", lambda e: e.tensor_scalar(
        out=pos_T[0:NE, 0:T], in0=pos_T[0:NE, 0:T], scalar1=off8[0:NE, 0:1],
        scalar2=None, op0=ALU.add))
    for tc2 in range(4):
        sl = slice(tc2 * 512, (tc2 + 1) * 512)
        s.add("tensor", lambda e, tc2=tc2: e.matmul(
            ps[1][0:NE, 0:512], ones_r[0:1, 0:NE],
            gate_row[0:1, tc2 * 512:(tc2 + 1) * 512], start=True, stop=True))
        s.add("vector", lambda e, sl=sl: e.tensor_scalar(
            out=keept[0:NE, 0:512], in0=pos_T[0:NE, sl], scalar1=float(CAP),
            scalar2=None, op0=ALU.is_lt))
        s.add("vector", lambda e, sl=sl: e.tensor_tensor(
            keept[0:NE, 0:512], keept[0:NE, 0:512], mask_T[0:NE, sl], op=ALU.mult))
        s.add("vector", lambda e, sl=sl: e.tensor_tensor(
            sel_b[0:NE, sl], keept[0:NE, 0:512], ps[1][0:NE, 0:512], op=ALU.mult))

    for tc2 in range(4):
        s.add("tensor", lambda e, tc2=tc2: e.matmul(
            ps[1][0:128, 0:512], ones_r[:, :],
            gate_row[0:1, tc2 * 512:(tc2 + 1) * 512], start=True, stop=True))
        for c in range(DCH):
            s.add("vector", lambda e, c=c, tc2=tc2: e.tensor_tensor(
                xnf[:, c * T + tc2 * 512:c * T + (tc2 + 1) * 512],
                xnf[:, c * T + tc2 * 512:c * T + (tc2 + 1) * 512],
                ps[1][0:128, 0:512], op=ALU.mult))

    w1b = aB.alloc(DCH * D)
    w2b = aB.alloc(DCH * D)
    hT = aB.alloc(DCH * 512)
    resetF()
    hrelu = aF.alloc(512)
    yda = aF.alloc(512)
    for e_i in range(NE):
        s.gdma(lambda e, e_i=e_i: e.dma_start(
            out=selrow[0:1, 0:T], in_=sel_b[e_i:e_i + 1, 0:T]))
        s.gdma(lambda e, e_i=e_i: e.dma_start(
            out=w1b.rearrange("p (c n) -> p c n", n=D),
            in_=w1_in[e_i].rearrange("(c p) n -> p c n", p=128)))
        s.gdma(lambda e, e_i=e_i: e.dma_start(
            out=w2b.rearrange("p (c n) -> p c n", n=D),
            in_=w2_in[e_i].rearrange("(c p) n -> p c n", p=128)))
        for tc2 in range(4):
            s.add("tensor", lambda e, tc2=tc2: e.matmul(
                ps[1][0:128, 0:512], ones_rb[:, :],
                selrow[0:1, tc2 * 512:(tc2 + 1) * 512],
                start=True, stop=True))
            for fc in range(DCH):
                def mmw1(e, fc=fc, tc2=tc2):
                    for dc in range(DCH):
                        last = e.matmul(
                            ps[0][0:128, 0:512],
                            w1b[:, dc * D + fc * 128: dc * D + (fc + 1) * 128],
                            xnf[:, dc * T + tc2 * 512: dc * T + (tc2 + 1) * 512],
                            start=(dc == 0), stop=(dc == DCH - 1))
                    return last
                s.add("tensor", mmw1)
                s.add("scalar", lambda e, e_i=e_i, fc=fc: e.activation(
                    out=hrelu[0:128, 0:512], in_=ps[0][0:128, 0:512], func=AF.Relu,
                    bias=b1s[:, e_i * DCH + fc: e_i * DCH + fc + 1]))
                s.add("vector", lambda e, fc=fc: e.tensor_tensor(
                    hT[:, fc * 512:(fc + 1) * 512], hrelu[0:128, 0:512],
                    ps[1][0:128, 0:512], op=ALU.mult))
            for d2 in range(DCH):
                def mmw2(e, d2=d2):
                    for fc in range(DCH):
                        last = e.matmul(
                            ps[2][0:128, 0:512],
                            w2b[:, fc * D + d2 * 128: fc * D + (d2 + 1) * 128],
                            hT[:, fc * 512:(fc + 1) * 512],
                            start=(fc == 0), stop=(fc == DCH - 1))
                    return last
                s.add("tensor", mmw2)
                s.add("vector", lambda e: e.tensor_copy(
                    yda[0:128, 0:512], ps[2][0:128, 0:512]))
                if e_i == 0:
                    s.gdma(lambda e, d2=d2, tc2=tc2: e.dma_start(
                        out=bass.AP(gacc_d, d2 * 128 * T + tc2 * 512,
                                    [[T, 128], [1, 512]]),
                        in_=yda[0:128, 0:512]))
                else:
                    s.gdma(lambda e, d2=d2, tc2=tc2: e.dma_start(
                        out=bass.AP(gacc_d, d2 * 128 * T + tc2 * 512,
                                    [[T, 128], [1, 512]]),
                        in_=yda[0:128, 0:512], accum_op=ALU.add))
    for tc2 in range(4):
        for d2 in range(DCH):
            s.add("tensor", lambda e, d2=d2, tc2=tc2: e.matmul(
                ps[0][0:128, 0:512], b2s[0:NE, d2 * 128:(d2 + 1) * 128],
                sel_b[0:NE, tc2 * 512:(tc2 + 1) * 512], start=True, stop=True))
            s.dma(lambda e, d2=d2, tc2=tc2: e.dma_start(
                out=hrelu[0:128, 0:512],
                in_=bass.AP(gacc_d, d2 * 128 * T + tc2 * 512, [[T, 128], [1, 512]])))
            s.add("vector", lambda e: e.tensor_tensor(
                hrelu[0:128, 0:512], hrelu[0:128, 0:512], ps[0][0:128, 0:512],
                op=ALU.add))
            s.add("scalar", lambda e: e.activation(
                out=yda[0:128, 0:512], in_=hrelu[0:128, 0:512], func=AF.Silu))
            s.dma(lambda e, d2=d2, tc2=tc2: e.dma_start(
                out=bass.AP(g_d, d2 * 128 * T + tc2 * 512, [[T, 128], [1, 512]]),
                in_=yda[0:128, 0:512]))
    aB.reset(); resetF()

    QT = aB.alloc(DCH * T)
    KT = aB.alloc(2 * T)
    VT = aB.alloc(NTILE * 256)
    markA = aB.off
    xnf = aB.alloc(DCH * T)
    wstr = aB.alloc(DCH * 256)
    emit_xnf(xnf)
    for ec in range(DCH):
        s.gdma(lambda e, ec=ec: e.dma_start(
            out=wstr[:, 0:DCH * 128].rearrange("p (c n) -> p c n", n=128),
            in_=bass.AP(wq_in.tensor, ec * DCH * 128 * 128,
                        [[128, 128], [128 * 128, DCH], [1, 128]])))
        for tc2 in range(4):
            def mmq(e, tc2=tc2):
                for c in range(DCH):
                    last = e.matmul(
                        ps[0][0:128, 0:512], wstr[:, c * 128:(c + 1) * 128],
                        xnf[:, c * T + tc2 * 512:c * T + (tc2 + 1) * 512],
                        start=(c == 0), stop=(c == DCH - 1))
                return last
            s.add("tensor", mmq)
            s.add("vector", lambda e, ec=ec, tc2=tc2: e.tensor_scalar(
                out=QT[:, ec * T + tc2 * 512: ec * T + (tc2 + 1) * 512],
                in0=ps[0][0:128, 0:512], scalar1=0.125, scalar2=None, op0=ALU.mult))
    for ec in range(2):
        s.gdma(lambda e, ec=ec: e.dma_start(
            out=wstr[:, 0:DCH * 128].rearrange("p (c n) -> p c n", n=128),
            in_=bass.AP(wk_in.tensor, ec * DCH * 128 * 128,
                        [[128, 128], [128 * 128, DCH], [1, 128]])))
        for tc2 in range(4):
            def mmk(e, tc2=tc2):
                for c in range(DCH):
                    last = e.matmul(
                        ps[0][0:128, 0:512], wstr[:, c * 128:(c + 1) * 128],
                        xnf[:, c * T + tc2 * 512:c * T + (tc2 + 1) * 512],
                        start=(c == 0), stop=(c == DCH - 1))
                return last
            s.add("tensor", mmk)
            s.add("vector", lambda e, ec=ec, tc2=tc2: e.tensor_copy(
                KT[:, ec * T + tc2 * 512: ec * T + (tc2 + 1) * 512],
                ps[0][0:128, 0:512]))
    s.gdma(lambda e: e.dma_start(
        out=wstr[:, 0:DCH * 256].rearrange("p (c n) -> p c n", n=256),
        in_=wv_in.rearrange("(c p) n -> p c n", p=128)))
    for ch in range(NTILE):
        def mmv(e, ch=ch):
            for c in range(DCH):
                last = e.matmul(
                    ps[0][0:128, 0:256],
                    xnf[:, c * T + ch * 128:c * T + (ch + 1) * 128],
                    wstr[:, c * 256:(c + 1) * 256],
                    start=(c == 0), stop=(c == DCH - 1))
            return last
        s.add("tensor", mmv)
        s.add("vector", lambda e, ch=ch: e.tensor_copy(
            VT[:, ch * 256:(ch + 1) * 256], ps[0][0:128, 0:256]))

    aB.off = markA
    ksl = aB.alloc(T)
    ET = aB.alloc(NTILE * 512)
    soT = aB.alloc(DCH * 512)
    wob = aB.alloc(DCH * D)
    rdeb = aF.alloc(512)
    den_r = aF.alloc(512)[0:1, :]
    gslb = aF.alloc(DCH * 512)
    att_t = aF.alloc(512)
    s.gdma(lambda e: e.dma_start(
        out=wob.rearrange("p (c n) -> p c n", n=D),
        in_=wo_in.rearrange("(c p) n -> p c n", p=128)))
    for qc in range(4):
        for h in range(16):
            g2 = h // 4
            hb = (h % 2) * 64
            s.add("vector", lambda e, g2=g2, hb=hb: e.tensor_copy(
                ksl[hb:hb + 64, 0:T],
                KT[(g2 % 2) * 64:(g2 % 2) * 64 + 64, (g2 // 2) * T:(g2 // 2) * T + T]))
            for kb in range(NTILE):
                s.add("tensor", lambda e, h=h, qc=qc, kb=kb, hb=hb: e.matmul(
                    ps[0][0:128, 0:512],
                    ksl[hb:hb + 64, kb * 128:(kb + 1) * 128],
                    QT[hb:hb + 64,
                       (h // 2) * T + qc * 512:(h // 2) * T + (qc + 1) * 512],
                    start=True, stop=True))
                s.add("scalar", lambda e, kb=kb: e.activation(
                    out=ET[:, kb * 512:(kb + 1) * 512], in_=ps[0][0:128, 0:512],
                    func=AF.Exp))

            def mmpv(e, h=h):
                g2 = h // 4
                for kb in range(NTILE):
                    e.matmul(ps[2][0:64, 0:512],
                             VT[:, kb * 256 + g2 * 64: kb * 256 + g2 * 64 + 64],
                             ET[:, kb * 512:(kb + 1) * 512],
                             start=(kb == 0), stop=(kb == NTILE - 1))
                for kb in range(NTILE):
                    last = e.matmul(ps[2][64:65, 0:512], ones_b[:, :],
                                    ET[:, kb * 512:(kb + 1) * 512],
                                    start=(kb == 0), stop=(kb == NTILE - 1))
                return last
            s.add("tensor", mmpv)
            s.add("vector", lambda e: e.tensor_copy(den_r[0:1, 0:512], ps[2][64:65, 0:512]))
            s.add("scalar", lambda e: e.activation(
                out=den_r[0:1, 0:512], in_=den_r[0:1, 0:512], func=AF.Ln))
            s.add("scalar", lambda e: e.activation(
                out=den_r[0:1, 0:512], in_=den_r[0:1, 0:512], func=AF.Exp, scale=-1.0))
            s.add("tensor", lambda e: e.matmul(
                ps[1][0:64, 0:512], ones_r[0:1, 0:64], den_r[0:1, 0:512],
                start=True, stop=True))
            s.add("vector", lambda e: e.tensor_copy(rdeb[0:64, 0:512], ps[1][0:64, 0:512]))
            s.add("vector", lambda e, h=h: e.tensor_tensor(
                soT[(h % 2) * 64:(h % 2) * 64 + 64,
                    (h // 2) * 512:(h // 2) * 512 + 512],
                ps[2][0:64, 0:512], rdeb[0:64, 0:512], op=ALU.mult))
        s.dma(lambda e, qc=qc: e.dma_start(
            out=gslb.rearrange("p (c t) -> p c t", t=512),
            in_=bass.AP(g_d, qc * 512, [[T, 128], [128 * T, DCH], [1, 512]])))
        for d2 in range(DCH):
            def mmwo(e, d2=d2):
                for ic in range(DCH):
                    last = e.matmul(
                        ps[0][0:128, 0:512],
                        wob[:, ic * D + d2 * 128: ic * D + (d2 + 1) * 128],
                        soT[:, ic * 512:(ic + 1) * 512],
                        start=(ic == 0), stop=(ic == DCH - 1))
                return last
            s.add("tensor", mmwo)
            s.add("vector", lambda e, d2=d2: e.tensor_tensor(
                att_t[0:128, 0:512],
                ps[0][0:128, 0:512], gslb[:, d2 * 512:(d2 + 1) * 512], op=ALU.mult))
            s.add("vector", lambda e, d2=d2, qc=qc: e.tensor_tensor(
                X[:, d2 * T + qc * 512:d2 * T + (qc + 1) * 512],
                X[:, d2 * T + qc * 512:d2 * T + (qc + 1) * 512],
                att_t[0:128, 0:512], op=ALU.add))

    if last:
        s.dma(lambda e: e.dma_start(
            out=yT.rearrange("(c p) t -> p c t", p=128),
            in_=X.ap().rearrange("p (c t) -> p c t", t=T)))
    aF.reset(); aB.reset()


def build_l2():
    nc = bass.Bass("TRN2", target_bir_lowering=False)
    with ExitStack() as st:
        C = Ctx(nc, st)
        emit_l2(C, first=True, last=True)
        block = st.enter_context(nc.Block())
        C.finish(block)
    return nc


def l2_in_maps(inputs, xT_list):
    wq = _f(np.asarray(inputs["L2_attn_Wq"])
            .reshape(DCH, 128, DCH, 128).transpose(2, 0, 1, 3))
    wk = _f(np.asarray(inputs["L2_attn_Wk"])
            .reshape(DCH, 128, 2, 128).transpose(2, 0, 1, 3))
    wv = _f(inputs["L2_attn_Wv"]); wo = _f(inputs["L2_attn_Wo"])
    wr = _f(inputs["L2_ffn_Wr"])
    br = _f(np.asarray(inputs["L2_ffn_br"]).reshape(NE, 1))
    w1 = _f(inputs["L2_ffn_W1"]); w2 = _f(inputs["L2_ffn_W2"])
    b1 = _f(np.asarray(inputs["L2_ffn_b1"]).reshape(NE, DCH, 128)
            .transpose(2, 0, 1).reshape(128, NE * DCH))
    b2 = _f(inputs["L2_ffn_b2"])
    maps = []
    for c in range(NCORES):
        pref = np.zeros((NE, 1), np.float32)
        pref[:c] = 1.0
        m = dict(_CONSTS)
        m.update({
            "xT": xT_list[c], "wq": wq, "wk": wk, "wv": wv, "wo": wo,
            "wr": wr, "br": br, "w1e": w1, "w2e": w2, "b1e": b1, "b2e": b2,
            "prefmask": pref,
        })
        maps.append(m)
    return maps


def run_l2(inputs, xT_list):
    nc = build_l2()
    res = run_bass_kernel_spmd(nc, l2_in_maps(inputs, xT_list),
                               core_ids=list(range(NCORES)))
    return [np.asarray(res.results[c]["yT"]) for c in range(NCORES)]


class _IdCols:

    def __init__(self, col):
        self.col = col

    def __getitem__(self, key):
        return self.col[:, 0:1]


L3MARKS = {}


def emit_l3(C, first=False, last=True):
    nc, s, ps, aF, aB = C.nc, C.s, C.ps, C.aF, C.aB
    X, idn_f, idn_b, ones_c, ones_r = C.X, C.idn_f, C.idn_b, C.ones_c, C.ones_r
    mT, rT, nmax, ssum, rsum, tcomb = C.mT, C.rT, C.nmax, C.ssum, C.rsum, C.tcomb
    xT, yT, g_d = C.xT, C.yT, C.g_d
    psA, psB, psQ = ps[0], ps[1], ps[2]
    Z = X

    vec_names = ["ln1w", "ln1b", "ln2w", "ln2b", "tdec", "tfst",
                 "tmk", "tmv", "tmr", "ftmk", "ftmr"]
    vins = {n: C.din(n, [128, DCH]) for n in vec_names}
    kwT_in = C.din("kwT", [2, DCH, 128, 512])
    vwT_in = C.din("vwT", [2, DCH, 128, 512])
    rwT_in = C.din("rwT", [2, DCH, 128, 512])
    owT_in = C.din("owT", [2, DCH, 128, 512])
    frwT_in = C.din("frwT", [2, DCH, 128, 512])
    fkwT_in = C.din("fkwT", [32, DCH, 128, 128])
    fvwT_in = C.din("fvwT", [DCH, 128, 32, 128])
    wg_sl = C.din("wg_sl", [FSL, D])
    wu_sl = C.din("wu_sl", [FSL, D])
    wd_sl = C.din("wd_sl", [WDSL, FF])
    wgT_sl_d = C.dint("wgT_sl_d", [D, FSL], BF16)
    wuT_sl_d = C.dint("wuT_sl_d", [D, FSL], BF16)
    wdT_sl_d = C.dint("wdT_sl_d", [FF, WDSL], BF16)
    wgT_d = C.dint("wgT_d", [NCORES * D, FSL], BF16)
    wuT_d = C.dint("wuT_d", [NCORES * D, FSL], BF16)
    wdT_d = C.dint("wdT_d", [NCORES * FF, WDSL], BF16)
    x3_d = C.dint("x3_d", [D, T], F32)

    vbase = aF.alloc(11 * DCH)
    vsb = {n: vbase[:, i * DCH:(i + 1) * DCH] for i, n in enumerate(vec_names)}
    sbase = aF.alloc(DCH * 4 + 2 * DCH + 1)
    adec = sbase[:, 0:DCH]
    etf = sbase[:, DCH:2 * DCH]
    Ast = sbase[:, 2 * DCH:3 * DCH]
    Bst = sbase[:, 3 * DCH:4 * DCH]
    hprev = sbase[:, 4 * DCH:6 * DCH]
    zcol = sbase[:, 6 * DCH:6 * DCH + 1]
    zstat = aF.off

    if first:
        s.dma(lambda e: e.dma_start(
            out=X.ap().rearrange("p (c t) -> p c t", t=T),
            in_=xT.rearrange("(c p) t -> p c t", p=128)))
    for n in vec_names:
        s.dma(lambda e, n=n: e.dma_start(out=vsb[n][:, 0:DCH], in_=vins[n]))
    s.add("scalar", lambda e: e.activation(
        out=adec[:, :], in_=vsb["tdec"][:, 0:DCH], func=AF.Exp))
    s.add("vector", lambda e: e.tensor_scalar(
        out=adec[:, :], in0=adec[:, :], scalar1=-1.0, scalar2=None, op0=ALU.mult))
    s.add("scalar", lambda e: e.activation(
        out=adec[:, :], in_=adec[:, :], func=AF.Exp))
    s.add("scalar", lambda e: e.activation(
        out=etf[:, :], in_=vsb["tfst"][:, 0:DCH], func=AF.Exp))
    s.add("vector", lambda e: e.memset(Ast[:, :], 0.0))
    s.add("vector", lambda e: e.memset(Bst[:, :], 0.0))
    s.add("vector", lambda e: e.memset(hprev[:, :], 0.0))
    s.add("vector", lambda e: e.memset(zcol[:, :], 0.0))

    q = quant_scratch(aF)
    qin = aF.alloc(QW)
    qout = aB.alloc(QW)
    ev16 = aB.alloc(128)
    emit_fp4_wq(s, ps, idn_b, q, qin, qout, ev16,
                wg_sl, wu_sl, wd_sl, wgT_sl_d, wuT_sl_d, wdT_sl_d,
                wgT_d, wuT_d, wdT_d)
    aF.off = zstat; aB.reset()
    L3MARKS['wq'] = len(s.ops)

    arow = aF.alloc(T)[0:1, :]
    brow = aF.alloc(T)[0:1, :]
    emit_ln(s, nc, aF, X, ones_c, ones_r, arow, brow, mT, rT, psA, psB, ps[3])
    s.dma(lambda e: e.dma_start(
        out=bass.AP(x3_d, 0, [[T, 128], [128 * T, DCH], [1, T]]),
        in_=X.ap().rearrange("p (c t) -> p c t", t=T)))
    for ch in range(NTILE):
        s.add("tensor", lambda e, ch=ch: e.matmul(
            psA[0:128, 0:128], ones_r[:, :], arow[0:1, ch * 128:(ch + 1) * 128],
            start=True, stop=True))
        s.add("tensor", lambda e, ch=ch: e.matmul(
            psB[0:128, 0:128], ones_r[:, :], brow[0:1, ch * 128:(ch + 1) * 128],
            start=True, stop=True))
        for c in range(DCH):
            s.add("vector", lambda e, c=c, ch=ch: e.tensor_tensor(
                Z[:, c * T + ch * 128:c * T + (ch + 1) * 128],
                X[:, c * T + ch * 128:c * T + (ch + 1) * 128],
                psA[0:128, 0:128], op=ALU.mult))
            s.add("vector", lambda e, c=c, ch=ch: e.tensor_tensor(
                Z[:, c * T + ch * 128:c * T + (ch + 1) * 128],
                Z[:, c * T + ch * 128:c * T + (ch + 1) * 128],
                psB[0:128, 0:128], op=ALU.add))

    L3MARKS['z'] = len(s.ops)
    aF.off = zstat
    q = quant_scratch(aF)
    xtok = aF.alloc(D)
    hfl = aF.alloc(FSL)
    gbufF = aF.alloc(GT)
    xq_tok = aB.alloc(D)
    xqT = aB.alloc(DCH * GT)
    wgT_buf = aB.alloc(DCH * FSL)
    wuT_buf = aB.alloc(DCH * FSL)
    hq = aB.alloc(FSL)
    hqT = aB.alloc(24 * GT)
    wdT_buf = aB.alloc(D)
    emit_fp4_ffn(s, ps, Z, _IdCols(zcol), _IdCols(ones_c), idn_f, idn_b, q,
                 xtok, hfl, xq_tok, xqT, wgT_buf, wuT_buf, hq, hqT, wdT_buf,
                 gbufF, wgT_d, wuT_d, wdT_d, g_d)
    L3MARKS['ffn'] = len(s.ops)

    aF.off = zstat
    aB.reset()
    h1 = aF.alloc(DCH * GT)
    Cf = aF.alloc(DCH * GT)
    Vf = aF.alloc(DCH * GT)
    SRf = aF.alloc(DCH * GT)
    cvs = aF.alloc(GT)
    SA = aF.alloc(GT); SB = aF.alloc(GT)
    numb = aF.alloc(GT); denb = aF.alloc(GT)
    scr1 = aF.alloc(GT)
    grow = {}
    for nm in ("m1", "a1", "b1"):
        grow[nm] = aF.alloc(GT)[0:1, :]
    wstr = aB.alloc(4096)
    XK = aB.alloc(DCH * GT)
    XV = aB.alloc(DCH * GT)
    XR = aB.alloc(DCH * GT)
    RWb = aB.alloc(DCH * GT)
    wstr2 = wstr
    KK = aB.alloc(32 * 128)

    def ln_inner(gi, wname, bname, dsth):
        t0 = gi * GT

        def mm_mean(e, t0=t0):
            for c in range(DCH):
                last = e.matmul(psA[0:1, 0:512], ones_c[:, :],
                                Z[:, c * T + t0: c * T + t0 + 512],
                                start=(c == 0), stop=(c == DCH - 1))
            return last
        s.add("tensor", mm_mean)
        for c in range(DCH):
            s.add("scalar", lambda e, c=c, t0=t0: e.activation(
                out=cvs[0:128, 0:512], in_=Z[:, c * T + t0: c * T + t0 + 512],
                func=AF.Square))
            s.add("tensor", lambda e, c=c: e.matmul(
                psB[0:1, 0:512], ones_c[:, :], cvs[0:128, 0:512],
                start=(c == 0), stop=(c == DCH - 1)))
        s.add("vector", lambda e: e.tensor_scalar(
            out=grow["m1"][0:1, 0:512], in0=psA[0:1, 0:512], scalar1=1.0 / D,
            scalar2=None, op0=ALU.mult))
        s.add("vector", lambda e: e.tensor_scalar(
            out=grow["a1"][0:1, 0:512], in0=psB[0:1, 0:512], scalar1=1.0 / D,
            scalar2=EPS, op0=ALU.mult, op1=ALU.add))
        s.add("vector", lambda e: e.tensor_tensor(
            grow["b1"][0:1, 0:512], grow["m1"][0:1, 0:512], grow["m1"][0:1, 0:512],
            op=ALU.mult))
        s.add("vector", lambda e: e.tensor_tensor(
            grow["a1"][0:1, 0:512], grow["a1"][0:1, 0:512], grow["b1"][0:1, 0:512],
            op=ALU.subtract))
        s.add("scalar", lambda e: e.activation(
            out=grow["a1"][0:1, 0:512], in_=grow["a1"][0:1, 0:512], func=AF.Ln))
        s.add("scalar", lambda e: e.activation(
            out=grow["a1"][0:1, 0:512], in_=grow["a1"][0:1, 0:512],
            func=AF.Exp, scale=-0.5))
        s.add("vector", lambda e: e.scalar_tensor_tensor(
            out=grow["b1"][0:1, 0:512], in0=grow["m1"][0:1, 0:512], scalar=-1.0,
            in1=grow["a1"][0:1, 0:512], op0=ALU.mult, op1=ALU.mult))
        s.add("tensor", lambda e: e.matmul(
            psA[0:128, 0:512], ones_r[:, :], grow["a1"][0:1, 0:512],
            start=True, stop=True))
        s.add("tensor", lambda e: e.matmul(
            psB[0:128, 0:512], ones_r[:, :], grow["b1"][0:1, 0:512],
            start=True, stop=True))
        for c in range(DCH):
            s.add("vector", lambda e, c=c, t0=t0: e.tensor_tensor(
                scr1[0:128, 0:512], Z[:, c * T + t0: c * T + t0 + 512],
                psA[0:128, 0:512], op=ALU.mult))
            s.add("vector", lambda e, c=c: e.tensor_tensor(
                scr1[0:128, 0:512], scr1[0:128, 0:512], psB[0:128, 0:512],
                op=ALU.add))
            s.add("vector", lambda e, c=c, wname=wname, bname=bname, dsth=dsth:
                  e.tensor_scalar(
                      out=dsth[:, c * 512:(c + 1) * 512], in0=scr1[0:128, 0:512],
                      scalar1=vsb[wname][:, c:c + 1], scalar2=vsb[bname][:, c:c + 1],
                      op0=ALU.mult, op1=ALU.add))

    def mixes(mix_list, hpoff):
        for c in range(DCH):
            for tm, dst in mix_list:
                tcol = vsb[tm][:, c:c + 1]
                s.add("vector", lambda e, c=c: e.tensor_tensor(
                    scr1[0:128, 0:511], h1[:, c * 512 + 1:(c + 1) * 512],
                    h1[:, c * 512:c * 512 + 511], op=ALU.subtract))
                s.add("vector", lambda e, c=c, tcol=tcol, dst=dst:
                      e.scalar_tensor_tensor(
                          out=dst[:, c * 512 + 1:(c + 1) * 512],
                          in0=scr1[0:128, 0:511], scalar=tcol,
                          in1=h1[:, c * 512:c * 512 + 511],
                          op0=ALU.mult, op1=ALU.add))
                s.add("vector", lambda e, c=c, hpoff=hpoff: e.tensor_tensor(
                    scr1[0:128, 511:512], h1[:, c * 512:c * 512 + 1],
                    hprev[:, hpoff + c:hpoff + c + 1], op=ALU.subtract))
                s.add("vector", lambda e, c=c, tcol=tcol, dst=dst, hpoff=hpoff:
                      e.scalar_tensor_tensor(
                          out=dst[:, c * 512:c * 512 + 1],
                          in0=scr1[0:128, 511:512], scalar=tcol,
                          in1=hprev[:, hpoff + c:hpoff + c + 1],
                          op0=ALU.mult, op1=ALU.add))
            s.add("vector", lambda e, c=c, hpoff=hpoff: e.tensor_copy(
                hprev[:, hpoff + c:hpoff + c + 1],
                h1[:, c * 512 + 511:(c + 1) * 512]))

    def stream_w(src_ap, half):
        s.gdma(lambda e, src_ap=src_ap, half=half: e.dma_start(
            out=wstr.rearrange("p (c n) -> p c n", n=512),
            in_=bass.AP(src_ap.tensor, half * 1024 * 512,
                        [[512, 128], [128 * 512, DCH], [1, 512]])))

    for gi in range(NGRP):
        L3MARKS[f'grp{gi}'] = len(s.ops)
        t0 = gi * GT
        ln_inner(gi, "ln1w", "ln1b", h1)
        mixes([("tmk", XK), ("tmv", XV), ("tmr", XR)], 0)
        for wsrc, fn in ((kwT_in, "exp"), (vwT_in, "copy"), (rwT_in, "sigm")):
            for half in range(2):
                stream_w(wsrc, half)
                for dcl in range(4):
                    dc = half * 4 + dcl
                    src = {"exp": XK, "copy": XV, "sigm": XR}[fn]

                    def mmp(e, dcl=dcl, src=src):
                        for c in range(DCH):
                            last = e.matmul(
                                psQ[0:128, 0:512],
                                wstr[:, c * 512 + dcl * 128:
                                     c * 512 + (dcl + 1) * 128],
                                src[:, c * 512:(c + 1) * 512],
                                start=(c == 0), stop=(c == DCH - 1))
                        return last
                    s.add("tensor", mmp)
                    if fn == "exp":
                        s.add("scalar", lambda e, dc=dc: e.activation(
                            out=Cf[:, dc * 512:(dc + 1) * 512],
                            in_=psQ[0:128, 0:512], func=AF.Exp))
                    elif fn == "copy":
                        s.add("vector", lambda e, dc=dc: e.tensor_copy(
                            Vf[:, dc * 512:(dc + 1) * 512], psQ[0:128, 0:512]))
                    else:
                        s.add("scalar", lambda e, dc=dc: e.activation(
                            out=SRf[:, dc * 512:(dc + 1) * 512],
                            in_=psQ[0:128, 0:512], func=AF.Sigmoid))
        for dc in range(DCH):
            dsl = slice(dc * 512, (dc + 1) * 512)
            s.add("vector", lambda e, dsl=dsl: e.tensor_tensor(
                cvs[0:128, 0:512], Cf[:, dsl], Vf[:, dsl], op=ALU.mult))
            s.add("vector", lambda e, dc=dc: e.tensor_tensor_scan(
                out=SA[0:128, 0:512],
                data0=adec[:, dc:dc + 1].broadcast_to([128, 512]),
                data1=cvs[0:128, 0:512],
                initial=Ast[:, dc:dc + 1], op0=ALU.mult, op1=ALU.add))
            s.add("vector", lambda e, dc=dc, dsl=dsl: e.tensor_tensor_scan(
                out=SB[0:128, 0:512],
                data0=adec[:, dc:dc + 1].broadcast_to([128, 512]),
                data1=Cf[:, dsl],
                initial=Bst[:, dc:dc + 1], op0=ALU.mult, op1=ALU.add))
            s.add("vector", lambda e, dc=dc: e.scalar_tensor_tensor(
                out=numb[0:128, 0:1], in0=cvs[0:128, 0:1], scalar=etf[:, dc:dc + 1],
                in1=Ast[:, dc:dc + 1], op0=ALU.mult, op1=ALU.add))
            s.add("vector", lambda e, dc=dc: e.scalar_tensor_tensor(
                out=numb[0:128, 1:512], in0=cvs[0:128, 1:512],
                scalar=etf[:, dc:dc + 1], in1=SA[0:128, 0:511],
                op0=ALU.mult, op1=ALU.add))
            s.add("vector", lambda e, dc=dc: e.scalar_tensor_tensor(
                out=denb[0:128, 0:1], in0=Cf[:, dc * 512:dc * 512 + 1],
                scalar=etf[:, dc:dc + 1], in1=Bst[:, dc:dc + 1],
                op0=ALU.mult, op1=ALU.add))
            s.add("vector", lambda e, dc=dc: e.scalar_tensor_tensor(
                out=denb[0:128, 1:512], in0=Cf[:, dc * 512 + 1:(dc + 1) * 512],
                scalar=etf[:, dc:dc + 1], in1=SB[0:128, 0:511],
                op0=ALU.mult, op1=ALU.add))
            s.add("vector", lambda e, dc=dc: e.tensor_copy(
                Ast[:, dc:dc + 1], SA[0:128, 511:512]))
            s.add("vector", lambda e, dc=dc: e.tensor_copy(
                Bst[:, dc:dc + 1], SB[0:128, 511:512]))
            s.add("scalar", lambda e: e.activation(
                out=denb[0:128, 0:512], in_=denb[0:128, 0:512], func=AF.Ln))
            s.add("scalar", lambda e: e.activation(
                out=denb[0:128, 0:512], in_=denb[0:128, 0:512],
                func=AF.Exp, scale=-1.0))
            s.add("vector", lambda e: e.tensor_tensor(
                numb[0:128, 0:512], numb[0:128, 0:512], denb[0:128, 0:512],
                op=ALU.mult))
            s.add("vector", lambda e, dc=dc, dsl=dsl: e.tensor_tensor(
                RWb[:, dsl], numb[0:128, 0:512], SRf[:, dsl], op=ALU.mult))
        for half in range(2):
            stream_w(owT_in, half)
            for c2l in range(4):
                c2 = half * 4 + c2l

                def mmo(e, c2l=c2l):
                    for dc in range(DCH):
                        last = e.matmul(
                            psQ[0:128, 0:512],
                            wstr[:, dc * 512 + c2l * 128:
                                 dc * 512 + (c2l + 1) * 128],
                            RWb[:, dc * 512:(dc + 1) * 512],
                            start=(dc == 0), stop=(dc == DCH - 1))
                    return last
                s.add("tensor", mmo)
                s.add("vector", lambda e, c2=c2, t0=t0: e.tensor_tensor(
                    Z[:, c2 * T + t0: c2 * T + t0 + 512],
                    Z[:, c2 * T + t0: c2 * T + t0 + 512],
                    psQ[0:128, 0:512], op=ALU.add))
        ln_inner(gi, "ln2w", "ln2b", h1)
        mixes([("ftmk", XK), ("ftmr", XR)], DCH)
        for half in range(2):
            stream_w(frwT_in, half)
            for dcl in range(4):
                dc = half * 4 + dcl

                def mmr2(e, dcl=dcl):
                    for c in range(DCH):
                        last = e.matmul(
                            psQ[0:128, 0:512],
                            wstr[:, c * 512 + dcl * 128: c * 512 + (dcl + 1) * 128],
                            XR[:, c * 512:(c + 1) * 512],
                            start=(c == 0), stop=(c == DCH - 1))
                    return last
                s.add("tensor", mmr2)
                s.add("scalar", lambda e, dc=dc: e.activation(
                    out=SRf[:, dc * 512:(dc + 1) * 512], in_=psQ[0:128, 0:512],
                    func=AF.Sigmoid))
        for qtr in range(4):
            q0 = qtr * 128
            for fc in range(32):
                s.gdma(lambda e, fc=fc: e.dma_start(
                    out=wstr2[:, 0:1024].rearrange("p (c n) -> p c n", n=128),
                    in_=bass.AP(fkwT_in.tensor, fc * DCH * 128 * 128,
                                [[128, 128], [128 * 128, DCH], [1, 128]])))

                def mmkk(e, fc=fc, q0=q0):
                    for c in range(DCH):
                        last = e.matmul(
                            psQ[0:128, 0:128],
                            wstr2[:, c * 128:(c + 1) * 128],
                            XK[:, c * 512 + q0: c * 512 + q0 + 128],
                            start=(c == 0), stop=(c == DCH - 1))
                    return last
                s.add("tensor", mmkk)
                s.add("scalar", lambda e: e.activation(
                    out=cvs[0:128, 0:128], in_=psQ[0:128, 0:128], func=AF.Relu))
                s.add("vector", lambda e, fc=fc: e.tensor_tensor(
                    KK[:, fc * 128:(fc + 1) * 128], cvs[0:128, 0:128],
                    cvs[0:128, 0:128], op=ALU.mult))
            for c2 in range(DCH):
                s.gdma(lambda e, c2=c2: e.dma_start(
                    out=wstr2[:, :].rearrange("p (c n) -> p c n", n=128),
                    in_=bass.AP(fvwT_in.tensor, c2 * 128 * 32 * 128,
                                [[32 * 128, 128], [128, 32], [1, 128]])))

                def mmkv(e, c2=c2):
                    for fc in range(32):
                        last = e.matmul(
                            psQ[0:128, 0:128],
                            wstr2[:, fc * 128:(fc + 1) * 128],
                            KK[:, fc * 128:(fc + 1) * 128],
                            start=(fc == 0), stop=(fc == 31))
                    return last
                s.add("tensor", mmkv)
                s.add("vector", lambda e, c2=c2, q0=q0: e.tensor_tensor(
                    cvs[0:128, 0:128], SRf[:, c2 * 512 + q0: c2 * 512 + q0 + 128],
                    psQ[0:128, 0:128], op=ALU.mult))
                s.add("vector", lambda e, c2=c2, t0=t0, q0=q0: e.tensor_tensor(
                    Z[:, c2 * T + t0 + q0: c2 * T + t0 + q0 + 128],
                    Z[:, c2 * T + t0 + q0: c2 * T + t0 + q0 + 128],
                    cvs[0:128, 0:128], op=ALU.add))

    L3MARKS['groups'] = len(s.ops)
    aF.off = zstat
    xck = aF.alloc(DCH * 512)
    gck = aF.alloc(DCH * 512)
    for tc2 in range(4):
        s.dma(lambda e, tc2=tc2: e.dma_start(
            out=xck.rearrange("p (c t) -> p c t", t=512),
            in_=bass.AP(x3_d, tc2 * 512, [[T, 128], [128 * T, DCH], [1, 512]])))
        s.dma(lambda e, tc2=tc2: e.dma_start(
            out=gck.rearrange("p (c t) -> p c t", t=512),
            in_=bass.AP(g_d, tc2 * 512, [[T, 128], [128 * T, DCH], [1, 512]])))
        for c in range(DCH):
            s.add("vector", lambda e, c=c, tc2=tc2: e.tensor_tensor(
                gck[:, c * 512:(c + 1) * 512],
                Z[:, c * T + tc2 * 512: c * T + (tc2 + 1) * 512],
                gck[:, c * 512:(c + 1) * 512], op=ALU.mult))
            s.add("vector", lambda e, c=c: e.tensor_tensor(
                xck[:, c * 512:(c + 1) * 512], xck[:, c * 512:(c + 1) * 512],
                gck[:, c * 512:(c + 1) * 512], op=ALU.add))
        s.dma(lambda e, tc2=tc2: e.dma_start(
            out=bass.AP(yT.tensor, tc2 * 512, [[T, 128], [128 * T, DCH], [1, 512]]),
            in_=xck.rearrange("p (c t) -> p c t", t=512)))

    aF.reset(); aB.reset()


def build_l3():
    nc = bass.Bass("TRN2", target_bir_lowering=False)
    with ExitStack() as st:
        C = Ctx(nc, st)
        emit_l3(C, first=True, last=True)
        block = st.enter_context(nc.Block())
        C.finish(block)
    return nc


def l3_in_maps(inputs, xT_list):
    def v128(name):
        return _bias128(inputs[name])
    wg = _f(inputs["L3_ffn_w_gate"]); wu = _f(inputs["L3_ffn_w_up"])
    wd = _f(inputs["L3_ffn_w_down"])
    common = {
        "ln1w": v128("L3_attn_ln1_w"), "ln1b": v128("L3_attn_ln1_b"),
        "ln2w": v128("L3_attn_ln2_w"), "ln2b": v128("L3_attn_ln2_b"),
        "tdec": v128("L3_attn_time_decay"), "tfst": v128("L3_attn_time_first"),
        "tmk": v128("L3_attn_tmk"), "tmv": v128("L3_attn_tmv"),
        "tmr": v128("L3_attn_tmr"),
        "ftmk": v128("L3_attn_f_tmk"), "ftmr": v128("L3_attn_f_tmr"),
        "kwT": _tile_half(inputs["L3_attn_att_kw"]),
        "vwT": _tile_half(inputs["L3_attn_att_vw"]),
        "rwT": _tile_half(inputs["L3_attn_att_rw"]),
        "owT": _tile_half(inputs["L3_attn_att_ow"]),
        "frwT": _tile_half(inputs["L3_attn_f_rw"]),
        "fkwT": _f(np.asarray(inputs["L3_attn_f_kw"]).T
                   .reshape(DCH, 128, 32, 128).transpose(2, 0, 1, 3)),
        "fvwT": _f(np.asarray(inputs["L3_attn_f_vw"]).T
                   .reshape(32, 128, DCH, 128).transpose(2, 1, 0, 3)),
    }
    maps = []
    for c in range(NCORES):
        m = dict(_CONSTS)
        m.update(common)
        m.update({
            "xT": xT_list[c],
            "wg_sl": _f(wg[c * FSL:(c + 1) * FSL]),
            "wu_sl": _f(wu[c * FSL:(c + 1) * FSL]),
            "wd_sl": _f(wd[c * WDSL:(c + 1) * WDSL]),
        })
        maps.append(m)
    return maps


def run_l3(inputs, xT_list):
    nc = build_l3()
    res = run_bass_kernel_spmd(nc, l3_in_maps(inputs, xT_list),
                               core_ids=list(range(NCORES)))
    return [np.asarray(res.results[c]["yT"]) for c in range(NCORES)]
